# version 12
# baseline (speedup 1.0000x reference)
import contextlib
import numpy as np
import ml_dtypes
import concourse.bass as bass
import concourse.mybir as mybir
from concourse.bass_utils import run_bass_kernel_spmd

F32 = mybir.dt.float32
BF16 = mybir.dt.bfloat16
AX = mybir.AxisListType.X
OP = mybir.AluOpType
AF = mybir.ActivationFunctionType

D = 1024
KC = D // 128
E = 8
H = 4096
HC = H // 128
NHEADS = 16
HDIM = 64
DEPTH_SCALE = float(1.0 / np.sqrt(12.0))
LN_EPS = 1e-5
RMS_EPS = 1e-6
NEG = -1e9
CCAP = 16384
NCORES = 8


def alibi_slopes():
    half = NHEADS // 2
    base = 24.0 ** (1.0 / half)
    return (1.0 / base ** np.arange(1, half + 1, dtype=np.float64)).astype(np.float64)


class Em:
    ENGINES = ("sync", "tensor", "vector", "scalar", "gpsimd")

    CHAIN = ("vector", "scalar")

    def __init__(self, nc, stack):
        self.nc = nc
        self.streams = {e: [] for e in self.ENGINES}
        self.stack = stack
        self.sems = {}
        self.counts = {}
        self.cch = self.sem("cch")
        self.dch = self.sem("dch")
        self.spd = self.sem("spd")
        self.selfsem = {e: self.sem(f"self_{e}") for e in self.CHAIN}
        self.last = {}

    def sem(self, name):
        assert name not in self.sems, name
        h = self.nc.alloc_semaphore(name=name)
        self.sems[name] = h
        if not hasattr(self, "phys_count"):
            self.phys_count = {}
        self.counts[name] = self.phys_count.get(h.num, 0)
        return name

    def release_phase_sems(self, keep=()):
        keepset = (set(keep) | {self.cch, self.dch, self.spd}
                   | set(self.selfsem.values()))
        handles = []
        for name in list(self.sems.keys()):
            if name in keepset:
                continue
            h = self.sems[name]
            handles.append(h)
            if not hasattr(self, "phys_count"):
                self.phys_count = {}
            self.phys_count[h.num] = 0
            del self.sems[name]
            del self.counts[name]
        fence = self.fence_t
        spd_h = self.sems[self.spd]
        self.streams["tensor"].append(lambda E_, s=spd_h: E_.nop().then_inc(s, 1))
        self.streams["vector"].append(lambda E_, s=spd_h: E_.engine_nop().then_inc(s, 1))
        self.streams["scalar"].append(
            lambda E_, s=spd_h, f=fence: E_.copy(out=f[:], in_=f[:]).then_inc(s, 1))
        self.streams["sync"].append(lambda E_, s=spd_h: E_.nop().then_inc(s, 1))
        self.counts[self.spd] += 4
        self.last["vector"] = (self.spd, self.counts[self.spd])
        self.last["scalar"] = (self.spd, self.counts[self.spd])
        self.wait("gpsimd", self.spd, self.counts[self.spd])
        self.wait("gpsimd", self.cch, self.counts[self.cch])
        self.wait("gpsimd", self.dch, self.counts[self.dch])
        for h in handles:
            self.streams["gpsimd"].append(
                lambda E_, h=h: E_.dma_reset(semaphore_range=range(h.num, h.num + 1)))
            self.streams["gpsimd"].append(
                lambda E_, h=h: E_.sem_clear(h))
            self.nc.release_semaphore(h)

        def bump(E_, s=self.sems[self.cch]):
            E_.memset(fence[:], 0.0).then_inc(s, 1)
        self.streams["gpsimd"].append(bump)
        self.counts[self.cch] += 1
        for eng in list(self.last.keys()):
            if self.last[eng] is not None and self.last[eng][0] not in self.sems:
                self.last[eng] = None

    def wait(self, eng, sem, val):
        h = self.sems[sem]
        if val > 0:
            self.streams[eng].append(lambda E_, h=h, v=val: E_.wait_ge(h, v))

    def emit(self, eng, fn, inc=None):
        chained = eng in self.CHAIN
        if chained:
            prev = self.last.get(eng)
            if prev is not None:
                self.wait(eng, prev[0], prev[1])
            if inc is None:
                inc = (self.selfsem[eng], 1)
        if inc is None:
            self.streams[eng].append(fn)
        else:
            s, n = inc
            h = self.sems[s]
            self.streams[eng].append(lambda E_, fn=fn, h=h, n=n: fn(E_).then_inc(h, n))
            self.counts[s] += n
            if chained:
                self.last[eng] = (s, self.counts[s])

    def ser(self, eng, fn, dma=False):
        self.wait(eng, self.cch, self.counts[self.cch])
        self.wait(eng, self.dch, self.counts[self.dch])
        self.emit(eng, fn, inc=(self.dch, 16) if dma else (self.cch, 1))

    def barrier(self):
        deps = [(self.cch, self.counts[self.cch]), (self.dch, self.counts[self.dch])]
        for s in self.selfsem.values():
            deps.append((s, self.counts[s]))
        self.join(deps)

    def join(self, deps):
        for eng in self.ENGINES:
            for s, v in deps:
                self.wait(eng, s, v)

    def finalize(self, blk):
        for name in self.ENGINES:
            ops = self.streams[name]

            def body(E_, ops=ops):
                for op in ops:
                    op(E_)
            getattr(blk, name)(body)


def host_constants(core_id):
    c = {}
    c["c_ident"] = np.eye(128, dtype=np.float32)
    c["c_ones_col"] = np.ones((128, 1), np.float32)
    c["c_ones1"] = np.ones((1, 128), np.float32)
    c["c_ones11"] = np.ones((1, 1), np.float32)
    oneh = np.zeros((8, 8 * 128), np.float32)
    for e in range(8):
        oneh[e, e * 128:(e + 1) * 128] = 1.0
    c["c_oneh"] = oneh
    c["c_p2row"] = np.broadcast_to((2.0 ** -np.arange(8, dtype=np.float32))[None, :],
                                   (128, 8)).copy()
    c["c_ones8"] = np.ones((8, 512), np.float32)
    i = np.arange(128)[:, None].astype(np.float32)
    jj = np.arange(512)[None, :].astype(np.float32)
    c["c_rel_kq512"] = (i - jj).astype(np.float32)
    for bb in range(4):
        rel = (i - jj + 128.0 * bb).astype(np.float32)
        vf = (i + 128.0 * bb) <= jj
        c[f"c_diag_f{bb}"] = np.where(vf, rel, -1e10).astype(np.float32)
        vb = (i + 128.0 * bb) >= jj
        c[f"c_diag_b{bb}"] = np.where(vb, rel, 1e10).astype(np.float32)
    doff = np.broadcast_to(128.0 * (np.arange(64, dtype=np.float32) - 32.0),
                           (128, 64)).copy()
    c["c_doff"] = doff
    c["c_ones64"] = np.ones((128, 64), np.float32)
    ii = np.arange(128, dtype=np.float32)[:, None]
    dd = np.arange(64, dtype=np.float32)[None, :]
    c["c_doff2"] = (ii + 128.0 * (dd - 32.0) - 256.0).astype(np.float32)
    bones2 = np.zeros((128, 2), np.float32)
    bones2[0:64, 0] = 1.0
    bones2[64:128, 1] = 1.0
    c["c_bones2"] = bones2
    pref = np.zeros((8, 1), np.float32)
    pref[:core_id, 0] = 1.0
    c["c_pref"] = pref
    c["c_ones8col"] = np.ones((8, 1), np.float32)
    return c


CONST_SPECS = {k: v.shape for k, v in host_constants(0).items()}


def xap(h, fc, tt, nf=1):
    return h[tt, fc]


class B:
    def __init__(self, nc, stack, em, T):
        self.nc = nc
        self.stack = stack
        self.em = em
        self.T = T
        self.NTT = T // 512
        self.NTC = T // 128
        self.cst = {}
        self.pb = []
        self.ext = {}

    def sb(self, name, shape, dtype):
        return self.stack.enter_context(self.nc.sbuf_tensor(name, shape, dtype))

    def push(self):
        self._saved = getattr(self, "_saved", [])
        self._saved.append(self.stack)
        self.stack = contextlib.ExitStack()
        return self.stack

    def pop(self):
        self.stack.close()
        self.stack = self._saved.pop()

    def din(self, name, shape):
        h = self.nc.dram_tensor(name, list(shape), F32, kind="ExternalInput")
        self.ext[name] = tuple(shape)
        return h


def setup_common(b):
    for name, shape in CONST_SPECS.items():
        if name.startswith("c_diag_") or name in ("c_rel_kq512", "c_ones64"):
            continue
        h = b.din(name, shape)
        t = b.sb(name + "_sb", list(shape), F32)
        b.cst[name] = t
        b.em.ser("sync", lambda E_, t=t, h=h: E_.dma_start(out=t[:], in_=h[:]), dma=True)
    b.pb = [b.stack.enter_context(b.nc.psum_tensor(f"pb{i}", [128, 512], F32))
            for i in range(8)]
    b.em.fence_t = b.sb("fence_t", [1, 1], F32)
    b.em.ser("vector", lambda E_: E_.memset(b.em.fence_t[:], 0.0))
    oc_b = b.sb("c_ones_colb", [128, 1], BF16)
    b.em.ser("scalar", lambda E_, oc_b=oc_b: E_.copy(
        out=oc_b[:], in_=b.cst["c_ones_col"][:]))
    b.cst["c_ones_colb"] = oc_b
    o1_b = b.sb("c_ones1b", [1, 128], BF16)
    b.em.ser("scalar", lambda E_, o1_b=o1_b: E_.copy(
        out=o1_b[:], in_=b.cst["c_ones1"][:]))
    b.cst["c_ones1b"] = o1_b


def vec2cols(b, vec_h, n, dst_sb, tmp_name):
    em = b.em
    nch = n // 128
    tmp = b.sb(tmp_name, [nch, 128], F32)
    em.ser("sync", lambda E_: E_.dma_start(
        out=tmp[:], in_=vec_h.ap().rearrange("(c p) -> c p", p=128)), dma=True)
    em.ser("tensor", lambda E_: E_.transpose(
        b.pb[6][:, 0:nch], tmp[:], b.cst["c_ident"][0:nch, 0:nch]))
    em.ser("vector", lambda E_: E_.tensor_copy(out=dst_sb[:, 0:nch], in_=b.pb[6][:, 0:nch]))


def row2col(b, row_ap, col_ap, n):
    em = b.em
    em.ser("tensor", lambda E_: E_.matmul(
        b.pb[6][0:n, 0:1], row_ap, b.cst["c_ones11"][:], start=True, stop=True))
    em.ser("vector", lambda E_: E_.tensor_copy(out=col_ap, in_=b.pb[6][0:n, 0:1]))


def col2row(b, col_ap, row_ap, n):
    em = b.em
    em.ser("tensor", lambda E_: E_.transpose(
        b.pb[6][0:1, 0:n], col_ap, b.cst["c_ident"][0:n, 0:n]))
    em.ser("vector", lambda E_: E_.tensor_copy(out=row_ap, in_=b.pb[6][0:1, 0:n]))


def alloc_rows(b, pfx, names, persist=(), persist_dtype=F32):
    return {n: b.sb(f"{pfx}row_{n}", [1, b.T] if n in persist else [1, 512],
                    persist_dtype if n in persist else F32)
            for n in names}


def emit_ln_pass(b, pfx, x_hbm, rows, want_rms=False, xn_sb=None, xn_rows=None):
    em = b.em
    T, NTT = b.T, b.NTT
    b.push()
    need = ["alpha", "beta", "a1", "t1", "t2", "t3", "t4"]
    if want_rms:
        need += ["gamma", "delta"]
    for nm in need:
        if nm not in rows:
            rows[nm] = b.sb(f"{pfx}row_{nm}", [1, 512], F32)
    xr = [b.sb(f"{pfx}xr{i}", [128, 512], F32) for i in range(KC)]
    sq = [b.sb(f"{pfx}sq{i}", [128, 512], F32) for i in range(2)]
    xs = [b.sb(f"{pfx}xs{i}", [128, 512], F32) for i in range(2)]
    sL = [em.sem(f"{pfx}L{i}") for i in range(KC)]
    sSq = em.sem(pfx + "Sq")
    sMM = em.sem(pfx + "MM")
    sRw = em.sem(pfx + "Rw")
    sSx = em.sem(pfx + "Sx")
    sBC = em.sem(pfx + "BC")
    sXN = em.sem(pfx + "XN")
    rpt = 2 if want_rms else 1

    def r(name, tt):
        buf = rows[name]
        if buf.shape[1] == T:
            return buf[0:1, tt * 512:(tt + 1) * 512]
        return buf[0:1, 0:512]

    for tt in range(NTT):
        for fc in range(KC):
            i = tt * KC + fc
            if tt >= 1:
                if xn_sb is not None:
                    em.wait("sync", sXN, (tt - 1) * KC + fc + 1)
                else:
                    em.wait("sync", sMM, (tt - 1) * KC + fc + 1)
            em.emit("sync", lambda E_, fc=fc, tt=tt: E_.dma_start(
                out=xr[fc][:], in_=x_hbm[tt, fc]),
                inc=(sL[fc], 16))
            em.wait("scalar", sL[fc], (tt + 1) * 16)
            if i >= 2:
                em.wait("scalar", sMM, i - 1)
            em.emit("scalar", lambda E_, fc=fc, i=i: E_.square(
                out=sq[i % 2][:], in_=xr[fc][:]), inc=(sSq, 1))
            em.wait("tensor", sSq, i + 1)
            em.wait("tensor", sL[fc], (tt + 1) * 16)
            if tt >= 1 and fc == 0:
                em.wait("tensor", sRw, (tt - 1) * rpt + 1)
            em.emit("tensor", lambda E_, fc=fc: E_.matmul(
                b.pb[0][0:1, 0:512], b.cst["c_ones_col"][:], xr[fc][:],
                start=(fc == 0), stop=(fc == KC - 1)))
            em.emit("tensor", lambda E_, fc=fc, i=i: E_.matmul(
                b.pb[1][0:1, 0:512], b.cst["c_ones_col"][:], sq[i % 2][:],
                start=(fc == 0), stop=(fc == KC - 1)), inc=(sMM, 1))
        em.wait("vector", sMM, (tt + 1) * KC)
        t1 = rows["t1"][0:1, 0:512]
        t2 = rows["t2"][0:1, 0:512]
        t3 = rows["t3"][0:1, 0:512]
        t4 = rows["t4"][0:1, 0:512]
        ral, rbe, ra1 = r("alpha", tt), r("beta", tt), r("a1", tt)

        def v(fn):
            em.emit("vector", fn)

        v(lambda E_: E_.tensor_scalar_mul(t1, b.pb[0][0:1, 0:512], 1.0 / D))
        v(lambda E_: E_.tensor_scalar_mul(t2, b.pb[1][0:1, 0:512], 1.0 / D))
        v(lambda E_: E_.tensor_mul(t3, t1, t1))
        v(lambda E_: E_.tensor_sub(t3, t2, t3))
        em.emit("vector", lambda E_: E_.tensor_scalar_add(t3, t3, LN_EPS),
                inc=(sRw, 1))
        em.wait("scalar", sRw, tt * rpt + 1)
        em.emit("scalar", lambda E_: E_.sqrt(out=t4, in_=t3), inc=(sSx, 1))
        em.wait("vector", sSx, tt * rpt + 1)
        v(lambda E_: E_.reciprocal(t2, t4))
        v(lambda E_, ral=ral: E_.tensor_copy(out=ral, in_=t2))
        v(lambda E_: E_.tensor_mul(t3, t1, t2))
        v(lambda E_: E_.tensor_scalar_mul(t3, t3, -1.0))
        v(lambda E_, rbe=rbe: E_.tensor_copy(out=rbe, in_=t3))
        v(lambda E_, ra1=ra1: E_.tensor_scalar_add(ra1, t2, 1.0))
        if want_rms:
            rga, rde = r("gamma", tt), r("delta", tt)
            v(lambda E_: E_.tensor_mul(t1, t4, t4))
            v(lambda E_: E_.tensor_scalar_add(t1, t1, -LN_EPS))
            v(lambda E_: E_.tensor_mul(t1, t1, t2))
            v(lambda E_: E_.tensor_mul(t1, t1, t2))
            em.emit("vector", lambda E_: E_.tensor_scalar_add(t1, t1, RMS_EPS),
                    inc=(sRw, 1))
            em.wait("scalar", sRw, tt * rpt + 2)
            em.emit("scalar", lambda E_: E_.sqrt(out=t4, in_=t1), inc=(sSx, 1))
            em.wait("vector", sSx, tt * rpt + 2)
            v(lambda E_: E_.reciprocal(t1, t4))
            v(lambda E_, rga=rga: E_.tensor_mul(rga, t2, t1))
            v(lambda E_, rde=rde: E_.tensor_mul(rde, t3, t1))
        if xn_sb is not None:
            gname, dname = xn_rows
            rg, rd = r(gname, tt), r(dname, tt)
            em.emit("vector", lambda E_: E_.engine_nop(), inc=(sBC, 1))
            em.wait("tensor", sBC, 2 * tt + 1)
            if tt >= 1:
                em.wait("tensor", sXN, tt * KC)
            em.emit("tensor", lambda E_, rg=rg: E_.matmul(
                b.pb[2][:, 0:512], b.cst["c_ones1"][:], rg, start=True, stop=True))
            em.emit("tensor", lambda E_, rd=rd: E_.matmul(
                b.pb[3][:, 0:512], b.cst["c_ones1"][:], rd, start=True, stop=True),
                inc=(sBC, 1))
            for fc in range(KC):
                i = tt * KC + fc
                em.wait("vector", sBC, 2 * tt + 2)
                if i >= 2:
                    em.wait("vector", sXN, i - 1)
                em.emit("vector", lambda E_, fc=fc, i=i: E_.tensor_tensor(
                    out=xs[i % 2][:], in0=xr[fc][:], in1=b.pb[2][:, 0:512], op=OP.mult))
                em.emit("vector", lambda E_, fc=fc, tt=tt, i=i: E_.tensor_tensor(
                    out=xn_sb[:, fc * T + tt * 512: fc * T + tt * 512 + 512],
                    in0=xs[i % 2][:], in1=b.pb[3][:, 0:512], op=OP.add),
                    inc=(sXN, 1))
    deps = [(sRw, em.counts[sRw]), (sMM, em.counts[sMM]), (sSx, em.counts[sSx])]
    if xn_sb is not None:
        deps.append((sXN, em.counts[sXN]))
    for s in sL:
        deps.append((s, em.counts[s]))
    for s in em.selfsem.values():
        deps.append((s, em.counts[s]))
    em.join(deps)
    b.pop()


def emit_phase_a(b, x0_hbm, x1_hbm):
    em, nc, T, NTT = b.em, b.nc, b.T, b.NTT
    pfx = "a"
    whg_h = b.din("L0_attn_w_hg", [E, 2 * D, D])
    wout_h = b.din("L0_attn_w_out", [E, D, D])
    nw_h = b.din("L0_attn_norm_weight", [D])
    rw_h = b.din("L0_attn_router_w", [E, D])
    rb_h = b.din("L0_attn_router_b", [1, E])

    rows = alloc_rows(b, pfx, ["alpha", "beta"], persist=("beta", "alpha"))
    xrms = b.sb("a_xrms", [128, KC * T], BF16)
    whgT = b.sb("a_whgT", [128, KC * 2048], BF16)
    woutT = b.sb("a_woutT", [128, KC * 1024], BF16)
    pcols = b.sb("a_pcols", [128, 8], F32)
    nwcols = b.sb("a_nwcols", [128, 8], F32)
    meanv = b.sb("a_meanv", [128, 8], F32)

    emit_ln_pass(b, pfx, x0_hbm, rows, want_rms=True, xn_sb=xrms,
                 xn_rows=("gamma", "delta"))

    b.push()
    scI = b.sb("a_scI", [128, 8 * 128], BF16)
    r8a = b.sb("a_r8a", [1, 8], F32)
    r8b = b.sb("a_r8b", [1, 8], F32)
    rw_sb = b.sb("a_rw", [8, D], F32)
    rwT = b.sb("a_rwT", [128, 64], F32)
    rbrow = b.sb("a_rb", [1, 8], F32)
    wring = [b.sb(f"a_wr{i}", [128, D], BF16) for i in range(8)]

    vec2cols(b, nw_h, D, nwcols, "a_nwtmp")
    em.ser("sync", lambda E_: E_.dma_start(out=rw_sb[:], in_=rw_h[:]), dma=True)
    em.ser("sync", lambda E_: E_.dma_start(out=rbrow[:], in_=rb_h[:]), dma=True)
    for fc in range(KC):
        em.ser("tensor", lambda E_, fc=fc: E_.transpose(
            b.pb[6][:, 0:8], rw_sb[:, fc * 128:(fc + 1) * 128],
            b.cst["c_ident"][0:8, 0:8]))
        em.ser("vector", lambda E_, fc=fc: E_.tensor_copy(
            out=rwT[:, fc * 8:(fc + 1) * 8], in_=b.pb[6][:, 0:8]))
    for fc in range(KC):
        em.ser("vector", lambda E_, fc=fc: E_.tensor_reduce(
            out=meanv[:, fc:fc + 1], in_=xrms[:, fc * T:(fc + 1) * T],
            axis=AX, op=OP.add))
        em.ser("vector", lambda E_, fc=fc: E_.tensor_scalar(
            out=meanv[:, fc:fc + 1], in0=meanv[:, fc:fc + 1],
            scalar1=nwcols[:, fc:fc + 1], scalar2=1.0 / T, op0=OP.mult, op1=OP.mult))
    for fc in range(KC):
        em.ser("tensor", lambda E_, fc=fc: E_.matmul(
            b.pb[6][0:1, 0:8], meanv[:, fc:fc + 1], rwT[:, fc * 8:(fc + 1) * 8],
            start=(fc == 0), stop=(fc == KC - 1)))
    em.ser("vector", lambda E_: E_.tensor_add(r8a[:], b.pb[6][0:1, 0:8], rbrow[:]))
    em.ser("vector", lambda E_: E_.tensor_reduce(
        out=r8b[0:1, 0:1], in_=r8a[:], axis=AX, op=OP.max))
    em.ser("vector", lambda E_: E_.tensor_scalar_mul(r8b[0:1, 1:2], r8b[0:1, 0:1], -1.0))
    em.ser("scalar", lambda E_: E_.activation(
        out=r8a[:], in_=r8a[:], func=AF.Exp, bias=r8b[0:1, 1:2], scale=1.0))
    em.ser("vector", lambda E_: E_.tensor_reduce(
        out=r8b[0:1, 2:3], in_=r8a[:], axis=AX, op=OP.add))
    em.ser("vector", lambda E_: E_.reciprocal(r8b[0:1, 3:4], r8b[0:1, 2:3]))
    em.ser("vector", lambda E_: E_.tensor_scalar(
        out=r8a[:], in0=r8a[:], scalar1=r8b[0:1, 3:4], scalar2=None, op0=OP.mult))
    em.ser("tensor", lambda E_: E_.matmul(
        b.pb[6][:, 0:8], b.cst["c_ones1"][:], r8a[:], start=True, stop=True))
    em.ser("vector", lambda E_: E_.tensor_copy(out=pcols[:], in_=b.pb[6][:, 0:8]))
    for e in range(E):
        em.ser("scalar", lambda E_, e=e: E_.activation(
            out=scI[:, e * 128:(e + 1) * 128], in_=b.cst["c_ident"][:],
            func=AF.Copy, bias=0.0, scale=pcols[:, e:e + 1]))

    sWL = [em.sem(f"a_wl{i}") for i in range(8)]
    sMix = em.sem("a_mix")
    sMixEv = em.sem("a_mixev")

    state = {"g": 0, "oc_total": 0}

    def mix(which, w_h, dst):
        outer = 16 if which == 0 else 8
        for oc in range(outer):
            oci = state["oc_total"]
            for e in range(E):
                if oci >= 1:
                    em.wait("gpsimd", sMix, oci * KC)
                em.emit("gpsimd", lambda E_, oc=oc, e=e, w_h=w_h: E_.dma_start(
                    out=wring[e][:], in_=w_h[e, oc * 128:(oc + 1) * 128, :]),
                    inc=(sWL[e], 16))
            for fc in range(KC):
                g = state["g"]
                bank = b.pb[4 + fc % 2]
                if g >= 2:
                    em.wait("tensor", sMixEv, g - 1)
                for e in range(E):
                    if fc == 0:
                        em.wait("tensor", sWL[e], (oci + 1) * 16)
                    em.emit("tensor", lambda E_, e=e, fc=fc, bank=bank: E_.matmul(
                        bank[:, 0:128],
                        wring[e][:, fc * 128:(fc + 1) * 128],
                        scI[:, e * 128:(e + 1) * 128],
                        start=(e == 0), stop=(e == E - 1)),
                        inc=(sMix, 1) if e == E - 1 else None)
                em.wait("vector", sMix, g + 1)
                if which == 0:
                    dstap = dst[:, fc * 2048 + oc * 128: fc * 2048 + oc * 128 + 128]
                    em.emit("vector", lambda E_, dstap=dstap, bank=bank, fc=fc: E_.tensor_scalar(
                        out=dstap, in0=bank[:, 0:128], scalar1=nwcols[:, fc:fc + 1],
                        scalar2=None, op0=OP.mult), inc=(sMixEv, 1))
                else:
                    dstap = dst[:, fc * 1024 + oc * 128: fc * 1024 + oc * 128 + 128]
                    em.emit("vector", lambda E_, dstap=dstap, bank=bank: E_.tensor_copy(
                        out=dstap, in_=bank[:, 0:128]), inc=(sMixEv, 1))
                state["g"] += 1
            state["oc_total"] += 1

    em.join([(em.cch, em.counts[em.cch]), (em.dch, em.counts[em.dch])])
    mix(0, whg_h, whgT)
    mix(1, wout_h, woutT)
    em.join([(sMixEv, em.counts[sMixEv])] +
            [(s, em.counts[s]) for s in sWL])
    em.barrier()
    b.pop()

    b.push()
    hidS = b.sb("a_hidS", [128, 512], F32)
    gatS = b.sb("a_gatS", [128, 512], F32)
    sgS = b.sb("a_sgS", [128, 512], F32)
    mskS = b.sb("a_mskS", [128, 512], F32)
    tS = b.sb("a_tS", [128, 512], F32)
    aT = b.sb("a_aT", [128, 512], F32)
    wT = b.sb("a_wT", [128, 512], F32)
    carry = b.sb("a_carry", [128, 8], F32)
    hbf = b.sb("a_hbf", [128, 8 * 512], BF16)
    a1bS = b.sb("a_a1b", [128, 512], F32)
    bbS = b.sb("a_bb", [128, 512], F32)
    x0r = [b.sb(f"a_x0r{i}", [128, 512], F32) for i in range(2)]
    x1o = [b.sb(f"a_x1o{i}", [128, 512], F32) for i in range(2)]

    sHid = em.sem("a_hid")
    sGat = em.sem("a_gat")
    sScl = em.sem("a_scl")
    sVec = em.sem("a_vec")
    sBc2 = em.sem("a_bc2")
    sBcC = em.sem("a_bcc")
    sOutM = em.sem("a_outm")
    sEv = em.sem("a_ev")
    sX0 = [em.sem(f"a_x0{i}") for i in range(2)]
    sSt = [em.sem(f"a_st{i}") for i in range(2)]

    for tt in range(NTT):
        for pair in range(8):
            pi = tt * 8 + pair
            hb = b.pb[pair % 2]
            gb = b.pb[2 + pair % 2]
            if pi >= 2:
                em.wait("tensor", sScl, pi - 1)
            if pair == 0 and tt >= 1:
                em.wait("tensor", sEv, tt * 8)
            for fc in range(KC):
                em.emit("tensor", lambda E_, fc=fc, pair=pair, tt=tt, hb=hb: E_.matmul(
                    hb[:, 0:512],
                    whgT[:, fc * 2048 + pair * 128: fc * 2048 + pair * 128 + 128],
                    xrms[:, fc * T + tt * 512: fc * T + tt * 512 + 512],
                    start=(fc == 0), stop=(fc == KC - 1)),
                    inc=(sHid, 1) if fc == KC - 1 else None)
            for fc in range(KC):
                em.emit("tensor", lambda E_, fc=fc, pair=pair, tt=tt, gb=gb: E_.matmul(
                    gb[:, 0:512],
                    whgT[:, fc * 2048 + 1024 + pair * 128: fc * 2048 + 1024 + pair * 128 + 128],
                    xrms[:, fc * T + tt * 512: fc * T + tt * 512 + 512],
                    start=(fc == 0), stop=(fc == KC - 1)),
                    inc=(sGat, 1) if fc == KC - 1 else None)
            em.wait("scalar", sHid, pi + 1)
            if pi >= 1:
                em.wait("scalar", sVec, pi)
            em.emit("scalar", lambda E_, hb=hb: E_.copy(out=hidS[:], in_=hb[:, 0:512]))
            em.wait("scalar", sGat, pi + 1)
            em.emit("scalar", lambda E_, gb=gb: E_.copy(out=gatS[:], in_=gb[:, 0:512]))
            em.emit("scalar", lambda E_: E_.activation(
                out=aT[:], in_=gatS[:], func=AF.Sigmoid, bias=0.0, scale=-1.0))
            em.emit("scalar", lambda E_: E_.activation(
                out=sgS[:], in_=hidS[:], func=AF.Sigmoid), inc=(sScl, 1))
            em.wait("vector", sScl, pi + 1)
            if pair == 0 and tt >= 1:
                em.wait("vector", sOutM, tt * 8)
            em.emit("vector", lambda E_: E_.tensor_single_scalar(
                out=mskS[:], in_=hidS[:], scalar=0.0, op=OP.is_ge))
            em.emit("vector", lambda E_: E_.scalar_tensor_tensor(
                out=tS[:], in0=hidS[:], scalar=0.5, in1=sgS[:],
                op0=OP.add, op1=OP.subtract))
            em.emit("vector", lambda E_: E_.tensor_mul(tS[:], mskS[:], tS[:]))
            em.emit("vector", lambda E_: E_.tensor_add(tS[:], sgS[:], tS[:]))
            em.emit("vector", lambda E_: E_.tensor_mul(wT[:], aT[:], tS[:]))
            em.emit("vector", lambda E_: E_.tensor_sub(wT[:], tS[:], wT[:]))
            hdst = hbf[:, pair * 512:(pair + 1) * 512]
            if tt == 0:
                em.emit("vector", lambda E_, hdst=hdst: E_.tensor_tensor_scan(
                    out=hdst, data0=aT[:], data1=wT[:], initial=0.0,
                    op0=OP.mult, op1=OP.add))
            else:
                em.emit("vector", lambda E_, hdst=hdst, pair=pair: E_.tensor_tensor_scan(
                    out=hdst, data0=aT[:], data1=wT[:],
                    initial=carry[:, pair:pair + 1], op0=OP.mult, op1=OP.add))
            em.emit("vector", lambda E_, pair=pair: E_.tensor_copy(
                out=carry[:, pair:pair + 1],
                in_=hbf[:, pair * 512 + 511: pair * 512 + 512]), inc=(sVec, 1))
        ra1 = rows["alpha"][0:1, tt * 512:(tt + 1) * 512]
        rbe = rows["beta"][0:1, tt * 512:(tt + 1) * 512]
        em.wait("tensor", sScl, tt * 8 + 8)
        if tt >= 1:
            em.wait("tensor", sEv, tt * 8)
        em.emit("tensor", lambda E_, ra1=ra1: E_.matmul(
            b.pb[4][:, 0:512], b.cst["c_ones1"][:], ra1, start=True, stop=True))
        em.emit("tensor", lambda E_, rbe=rbe: E_.matmul(
            b.pb[5][:, 0:512], b.cst["c_ones1"][:], rbe, start=True, stop=True),
            inc=(sBc2, 1))
        em.wait("vector", sBc2, tt + 1)
        em.emit("vector", lambda E_: E_.tensor_scalar_add(
            a1bS[:], b.pb[4][:, 0:512], 1.0))
        em.emit("vector", lambda E_: E_.tensor_copy(out=bbS[:], in_=b.pb[5][:, 0:512]),
                inc=(sBcC, 2))
        em.wait("tensor", sVec, tt * 8 + 8)
        em.wait("tensor", sBcC, tt * 2 + 2)
        if tt >= 1:
            em.wait("tensor", sEv, tt * 8)
        for oc in range(KC):
            for kc in range(KC):
                em.emit("tensor", lambda E_, oc=oc, kc=kc: E_.matmul(
                    b.pb[oc][:, 0:512],
                    woutT[:, kc * 1024 + oc * 128: kc * 1024 + oc * 128 + 128],
                    hbf[:, kc * 512:(kc + 1) * 512],
                    start=(kc == 0), stop=(kc == KC - 1)),
                    inc=(sOutM, 1) if kc == KC - 1 else None)
        for oc in range(KC):
            i = tt * KC + oc
            if i >= 2:
                em.wait("sync", sEv, i - 1)
            em.emit("sync", lambda E_, oc=oc, tt=tt, slot=i % 2: E_.dma_start(
                out=x0r[slot][:], in_=x0_hbm[tt, oc]),
                inc=(sX0[i % 2], 16))
            em.wait("vector", sOutM, tt * KC + oc + 1)
            em.wait("vector", sX0[i % 2], (i // 2 + 1) * 16)
            if i >= 2:
                em.wait("vector", sSt[i % 2], (i // 2) * 16)
            em.emit("vector", lambda E_, slot=i % 2: E_.tensor_mul(
                tS[:], x0r[slot][:], a1bS[:]))
            em.emit("vector", lambda E_: E_.tensor_add(tS[:], tS[:], bbS[:]))
            em.emit("vector", lambda E_, oc=oc, slot=i % 2: E_.tensor_tensor(
                out=x1o[slot][:], in0=tS[:], in1=b.pb[oc][:, 0:512], op=OP.add),
                inc=(sEv, 1))
            em.wait("sync", sEv, i + 1)
            em.emit("sync", lambda E_, oc=oc, tt=tt, slot=i % 2: E_.dma_start(
                out=x1_hbm[tt, oc], in_=x1o[slot][:]), inc=(sSt[i % 2], 16))
    em.join([(sSt[0], em.counts[sSt[0]]), (sSt[1], em.counts[sSt[1]]),
             (sEv, em.counts[sEv])])
    b.pop()


def emit_phase_b(b, x1_hbm, x2_hbm):
    em, nc, T, NTT, NTC = b.em, b.nc, b.T, b.NTT, b.NTC
    pfx = "bb"
    wr_h = b.din("L0_ffn_Wr", [D, E])
    wg_h = b.din("L0_ffn_Wg", [E, H // 256, 128, KC * 256])
    wu_h = b.din("L0_ffn_Wu", [E, H // 256, 128, KC * 256])
    wd_h = b.din("L0_ffn_Wd", [E, H, D])

    rows = {}
    xn = b.sb("b_xn", [128, KC * T], BF16)
    Mc = b.sb("b_Mc", [8, T], F32)
    emit_ln_pass(b, pfx, x1_hbm, rows, want_rms=False, xn_sb=xn,
                 xn_rows=("alpha", "beta"))

    em.ser("sync", lambda E_: E_.dma_start(out=x2_hbm[:], in_=x1_hbm[:]), dma=True)

    b.push()
    wr_sb = b.sb("b_wr", [128, 8 * 8], F32)
    wrb = b.sb("b_wrb", [128, 8 * 8], BF16)
    logT = b.sb("b_logT", [8, T], F32)
    for fc in range(KC):
        em.ser("sync", lambda E_, fc=fc: E_.dma_start(
            out=wr_sb[:, fc * 8:(fc + 1) * 8], in_=wr_h[fc * 128:(fc + 1) * 128, :]),
            dma=True)
    em.ser("scalar", lambda E_: E_.copy(out=wrb[:], in_=wr_sb[:]))
    for tt in range(NTT):
        for fc in range(KC):
            em.ser("tensor", lambda E_, fc=fc, tt=tt: E_.matmul(
                b.pb[6][0:8, 0:512], wrb[:, fc * 8:(fc + 1) * 8],
                xn[:, fc * T + tt * 512: fc * T + tt * 512 + 512],
                start=(fc == 0), stop=(fc == KC - 1)))
        em.ser("vector", lambda E_, tt=tt: E_.tensor_copy(
            out=logT[0:8, tt * 512:(tt + 1) * 512], in_=b.pb[6][0:8, 0:512]))

    lg = b.sb("b_lg", [128, 8], F32)
    c1 = b.sb("b_c1", [128, 8], F32)
    c2 = b.sb("b_c2", [128, 8], F32)
    cc = b.sb("b_cc", [128, 8], F32)
    m1 = b.sb("b_m1", [128, 6], F32)
    for tc in range(NTC):
        em.ser("tensor", lambda E_, tc=tc: E_.transpose(
            b.pb[6][:, 0:8], logT[0:8, tc * 128:(tc + 1) * 128],
            b.cst["c_ident"][0:8, 0:8]))
        em.ser("vector", lambda E_: E_.tensor_copy(out=lg[:], in_=b.pb[6][:, 0:8]))
        em.ser("vector", lambda E_: E_.tensor_reduce(
            out=m1[:, 0:1], in_=lg[:], axis=AX, op=OP.max))
        em.ser("vector", lambda E_: E_.tensor_scalar(
            out=c1[:], in0=lg[:], scalar1=m1[:, 0:1], scalar2=None, op0=OP.is_equal))
        em.ser("vector", lambda E_: E_.tensor_mul(c1[:], c1[:], b.cst["c_p2row"][:]))
        em.ser("vector", lambda E_: E_.tensor_reduce(
            out=m1[:, 1:2], in_=c1[:], axis=AX, op=OP.max))
        em.ser("vector", lambda E_: E_.tensor_scalar(
            out=c1[:], in0=c1[:], scalar1=m1[:, 1:2], scalar2=None, op0=OP.is_equal))
        em.ser("vector", lambda E_: E_.scalar_tensor_tensor(
            out=cc[:], in0=c1[:], scalar=-1e30, in1=lg[:], op0=OP.mult, op1=OP.add))
        em.ser("vector", lambda E_: E_.tensor_reduce(
            out=m1[:, 2:3], in_=cc[:], axis=AX, op=OP.max))
        em.ser("vector", lambda E_: E_.tensor_scalar(
            out=c2[:], in0=cc[:], scalar1=m1[:, 2:3], scalar2=None, op0=OP.is_equal))
        em.ser("vector", lambda E_: E_.tensor_mul(c2[:], c2[:], b.cst["c_p2row"][:]))
        em.ser("vector", lambda E_: E_.tensor_reduce(
            out=m1[:, 3:4], in_=c2[:], axis=AX, op=OP.max))
        em.ser("vector", lambda E_: E_.tensor_scalar(
            out=c2[:], in0=c2[:], scalar1=m1[:, 3:4], scalar2=None, op0=OP.is_equal))
        em.ser("vector", lambda E_: E_.tensor_sub(m1[:, 4:5], m1[:, 0:1], m1[:, 2:3]))
        em.ser("scalar", lambda E_: E_.activation(
            out=m1[:, 4:5], in_=m1[:, 4:5], func=AF.Sigmoid))
        em.ser("vector", lambda E_: E_.tensor_scalar_mul(
            m1[:, 4:5], m1[:, 4:5], DEPTH_SCALE))
        em.ser("vector", lambda E_: E_.tensor_scalar(
            out=m1[:, 5:6], in0=m1[:, 4:5], scalar1=-1.0, scalar2=DEPTH_SCALE,
            op0=OP.mult, op1=OP.add))
        em.ser("vector", lambda E_: E_.tensor_scalar(
            out=cc[:], in0=c1[:], scalar1=m1[:, 4:5], scalar2=None, op0=OP.mult))
        em.ser("vector", lambda E_: E_.scalar_tensor_tensor(
            out=cc[:], in0=c2[:], scalar=m1[:, 5:6], in1=cc[:],
            op0=OP.mult, op1=OP.add))
        em.ser("tensor", lambda E_, tc=tc: E_.transpose(
            b.pb[6][0:8, 0:128], cc[:], b.cst["c_ident"][:]))
        em.ser("vector", lambda E_, tc=tc: E_.tensor_copy(
            out=Mc[0:8, tc * 128:(tc + 1) * 128], in_=b.pb[6][0:8, 0:128]))
    em.join([(em.cch, em.counts[em.cch]), (em.dch, em.counts[em.dch])])
    b.pop()

    b.push()
    emit_moe_loop(b, "b", xn, Mc, x2_hbm, wg_h, wu_h, wd_h, None, None,
                  silu_glu=True)
    b.pop()


def emit_moe_loop(b, pfx, xn, Mc, xout_hbm, w1_h, wu_h, w2_h, b1cols, b2cols,
                  silu_glu):
    em, T = b.em, b.T
    em.join([(em.cch, em.counts[em.cch]), (em.dch, em.counts[em.dch])])
    TB = 512
    NTB = T // TB
    TBG = 2
    NPASS = NTB // TBG
    HQ8 = 16
    QW = H // HQ8
    HQC = HC // HQ8

    wgb = [b.sb(f"{pfx}_wgb{i}", [128, KC * QW], BF16) for i in range(2)]
    wub = ([b.sb(f"{pfx}_wub{i}", [128, KC * QW], BF16) for i in range(2)]
           if silu_glu else None)
    NWR = 4
    wdb = [b.sb(f"{pfx}_wdb{i}", [128, D], BF16) for i in range(NWR)]
    h_all = [b.sb(f"{pfx}_h{i}", [128, HC * TB], BF16) for i in range(TBG)]
    acc = b.sb(f"{pfx}_acc", [128, KC * TB], F32)
    ce_t = [b.sb(f"{pfx}_ce{i}", [128, TB], F32) for i in range(TBG)]
    silu_t = [b.sb(f"{pfx}_si{i}", [128, TB], BF16) for i in range(2)]
    evt = ([b.sb(f"{pfx}_evt{i}", [128, TB], F32) for i in range(2)]
           if b2cols is not None else None)

    sW = [em.sem(f"{pfx}_w{i}") for i in range(2)]
    sWD = [em.sem(f"{pfx}_wd{i}") for i in range(NWR)]
    sWDF = em.sem(f"{pfx}_wdf")
    sG = em.sem(f"{pfx}_g")
    sU = em.sem(f"{pfx}_u")
    sGF = em.sem(f"{pfx}_gf")
    sH = em.sem(f"{pfx}_hh")
    sD = em.sem(f"{pfx}_d")
    sCe = em.sem(f"{pfx}_cem")
    sCeC = em.sem(f"{pfx}_cec")
    sB2 = em.sem(f"{pfx}_b2")
    sEv = em.sem(f"{pfx}_ev")
    sSt = [em.sem(f"{pfx}_sb{i}") for i in range(NTB)]
    hdone = sH if silu_glu else sGF

    per_q = 2 * 16 if silu_glu else 16
    NQ = E * NPASS * HQ8

    def load_q(q, par):
        e2 = q // (NPASS * HQ8)
        hq2 = q % HQ8
        em.emit("gpsimd", lambda E_, e2=e2, hq2=hq2, par=par: E_.dma_start(
            out=wgb[par][:], in_=w1_h[e2, hq2]), inc=(sW[par], 16))
        if silu_glu:
            em.emit("gpsimd", lambda E_, e2=e2, hq2=hq2, par=par: E_.dma_start(
                out=wub[par][:], in_=wu_h[e2, hq2]), inc=(sW[par], 16))

    load_q(0, 0)
    q_loaded = 1
    gi_c = [0]
    store_done = {}

    for e in range(E):
        for p in range(NPASS):
            pi = e * NPASS + p
            tbs = [p * TBG + i for i in range(TBG)]
            for i2, tb in enumerate(tbs):
                t0 = tb * TB
                bank = b.pb[6 + i2]
                if pi >= 1:
                    em.wait("tensor", sEv, pi * TBG * KC)
                em.emit("tensor", lambda E_, e=e, t0=t0, bank=bank: E_.matmul(
                    bank[:, 0:TB], b.cst["c_oneh"][0:8, e * 128:(e + 1) * 128],
                    Mc[0:8, t0:t0 + TB], start=True, stop=True), inc=(sCe, 1))
                cCe = em.counts[sCe]
                em.wait("vector", sCe, cCe)
                em.emit("vector", lambda E_, i2=i2, bank=bank: E_.tensor_copy(
                    out=ce_t[i2][:], in_=bank[:, 0:TB]), inc=(sCeC, 1))
            for hq in range(HQ8):
                qi = pi * HQ8 + hq
                par = qi % 2
                if q_loaded < NQ:
                    if q_loaded >= 2:
                        em.wait("gpsimd", sU if silu_glu else sG,
                                (q_loaded - 1) * HQC * TBG)
                    load_q(q_loaded, q_loaded % 2)
                    q_loaded += 1
                em.wait("tensor", sW[par], (qi // 2 + 1) * per_q)
                for hc4 in range(HQC):
                    hc = hq * HQC + hc4
                    for i2, tb in enumerate(tbs):
                        t0 = tb * TB
                        gi = gi_c[0]
                        gi_c[0] += 1
                        gb = b.pb[gi % 2]
                        ub = b.pb[2 + gi % 2] if silu_glu else None
                        if gi >= 2:
                            em.wait("tensor", sGF, gi - 1)
                            if silu_glu:
                                em.wait("tensor", sH, gi - 1)
                        if hc == 0 and i2 == 0 and pi >= 1:
                            em.wait("tensor", sEv, pi * TBG * KC)
                        for kc in range(KC):
                            em.emit("tensor", lambda E_, par=par, kc=kc, hc4=hc4, gb=gb, t0=t0: E_.matmul(
                                gb[:, 0:TB],
                                wgb[par][:, kc * QW + hc4 * 128: kc * QW + hc4 * 128 + 128],
                                xn[:, kc * T + t0: kc * T + t0 + TB],
                                start=(kc == 0), stop=(kc == KC - 1)),
                                inc=(sG, 1) if kc == KC - 1 else None)
                        if silu_glu:
                            for kc in range(KC):
                                em.emit("tensor", lambda E_, par=par, kc=kc, hc4=hc4, ub=ub, t0=t0: E_.matmul(
                                    ub[:, 0:TB],
                                    wub[par][:, kc * QW + hc4 * 128: kc * QW + hc4 * 128 + 128],
                                    xn[:, kc * T + t0: kc * T + t0 + TB],
                                    start=(kc == 0), stop=(kc == KC - 1)),
                                    inc=(sU, 1) if kc == KC - 1 else None)
                        em.wait("scalar", sG, gi + 1)
                        if silu_glu:
                            if gi >= 2:
                                em.wait("scalar", sH, gi - 1)
                            em.emit("scalar", lambda E_, gi=gi, gb=gb: E_.activation(
                                out=silu_t[gi % 2][:], in_=gb[:, 0:TB], func=AF.Silu),
                                inc=(sGF, 1))
                            em.wait("vector", sU, gi + 1)
                            em.wait("vector", sGF, gi + 1)
                            em.emit("vector", lambda E_, gi=gi, ub=ub, hc=hc, i2=i2: E_.tensor_tensor(
                                out=h_all[i2][:, hc * TB:(hc + 1) * TB],
                                in0=silu_t[gi % 2][:], in1=ub[:, 0:TB], op=OP.mult),
                                inc=(sH, 1))
                        else:
                            em.emit("scalar", lambda E_, e=e, hc=hc, gb=gb, i2=i2: E_.activation(
                                out=h_all[i2][:, hc * TB:(hc + 1) * TB], in_=gb[:, 0:TB],
                                func=AF.Gelu, bias=b1cols[:, e * HC + hc: e * HC + hc + 1],
                                scale=1.0), inc=(sGF, 1))
            for i2, tb in enumerate(tbs):
                di = pi * TBG + i2
                t0 = tb * TB
                for hc in range(HC):
                    wi = di * HC + hc
                    if wi >= NWR:
                        w2 = wi - NWR
                        if w2 % HC == HC - 1:
                            em.wait("gpsimd", sD, (w2 // HC + 1) * KC)
                        else:
                            em.wait("gpsimd", sWDF, (w2 // HC) * (HC - 1) + (w2 % HC) + 1)
                    em.emit("gpsimd", lambda E_, e=e, hc=hc, r=wi % NWR: E_.dma_start(
                        out=wdb[r][:], in_=w2_h[e, hc * 128:(hc + 1) * 128, :]),
                        inc=(sWD[wi % NWR], 16))
                for hc in range(HC):
                    wi = di * HC + hc
                    par = wi % NWR
                    em.wait("tensor", sWD[par], (wi // NWR + 1) * 16)
                    if hc == 0:
                        em.wait("tensor", hdone, (pi + 1) * HQ8 * HQC * TBG)
                        if silu_glu:
                            em.wait("tensor", sGF, (pi + 1) * HQ8 * HQC * TBG)
                        em.wait("tensor", sEv, di * KC)
                        em.wait("tensor", sCeC, (pi + 1) * TBG)
                    for dc in range(KC):
                        if hc == HC - 1:
                            inc = (sD, 1)
                        elif dc == KC - 1:
                            inc = (sWDF, 1)
                        else:
                            inc = None
                        em.emit("tensor", lambda E_, par=par, dc=dc, hc=hc, i2=i2: E_.matmul(
                            b.pb[dc][:, 0:TB],
                            wdb[par][:, dc * 128:(dc + 1) * 128],
                            h_all[i2][:, hc * TB:(hc + 1) * TB],
                            start=(hc == 0), stop=(hc == HC - 1)), inc=inc)
                cDbase = di * KC
                for dc in range(KC):
                    em.wait("vector", sD, cDbase + dc + 1)
                    if dc == 0:
                        for tbx, cnt in store_done.items():
                            em.wait("vector", sSt[tbx], cnt)
                    if b2cols is None:
                        em.emit("vector", lambda E_, dc=dc, i2=i2: E_.tensor_tensor(
                            out=acc[:, dc * TB:(dc + 1) * TB], in0=b.pb[dc][:, 0:TB],
                            in1=ce_t[i2][:], op=OP.mult), inc=(sEv, 1))
                    else:
                        nv = cDbase + dc
                        em.wait("scalar", sD, nv + 1)
                        if nv >= 2:
                            em.wait("scalar", sEv, nv - 1)
                        em.emit("scalar", lambda E_, dc=dc, e=e, p=nv % 2: E_.add(
                            out=evt[p][:], in_=b.pb[dc][:, 0:TB],
                            add=b2cols[:, e * KC + dc: e * KC + dc + 1]), inc=(sB2, 1))
                        em.wait("vector", sB2, nv + 1)
                        em.emit("vector", lambda E_, dc=dc, i2=i2, p=nv % 2: E_.tensor_tensor(
                            out=acc[:, dc * TB:(dc + 1) * TB], in0=evt[p][:],
                            in1=ce_t[i2][:], op=OP.mult), inc=(sEv, 1))
                em.wait("gpsimd", sEv, em.counts[sEv])
                em.wait("gpsimd", sSt[tb], em.counts[sSt[tb]])
                for dc in range(KC):
                    em.emit("gpsimd", lambda E_, dc=dc, tb=tb: E_.dma_start(
                        out=xout_hbm[tb, dc],
                        in_=acc[:, dc * TB:(dc + 1) * TB], accum_op=OP.add),
                        inc=(sSt[tb], 16))
                store_done[tb] = em.counts[sSt[tb]]
    em.join([(s, em.counts[s]) for s in sSt])


def emit_phase_c(b, x2_hbm, x3_hbm):
    em, nc, T, NTT, NTC = b.em, b.nc, b.T, b.NTT, b.NTC
    pfx = "cc"
    wq_h = b.din("L1_attn_Wq", [D, D])
    bq_h = b.din("L1_attn_bq", [D])
    wk_h = b.din("L1_attn_Wk", [D, D])
    bk_h = b.din("L1_attn_bk", [D])
    wv_h = b.din("L1_attn_Wv", [D, D])
    bv_h = b.din("L1_attn_bv", [D])
    wf_h = b.din("L1_attn_Wf", [2, KC, 64, 8 * 128])
    bf_h = b.din("L1_attn_bf", [D])

    rows = alloc_rows(b, pfx, ["alpha", "beta"], persist=("alpha", "beta"),
                      persist_dtype=BF16)
    em.ser("sync", lambda E_: E_.dma_start(out=x3_hbm[:], in_=x2_hbm[:]), dma=True)
    cpy_d = em.counts[em.dch]
    emit_ln_pass(b, pfx, x2_hbm, rows, want_rms=False)

    VST = 8 * 65
    kTh = b.sb("c_kTh", [128, 4 * T], BF16)
    qTh = b.sb("c_qTh", [128, 4 * T], BF16)
    vh = b.sb("c_vh", [128, NTC * VST], BF16)
    lrow = b.sb("c_lrow", [1, 512], F32)
    bqc = b.sb("c_bqc", [128, KC], F32)
    bkc = b.sb("c_bkc", [128, KC], F32)
    bfc = b.sb("c_bfc", [128, KC], F32)
    bvb = b.sb("c_bvb", [128, D], BF16)
    nrm = b.sb("c_nrm", [2, 8], F32)
    mc1 = b.sb("c_mc1", [2, 1], F32)
    bndc = b.sb("c_bndc", [2, 4], F32)
    bcolsP = [b.sb(f"c_bcol{i}", [128, 4], F32) for i in range(2)]
    attnh = b.sb("c_attnh", [64, 8 * 512], BF16)
    wfb = [b.sb(f"c_wfb{i}", [64, 8 * 128], BF16) for i in range(2)]
    dnames = [f"c_diag_{d}{bb}" for d in ("f", "b") for bb in range(4)]
    dnames += ["c_rel_kq512", "c_ones64"]
    for dname in dnames:
        b.cst[dname] = b.sb(dname + "_sb", list(CONST_SPECS[dname]), BF16)
    b.push()
    stage = b.sb("c_stage", [128, 512], F32)
    for dname in dnames:
        h = b.din(dname, CONST_SPECS[dname])
        ncol = CONST_SPECS[dname][1]
        em.ser("sync", lambda E_, h=h, ncol=ncol: E_.dma_start(
            out=stage[:, 0:ncol], in_=h[:]), dma=True)
        em.ser("scalar", lambda E_, t=b.cst[dname], ncol=ncol: E_.copy(
            out=t[:], in_=stage[:, 0:ncol]))
    for hh in range(2):
        em.ser("sync", lambda E_, hh=hh: E_.dma_start(
            out=stage[0:1, 0:512],
            in_=bv_h.ap().rearrange("(o d) -> o d", o=2)[hh:hh + 1, :]), dma=True)
        em.ser("tensor", lambda E_: E_.matmul(
            b.pb[6][:, 0:512], b.cst["c_ones1"][:], stage[0:1, 0:512],
            start=True, stop=True))
        em.ser("vector", lambda E_, hh=hh: E_.tensor_copy(
            out=bvb[:, hh * 512:(hh + 1) * 512], in_=b.pb[6][:, 0:512]))
    em.barrier()
    b.pop()
    wbuf = b.sb("c_wbuf", [128, KC * D], BF16)
    x2t = [b.sb(f"c_x2t{i}", [128, 2 * 512], F32) for i in range(2)]
    xnt = [b.sb(f"c_xnt{i}", [128, KC * 512], BF16) for i in range(2)]
    srelP = [b.sb(f"c_srelP{i}", [128, 512], BF16) for i in range(2)]
    offc = [b.sb(f"c_offc{i}", [128, 64], F32) for i in range(2)]
    offc2 = [b.sb(f"c_offc2{i}", [128, 64], F32) for i in range(2)]
    diagP = [[b.sb(f"c_diagP{i}_{hx}", [128, 512], BF16) for i in range(4)]
             for hx in range(2)]
    scoreS = [x2t[0][:, 0:512], x2t[1][:, 0:512], x2t[0][:, 512:1024]]
    rb = x2t[1][0:64, 512:1024]
    sq2 = scoreS[0:2]
    pT = [b.sb(f"c_pT{i}", [128, 512], BF16) for i in range(3)]
    st3 = [b.sb(f"c_st3{i}", [128, 512], F32) for i in range(2)]
    txf = b.sb("c_txf", [128, 512], F32)
    slopes = alibi_slopes()

    for vh_, vc_, vn_ in ((bq_h, bqc, "c_bqt"), (bk_h, bkc, "c_bkt"),
                          (bf_h, bfc, "c_bft")):
        b.push()
        vec2cols(b, vh_, D, vc_, vn_)
        em.barrier()
        b.pop()

    for tc in range(NTC):
        for h8 in range(8):
            em.emit("vector", lambda E_, tc=tc, h8=h8: E_.memset(
                vh[:, tc * VST + h8 * 65 + 64: tc * VST + h8 * 65 + 65], 1.0))

    sXL = [em.sem(f"c_xl{i}") for i in range(2)]
    sXB = em.sem("c_xb")
    sBC2 = em.sem("c_bc2")
    sKB = em.sem("c_kb")
    sKE = em.sem("c_ke")
    sWB = em.sem("c_wbl")
    sSq = em.sem("c_sq2")
    sNmm = em.sem("c_nmm")
    sRed = em.sem("c_red")
    sS = em.sem("c_sS")
    sTx = em.sem("c_stx")
    sPe = em.sem("c_spe")
    sPv = em.sem("c_spv")
    sRcp = em.sem("c_rcp")
    sRbm = em.sem("c_rbm")
    sPat = em.sem("c_pat")
    sEvh = em.sem("c_evh")
    sWF = em.sem("c_swf")
    sF = em.sem("c_sf")
    sFe = em.sem("c_sfe")
    sV3m = em.sem("c_v3m")
    sSt3 = [em.sem(f"c_st3s{i}") for i in range(2)]
    st = {"xc": 0, "xt": 0}

    first_store = [True]
    ihead = [0]
    cF_attnh = [0]

    for half in range(2):
        hbase = half * 8
        em.join([(s, em.counts[s]) for s in
                 (sKE, sKB, sXB, sBC2, sWB, sS, sTx, sPe, sPv, sRcp, sRbm,
                  sEvh, sWF, sF, sFe, sV3m, sSt3[0], sSt3[1], sXL[0], sXL[1],
                  sSq, sNmm, sRed, em.cch, em.dch)]
                + [(s, em.counts[s]) for s in em.selfsem.values()])

        mxb = txf

        def build_xnt(tt, guard=0):
            ti = st["xt"]
            st["xt"] += 1
            p = ti % 2
            if guard > 0:
                em.wait("vector", sKE, guard)
            ral = rows["alpha"][0:1, tt * 512:(tt + 1) * 512]
            rbe = rows["beta"][0:1, tt * 512:(tt + 1) * 512]
            for r in range(4):
                i = st["xc"]
                st["xc"] += 1
                pr = i % 2
                if i >= 2:
                    em.wait("gpsimd", sXB, i - 1)
                for fc4 in range(2):
                    fc = r * 2 + fc4
                    em.emit("gpsimd", lambda E_, tt=tt, pr=pr, fc=fc, fc4=fc4: E_.dma_start(
                        out=x2t[pr][:, fc4 * 512:(fc4 + 1) * 512],
                        in_=x2_hbm[tt, fc]), inc=(sXL[pr], 16))
                cL = em.counts[sXL[pr]]
                if r == 0:
                    em.wait("tensor", sXB, 4 * ti)
                    em.emit("tensor", lambda E_, ral=ral: E_.matmul(
                        b.pb[2][:, 0:512], b.cst["c_ones1b"][:], ral,
                        start=True, stop=True))
                    em.emit("tensor", lambda E_, rbe=rbe: E_.matmul(
                        b.pb[3][:, 0:512], b.cst["c_ones1b"][:], rbe,
                        start=True, stop=True), inc=(sBC2, 1))
                    cB = em.counts[sBC2]
                    em.wait("vector", sBC2, cB)
                em.wait("vector", sXL[pr], cL)
                for fc4 in range(2):
                    fc = r * 2 + fc4
                    em.emit("vector", lambda E_, fc=fc, fc4=fc4, pr=pr: E_.tensor_tensor(
                        out=mxb[:], in0=x2t[pr][:, fc4 * 512:(fc4 + 1) * 512],
                        in1=b.pb[2][:, 0:512], op=OP.mult))
                    em.emit("vector", lambda E_, fc=fc, p=p: E_.tensor_tensor(
                        out=xnt[p][:, fc * 512:(fc + 1) * 512], in0=mxb[:],
                        in1=b.pb[3][:, 0:512], op=OP.add),
                        inc=(sXB, 1) if fc4 == 1 else None)
            return p, em.counts[sXB]

        def load_w(wh):
            em.wait("gpsimd", sKB, em.counts[sKB])
            for fc in range(KC):
                em.emit("gpsimd", lambda E_, fc=fc, wh=wh: E_.dma_start(
                    out=wbuf[:, fc * D:(fc + 1) * D],
                    in_=wh[fc * 128:(fc + 1) * 128, :]), inc=(sWB, 16))
            return em.counts[sWB]

        wk_d = load_w(wk_h)
        cKE0 = em.counts[sKE]
        for tt in range(NTT):
            p, cXB = build_xnt(tt, cKE0 + 4 * (tt - 1) if tt >= 2 else 0)
            em.wait("tensor", sXB, cXB)
            if tt == 0:
                em.wait("tensor", sWB, wk_d)
            for c4 in range(4):
                dcg = half * 4 + c4
                bank = b.pb[4 + c4 % 2]
                if em.counts[sKB] >= 2:
                    em.wait("tensor", sKE, em.counts[sKB] - 1)
                for fc in range(KC):
                    em.emit("tensor", lambda E_, fc=fc, dcg=dcg, p=p, bank=bank: E_.matmul(
                        bank[:, 0:512],
                        wbuf[:, fc * D + dcg * 128: fc * D + dcg * 128 + 128],
                        xnt[p][:, fc * 512:(fc + 1) * 512],
                        start=(fc == 0), stop=(fc == KC - 1)),
                        inc=(sKB, 1) if fc == KC - 1 else None)
                cKB = em.counts[sKB]
                em.wait("scalar", sKB, cKB)
                em.emit("scalar", lambda E_, c4=c4, tt=tt, dcg=dcg, bank=bank: E_.add(
                    out=kTh[:, c4 * T + tt * 512: c4 * T + tt * 512 + 512],
                    in_=bank[:, 0:512], add=bkc[:, dcg:dcg + 1]), inc=(sKE, 1))
        wv_d = load_w(wv_h)
        cKE0 = em.counts[sKE]
        for tt in range(NTT):
            p, cXB = build_xnt(tt, cKE0 + 4 * (tt - 1) if tt >= 2 else 0)
            em.wait("tensor", sXB, cXB)
            if tt == 0:
                em.wait("tensor", sWB, wv_d)
                em.wait("tensor", sKE, em.counts[sKE])
            for tc4 in range(4):
                tc = tt * 4 + tc4
                bank = b.pb[4 + tc4 % 2]
                if em.counts[sKB] >= 2:
                    em.wait("tensor", sKE, em.counts[sKB] - 1)
                for fc in range(KC):
                    em.emit("tensor", lambda E_, fc=fc, tc4=tc4, p=p, bank=bank, hbase=hbase: E_.matmul(
                        bank[:, 0:512],
                        xnt[p][:, fc * 512 + tc4 * 128: fc * 512 + tc4 * 128 + 128],
                        wbuf[:, fc * D + hbase * 64: fc * D + hbase * 64 + 512],
                        start=(fc == 0), stop=(fc == KC - 1)),
                        inc=(sKB, 1) if fc == KC - 1 else None)
                cKB = em.counts[sKB]
                em.wait("vector", sKB, cKB)
                for h8 in range(8):
                    em.emit("vector", lambda E_, tc=tc, h8=h8, bank=bank, half=half: E_.tensor_tensor(
                        out=vh[:, tc * VST + h8 * 65: tc * VST + h8 * 65 + 64],
                        in0=bank[:, h8 * 64:(h8 + 1) * 64],
                        in1=bvb[:, half * 512 + h8 * 64: half * 512 + h8 * 64 + 64],
                        op=OP.add), inc=(sKE, 1) if h8 == 7 else None)
        wq_d = load_w(wq_h)
        cKE0 = em.counts[sKE]
        for tt in range(NTT):
            p, cXB = build_xnt(tt, cKE0 + 4 * (tt - 1) if tt >= 2 else 0)
            em.wait("tensor", sXB, cXB)
            if tt == 0:
                em.wait("tensor", sWB, wq_d)
                em.wait("tensor", sKE, em.counts[sKE])
            for c4 in range(4):
                dcg = half * 4 + c4
                bank = b.pb[4 + c4 % 2]
                if em.counts[sKB] >= 2:
                    em.wait("tensor", sKE, em.counts[sKB] - 1)
                for fc in range(KC):
                    em.emit("tensor", lambda E_, fc=fc, dcg=dcg, p=p, bank=bank: E_.matmul(
                        bank[:, 0:512],
                        wbuf[:, fc * D + dcg * 128: fc * D + dcg * 128 + 128],
                        xnt[p][:, fc * 512:(fc + 1) * 512],
                        start=(fc == 0), stop=(fc == KC - 1)),
                        inc=(sKB, 1) if fc == KC - 1 else None)
                cKB = em.counts[sKB]
                em.wait("scalar", sKB, cKB)
                em.emit("scalar", lambda E_, c4=c4, tt=tt, dcg=dcg, bank=bank: E_.add(
                    out=qTh[:, c4 * T + tt * 512: c4 * T + tt * 512 + 512],
                    in_=bank[:, 0:512], add=bqc[:, dcg:dcg + 1]), inc=(sKE, 1))
        em.join([(s, em.counts[s]) for s in
                 (sKE, sKB, sXB, sBC2, sWB, sXL[0], sXL[1], em.cch, em.dch)]
                + [(s, em.counts[s]) for s in em.selfsem.values()])

        for si, src in enumerate((kTh, qTh)):
            for hc2 in range(4):
                for tt in range(NTT):
                    i = em.counts[sSq]
                    if i >= 2:
                        em.wait("scalar", sNmm, i - 1)
                    em.emit("scalar", lambda E_, src=src, hc2=hc2, tt=tt, p=i % 2: E_.square(
                        out=sq2[p][:],
                        in_=src[:, hc2 * T + tt * 512: hc2 * T + tt * 512 + 512]),
                        inc=(sSq, 1))
                    em.wait("tensor", sSq, i + 1)
                    if i >= 2:
                        em.wait("tensor", sRed, i - 1)
                    em.emit("tensor", lambda E_, p=i % 2, bk=b.pb[6 + i % 2]: E_.matmul(
                        bk[0:2, 0:512], b.cst["c_bones2"][:], sq2[p][:],
                        start=True, stop=True), inc=(sNmm, 1))
                    em.wait("vector", sNmm, i + 1)
                    col = nrm[0:2, si * 4 + hc2: si * 4 + hc2 + 1]
                    if tt == 0:
                        em.emit("vector", lambda E_, col=col, bk=b.pb[6 + i % 2]: E_.tensor_reduce(
                            out=col, in_=bk[0:2, 0:512], axis=AX, op=OP.max),
                            inc=(sRed, 1))
                    else:
                        em.emit("vector", lambda E_, bk=b.pb[6 + i % 2]: E_.tensor_reduce(
                            out=mc1[:], in_=bk[0:2, 0:512], axis=AX, op=OP.max))
                        em.emit("vector", lambda E_, col=col: E_.tensor_tensor(
                            out=col, in0=col, in1=mc1[:], op=OP.max), inc=(sRed, 1))
        em.join([(sSq, em.counts[sSq]), (sNmm, em.counts[sNmm]),
                 (sRed, em.counts[sRed])]
                + [(s, em.counts[s]) for s in em.selfsem.values()])
        em.ser("scalar", lambda E_: E_.sqrt(out=nrm[:], in_=nrm[:]))
        em.ser("vector", lambda E_: E_.tensor_mul(bndc[:], nrm[0:2, 0:4],
                                                  nrm[0:2, 4:8]))
        em.ser("vector", lambda E_: E_.tensor_scalar(
            out=bndc[:], in0=bndc[:], scalar1=0.125, scalar2=1.0,
            op0=OP.mult, op1=OP.add))
        for hp in range(2):
            em.ser("tensor", lambda E_, hp=hp: E_.matmul(
                b.pb[6][:, 0:4], b.cst["c_oneh"][0:2, hp * 128:(hp + 1) * 128],
                bndc[:], start=True, stop=True))
            em.ser("vector", lambda E_, hp=hp: E_.tensor_copy(
                out=bcolsP[hp][:], in_=b.pb[6][:, 0:4]))

        stt_exp = []
        for qt in range(NTT):
            for hpair in range(4):
                heads = (2 * hpair, 2 * hpair + 1)
                hc2 = hpair
                fwd = (hbase + heads[0]) < 8
                slope = float(slopes[hc2 * 2]), float(slopes[hc2 * 2 + 1])
                slh = [(s if fwd else -s) for s in slope]
                shallow = hpair >= 2
                em.wait("vector", sPe, em.counts[sPe])
                for hx in range(2):
                    sl = slh[hx]
                    bcol = bcolsP[heads[hx] % 2][:, hc2:hc2 + 1]
                    if shallow:
                        em.emit("vector", lambda E_, sl=sl, bcol=bcol, hx=hx: E_.tensor_scalar(
                            out=offc2[hx][:], in0=b.cst["c_doff2"][:], scalar1=sl,
                            scalar2=bcol, op0=OP.mult, op1=OP.subtract),
                            inc=(sPat, 1))
                        if hx == 0:
                            for bb in range(4):
                                dname = (f"c_diag_f{bb}" if fwd else f"c_diag_b{bb}")
                                thr = -1e9 if fwd else 1e9
                                opc = OP.is_le if fwd else OP.is_ge
                                em.emit("vector", lambda E_, dname=dname, bb=bb, thr=thr, opc=opc: E_.tensor_single_scalar(
                                    out=diagP[0][bb][:], in_=b.cst[dname][:],
                                    scalar=thr, op=opc))
                                em.emit("vector", lambda E_, bb=bb: E_.tensor_scalar_mul(
                                    diagP[0][bb][:], diagP[0][bb][:], -1e10))
                    else:
                        em.emit("vector", lambda E_, sl=sl, bcol=bcol, hx=hx: E_.tensor_scalar(
                            out=srelP[hx][:], in0=b.cst["c_rel_kq512"][:], scalar1=sl,
                            scalar2=bcol, op0=OP.mult, op1=OP.subtract))
                        for bb in range(4):
                            dname = (f"c_diag_f{bb}" if fwd else f"c_diag_b{bb}")
                            em.emit("vector", lambda E_, dname=dname, sl=sl, bcol=bcol, bb=bb, hx=hx: E_.tensor_scalar(
                                out=diagP[hx][bb][:], in0=b.cst[dname][:], scalar1=sl,
                                scalar2=bcol, op0=OP.mult, op1=OP.subtract))
                        em.emit("vector", lambda E_, sl=sl, hx=hx: E_.tensor_scalar_mul(
                            offc[hx][:], b.cst["c_doff"][:], sl))
                kt_lo = 0 if fwd else qt * 4
                kt_hi = qt * 4 + 4 if fwd else NTC
                nk = kt_hi - kt_lo
                nj = 2 * nk
                base_e = em.counts[sPe]
                pvb = [b.pb[3], b.pb[4]]
                sbanks = {}

                def emit_score(j):
                    kk, hx = j // 2, j % 2
                    kt = kt_lo + kk
                    h8 = heads[hx]
                    hp = (h8 % 2) * 64
                    isc = em.counts[sS]
                    bank = b.pb[isc % 3]
                    sbanks[j] = bank
                    if isc >= 3:
                        em.wait("tensor", sPe, isc - 2)
                    if j == 0 and ihead[0] >= 2:
                        em.wait("tensor", sEvh, ihead[0])
                    em.emit("tensor", lambda E_, kt=kt, bank=bank, hp=hp, hc2=hc2, qt=qt: E_.matmul(
                        bank[:, 0:512],
                        kTh[hp:hp + 64, hc2 * T + kt * 128: hc2 * T + kt * 128 + 128],
                        qTh[hp:hp + 64, hc2 * T + qt * 512: hc2 * T + qt * 512 + 512],
                        start=True, stop=True), inc=(sS, 1))

                for j0 in range(min(2, nj)):
                    emit_score(j0)
                for j in range(nj):
                    if j + 2 < nj:
                        emit_score(j + 2)
                    kk, hx = j // 2, j % 2
                    kt = kt_lo + kk
                    h8 = heads[hx]
                    bank = sbanks.pop(j)
                    diag = (qt * 4 <= kt < qt * 4 + 4)
                    m = em.counts[sPe]
                    if shallow:
                        dcol = offc2[hx][:, kt - 4 * qt + 32: kt - 4 * qt + 33]
                        if diag:
                            n = em.counts[sTx]
                            em.wait("vector", sS, base_e + j + 1)
                            if len(stt_exp) >= 3:
                                em.wait("vector", sPe, stt_exp[-3])
                            em.emit("vector", lambda E_, bank=bank, pat=diagP[0][kt - qt * 4], p=n % 3: E_.scalar_tensor_tensor(
                                out=scoreS[p][:], in0=bank[:, 0:512], scalar=0.125,
                                in1=pat[:], op0=OP.mult, op1=OP.add), inc=(sTx, 1))
                            stt_exp.append(m + 1)
                            em.wait("scalar", sTx, n + 1)
                            if m >= 3:
                                em.wait("scalar", sPv, m - 2)
                            em.emit("scalar", lambda E_, p=n % 3, dcol=dcol, q=m % 3: E_.activation(
                                out=pT[q][:], in_=scoreS[p][:], func=AF.Exp,
                                bias=dcol, scale=1.0), inc=(sPe, 1))
                        else:
                            em.wait("scalar", sS, base_e + j + 1)
                            em.wait("scalar", sPat, em.counts[sPat])
                            if m >= 3:
                                em.wait("scalar", sPv, m - 2)
                            em.emit("scalar", lambda E_, bank=bank, dcol=dcol, q=m % 3: E_.activation(
                                out=pT[q][:], in_=bank[:, 0:512], func=AF.Exp,
                                bias=dcol, scale=0.125), inc=(sPe, 1))
                    else:
                        n = em.counts[sTx]
                        em.wait("vector", sS, base_e + j + 1)
                        if len(stt_exp) >= 3:
                            em.wait("vector", sPe, stt_exp[-3])
                        pat = (diagP[hx][kt - qt * 4] if diag else srelP[hx])
                        em.emit("vector", lambda E_, bank=bank, pat=pat, p=n % 3: E_.scalar_tensor_tensor(
                            out=scoreS[p][:], in0=bank[:, 0:512], scalar=0.125,
                            in1=pat[:], op0=OP.mult, op1=OP.add), inc=(sTx, 1))
                        stt_exp.append(m + 1)
                        em.wait("scalar", sTx, n + 1)
                        if m >= 3:
                            em.wait("scalar", sPv, m - 2)
                        bias_ap = (0.0 if diag
                                   else offc[hx][:, kt - 4 * qt + 32: kt - 4 * qt + 33])
                        em.emit("scalar", lambda E_, p=n % 3, bias_ap=bias_ap, q=m % 3: E_.activation(
                            out=pT[q][:], in_=scoreS[p][:], func=AF.Exp,
                            bias=bias_ap, scale=1.0), inc=(sPe, 1))
                    em.wait("tensor", sPe, m + 1)
                    em.emit("tensor", lambda E_, kt=kt, h8=h8, q=m % 3, kk=kk, nk=nk, pv=pvb[hx]: E_.matmul(
                        pv[0:65, 0:512], vh[:, kt * VST + h8 * 65: kt * VST + h8 * 65 + 65],
                        pT[q][:], start=(kk == 0), stop=(kk == nk - 1)),
                        inc=(sPv, 1))
                cPv = em.counts[sPv]
                em.wait("scalar", sPv, cPv)
                em.wait("vector", sPv, cPv)
                if cF_attnh[0] > 0:
                    em.wait("vector", sF, cF_attnh[0])
                for hx in range(2):
                    h8 = heads[hx]
                    pv = pvb[hx]
                    if ihead[0] >= 1:
                        em.wait("scalar", sRbm, em.counts[sRbm])
                    em.emit("scalar", lambda E_, pv=pv: E_.copy(
                        out=lrow[:], in_=pv[64:65, 0:512]), inc=(sRcp, 1))
                    cRc = em.counts[sRcp]
                    em.wait("tensor", sRcp, cRc)
                    if ihead[0] >= 1:
                        em.wait("tensor", sEvh, ihead[0])
                    em.emit("tensor", lambda E_: E_.matmul(
                        b.pb[5][0:64, 0:512], b.cst["c_ones1"][0:1, 0:64], lrow[:],
                        start=True, stop=True), inc=(sRbm, 1))
                    cRb = em.counts[sRbm]
                    em.wait("vector", sRbm, cRb)
                    em.emit("vector", lambda E_: E_.reciprocal(rb[:], b.pb[5][0:64, 0:512]))
                    em.emit("vector", lambda E_, pv=pv, h8=h8: E_.tensor_tensor(
                        out=attnh[:, h8 * 512:(h8 + 1) * 512],
                        in0=pv[0:64, 0:512], in1=rb[:], op=OP.mult), inc=(sEvh, 1))
                    ihead[0] += 1
            em.wait("tensor", sEvh, em.counts[sEvh])
            for dc in range(KC):
                fb = b.pb[6 + dc % 2]
                par = dc % 2
                em.wait("gpsimd", sFe, em.counts[sFe] - 1)
                em.emit("gpsimd", lambda E_, dc=dc, par=par, half=half: E_.dma_start(
                    out=wfb[par][:], in_=wf_h[half, dc]), inc=(sWF, 16))
                cWF = em.counts[sWF]
                em.wait("tensor", sWF, cWF)
                em.wait("tensor", sFe, em.counts[sFe] - 1)
                for h8 in range(8):
                    em.emit("tensor", lambda E_, dc=dc, h8=h8, fb=fb, par=par: E_.matmul(
                        fb[:, 0:512], wfb[par][0:64, h8 * 128:(h8 + 1) * 128],
                        attnh[:, h8 * 512:(h8 + 1) * 512],
                        start=(h8 == 0), stop=(h8 == 7)),
                        inc=(sF, 1) if h8 == 7 else None)
                cF = em.counts[sF]
                em.wait("scalar", sF, cF)
                em.wait("scalar", sV3m, em.counts[sV3m])
                if half == 0:
                    em.emit("scalar", lambda E_, dc=dc, fb=fb: E_.add(
                        out=txf[:], in_=fb[:, 0:512], add=bfc[:, dc:dc + 1]),
                        inc=(sFe, 1))
                else:
                    em.emit("scalar", lambda E_, fb=fb: E_.copy(
                        out=txf[:], in_=fb[:, 0:512]), inc=(sFe, 1))
                cFe = em.counts[sFe]
                em.wait("vector", sFe, cFe)
                em.wait("vector", sSt3[dc % 2], em.counts[sSt3[dc % 2]])
                em.emit("vector", lambda E_, dc=dc: E_.tensor_copy(
                    out=st3[dc % 2][:], in_=txf[:]), inc=(sV3m, 1))
                cV3 = em.counts[sV3m]
                em.wait("gpsimd", sV3m, cV3)
                if first_store[0]:
                    em.wait("gpsimd", em.dch, cpy_d)
                    first_store[0] = False
                em.emit("gpsimd", lambda E_, dc=dc, qt=qt: E_.dma_start(
                    out=x3_hbm[qt, dc],
                    in_=st3[dc % 2][:], accum_op=OP.add), inc=(sSt3[dc % 2], 16))
            cF_attnh[0] = em.counts[sF]
    em.join([(sSt3[0], em.counts[sSt3[0]]), (sSt3[1], em.counts[sSt3[1]])])


def emit_phase_d(b, x3_hbm, x4_hbm, n_cores=8):
    em, nc, T, NTT, NTC = b.em, b.nc, b.T, b.NTT, b.NTC
    pfx = "dd"
    ccap = float(2 * 2 * (T * n_cores) // E)
    wr_h = b.din("L1_ffn_Wg", [D, E])
    w1_h = b.din("L1_ffn_W1", [E, H // 256, 128, KC * 256])
    b1_h = b.din("L1_ffn_b1", [E, H])
    w2_h = b.din("L1_ffn_W2", [E, H, D])
    b2_h = b.din("L1_ffn_b2", [E, D])

    rows = {}
    xn = b.sb("d_xn", [128, KC * T], BF16)
    Mc = b.sb("d_Mc", [8, T], F32)
    b1cols = b.sb("d_b1c", [128, E * HC], F32)
    b2cols = b.sb("d_b2c", [128, E * KC], F32)
    emit_ln_pass(b, pfx, x3_hbm, rows, want_rms=False, xn_sb=xn,
                 xn_rows=("alpha", "beta"))
    em.ser("sync", lambda E_: E_.dma_start(out=x4_hbm[:], in_=x3_hbm[:]), dma=True)

    b.push()
    tmp32 = b.sb("d_tmp32", [32, 128], F32)
    tmp8 = b.sb("d_tmp8", [8, 128], F32)
    for e in range(E):
        em.ser("sync", lambda E_, e=e: E_.dma_start(
            out=tmp32[:], in_=b1_h[e].rearrange("(c p) -> c p", p=128)), dma=True)
        em.ser("tensor", lambda E_: E_.transpose(
            b.pb[6][:, 0:32], tmp32[:], b.cst["c_ident"][0:32, 0:32]))
        em.ser("vector", lambda E_, e=e: E_.tensor_copy(
            out=b1cols[:, e * HC:(e + 1) * HC], in_=b.pb[6][:, 0:32]))
        em.ser("sync", lambda E_, e=e: E_.dma_start(
            out=tmp8[:], in_=b2_h[e].rearrange("(c p) -> c p", p=128)), dma=True)
        em.ser("tensor", lambda E_: E_.transpose(
            b.pb[6][:, 0:8], tmp8[:], b.cst["c_ident"][0:8, 0:8]))
        em.ser("vector", lambda E_, e=e: E_.tensor_copy(
            out=b2cols[:, e * KC:(e + 1) * KC], in_=b.pb[6][:, 0:8]))

    wr_sb = b.sb("d_wr", [128, 8 * 8], F32)
    wrb = b.sb("d_wrb", [128, 8 * 8], BF16)
    logT = b.sb("d_logT", [8, T], F32)
    for fc in range(KC):
        em.ser("sync", lambda E_, fc=fc: E_.dma_start(
            out=wr_sb[:, fc * 8:(fc + 1) * 8], in_=wr_h[fc * 128:(fc + 1) * 128, :]),
            dma=True)
    em.ser("scalar", lambda E_: E_.copy(out=wrb[:], in_=wr_sb[:]))
    for tt in range(NTT):
        for fc in range(KC):
            em.ser("tensor", lambda E_, fc=fc, tt=tt: E_.matmul(
                b.pb[6][0:8, 0:512], wrb[:, fc * 8:(fc + 1) * 8],
                xn[:, fc * T + tt * 512: fc * T + tt * 512 + 512],
                start=(fc == 0), stop=(fc == KC - 1)))
        em.ser("vector", lambda E_, tt=tt: E_.tensor_copy(
            out=logT[0:8, tt * 512:(tt + 1) * 512], in_=b.pb[6][0:8, 0:512]))

    m1T = b.sb("d_m1T", [8, T], BF16)
    m2T = b.sb("d_m2T", [8, T], BF16)
    gT = b.sb("d_gT", [8, T], F32)
    scr96 = b.sb("d_scr96", [128, 96], F32)
    lg = b.sb("d_lg", [128, 8], F32)
    cc = b.sb("d_cc", [128, 8], F32)
    m1 = b.sb("d_m1", [128, 6], F32)
    for tc in range(NTC):
        em.ser("tensor", lambda E_, tc=tc: E_.transpose(
            b.pb[6][:, 0:8], logT[0:8, tc * 128:(tc + 1) * 128],
            b.cst["c_ident"][0:8, 0:8]))
        em.ser("vector", lambda E_: E_.tensor_copy(out=lg[:], in_=b.pb[6][:, 0:8]))
        em.ser("vector", lambda E_: E_.tensor_reduce(
            out=m1[:, 0:1], in_=lg[:], axis=AX, op=OP.max))
        em.ser("vector", lambda E_: E_.tensor_scalar(
            out=cc[:], in0=lg[:], scalar1=m1[:, 0:1], scalar2=None, op0=OP.is_equal))
        em.ser("vector", lambda E_: E_.tensor_mul(cc[:], cc[:], b.cst["c_p2row"][:]))
        em.ser("vector", lambda E_: E_.tensor_reduce(
            out=m1[:, 1:2], in_=cc[:], axis=AX, op=OP.max))
        em.ser("vector", lambda E_: E_.tensor_scalar(
            out=scr96[:, 0:8], in0=cc[:], scalar1=m1[:, 1:2], scalar2=None,
            op0=OP.is_equal))
        em.ser("vector", lambda E_: E_.scalar_tensor_tensor(
            out=cc[:], in0=scr96[:, 0:8], scalar=-1e30, in1=lg[:],
            op0=OP.mult, op1=OP.add))
        em.ser("vector", lambda E_: E_.tensor_reduce(
            out=m1[:, 2:3], in_=cc[:], axis=AX, op=OP.max))
        em.ser("vector", lambda E_: E_.tensor_scalar(
            out=cc[:], in0=cc[:], scalar1=m1[:, 2:3], scalar2=None, op0=OP.is_equal))
        em.ser("vector", lambda E_: E_.tensor_mul(cc[:], cc[:], b.cst["c_p2row"][:]))
        em.ser("vector", lambda E_: E_.tensor_reduce(
            out=m1[:, 3:4], in_=cc[:], axis=AX, op=OP.max))
        em.ser("vector", lambda E_: E_.tensor_scalar(
            out=scr96[:, 32:40], in0=cc[:], scalar1=m1[:, 3:4], scalar2=None,
            op0=OP.is_equal))
        em.ser("vector", lambda E_: E_.tensor_scalar_mul(m1[:, 4:5], m1[:, 0:1], -1.0))
        em.ser("scalar", lambda E_: E_.activation(
            out=scr96[:, 64:72], in_=lg[:], func=AF.Exp, bias=m1[:, 4:5], scale=1.0))
        em.ser("vector", lambda E_: E_.tensor_reduce(
            out=m1[:, 5:6], in_=scr96[:, 64:72], axis=AX, op=OP.add))
        em.ser("vector", lambda E_: E_.reciprocal(m1[:, 5:6], m1[:, 5:6]))
        em.ser("vector", lambda E_: E_.tensor_scalar(
            out=scr96[:, 64:72], in0=scr96[:, 64:72], scalar1=m1[:, 5:6],
            scalar2=None, op0=OP.mult))
        em.ser("tensor", lambda E_, tc=tc: E_.transpose(
            b.pb[6][0:96, 0:128], scr96[:], b.cst["c_ident"][:]))
        em.ser("vector", lambda E_, tc=tc: E_.tensor_copy(
            out=m1T[0:8, tc * 128:(tc + 1) * 128], in_=b.pb[6][0:8, 0:128]))
        em.ser("vector", lambda E_, tc=tc: E_.tensor_copy(
            out=m2T[0:8, tc * 128:(tc + 1) * 128], in_=b.pb[6][32:40, 0:128]))
        em.ser("vector", lambda E_, tc=tc: E_.tensor_copy(
            out=gT[0:8, tc * 128:(tc + 1) * 128], in_=b.pb[6][64:72, 0:128]))

    pos1 = b.sb("d_pos1", [8, T], F32)
    pos2 = b.sb("d_pos2", [8, T], F32)
    for (msk, pos) in ((m1T, pos1), (m2T, pos2)):
        for tt in range(NTT):
            if tt == 0:
                em.ser("vector", lambda E_, msk=msk, pos=pos: E_.tensor_tensor_scan(
                    out=pos[0:8, 0:512], data0=b.cst["c_ones8"][:],
                    data1=msk[0:8, 0:512], initial=0.0, op0=OP.mult, op1=OP.add))
            else:
                em.ser("vector", lambda E_, msk=msk, pos=pos, tt=tt: E_.tensor_tensor_scan(
                    out=pos[0:8, tt * 512:(tt + 1) * 512], data0=b.cst["c_ones8"][:],
                    data1=msk[0:8, tt * 512:(tt + 1) * 512],
                    initial=pos[0:8, tt * 512 - 1:tt * 512], op0=OP.mult, op1=OP.add))

    ccin_sb = b.sb("d_ccin", [1, 16], F32)
    ccall = b.sb("d_ccall", [8, 16], F32)
    ccin_d = nc.dram_tensor("d_ccin_d", [1, 16], F32, kind="Internal")
    ccout_d = nc.dram_tensor("d_ccout_d", [n_cores, 16], F32, kind="Internal",
                             addr_space="Shared" if n_cores > 1 else "Local")
    em.ser("tensor", lambda E_: E_.transpose(
        b.pb[6][0:1, 0:8], pos1[0:8, T - 1:T], b.cst["c_ident"][0:8, 0:8]))
    em.ser("vector", lambda E_: E_.tensor_copy(
        out=ccin_sb[0:1, 0:8], in_=b.pb[6][0:1, 0:8]))
    em.ser("tensor", lambda E_: E_.transpose(
        b.pb[6][0:1, 0:8], pos2[0:8, T - 1:T], b.cst["c_ident"][0:8, 0:8]))
    em.ser("vector", lambda E_: E_.tensor_copy(
        out=ccin_sb[0:1, 8:16], in_=b.pb[6][0:1, 0:8]))
    em.ser("sync", lambda E_: E_.dma_start(out=ccin_d[:], in_=ccin_sb[:]), dma=True)
    sCC = em.sem("d_cc")
    em.wait("gpsimd", em.cch, em.counts[em.cch])
    em.wait("gpsimd", em.dch, em.counts[em.dch])
    em.emit("gpsimd", lambda E_: E_.collective_compute(
        "AllGather", OP.bypass, replica_groups=[list(range(n_cores))],
        ins=[ccin_d[:]], outs=[ccout_d[:]]), inc=(sCC, 1))
    em.join([(sCC, 1)])
    em.ser("vector", lambda E_: E_.memset(ccall[:], 0.0))
    em.ser("sync", lambda E_, n_cores=n_cores: E_.dma_start(
        out=ccall[0:n_cores, :], in_=ccout_d[:]), dma=True)
    rx = b.sb("d_rx", [1, 8 * 4], F32)
    thrc = b.sb("d_thrc", [8, 2], F32)
    em.ser("tensor", lambda E_: E_.matmul(
        b.pb[6][0:1, 0:8], b.cst["c_pref"][:], ccall[0:8, 0:8], start=True, stop=True))
    em.ser("vector", lambda E_: E_.tensor_copy(out=rx[0:1, 0:8], in_=b.pb[6][0:1, 0:8]))
    em.ser("tensor", lambda E_: E_.matmul(
        b.pb[6][0:1, 0:8], b.cst["c_ones8col"][:], ccall[0:8, 0:8],
        start=True, stop=True))
    em.ser("vector", lambda E_: E_.tensor_copy(out=rx[0:1, 8:16], in_=b.pb[6][0:1, 0:8]))
    em.ser("tensor", lambda E_: E_.matmul(
        b.pb[6][0:1, 0:8], b.cst["c_pref"][:], ccall[0:8, 8:16], start=True, stop=True))
    em.ser("vector", lambda E_: E_.tensor_copy(out=rx[0:1, 16:24], in_=b.pb[6][0:1, 0:8]))
    em.ser("vector", lambda E_, ccap=ccap: E_.tensor_scalar(
        out=rx[0:1, 24:32], in0=rx[0:1, 0:8], scalar1=-1.0, scalar2=ccap,
        op0=OP.mult, op1=OP.add))
    row2col(b, rx[0:1, 24:32], thrc[:, 0:1], 8)
    em.ser("vector", lambda E_: E_.tensor_add(rx[0:1, 16:24], rx[0:1, 16:24],
                                              rx[0:1, 8:16]))
    em.ser("vector", lambda E_, ccap=ccap: E_.tensor_scalar(
        out=rx[0:1, 24:32], in0=rx[0:1, 16:24], scalar1=-1.0, scalar2=ccap,
        op0=OP.mult, op1=OP.add))
    row2col(b, rx[0:1, 24:32], thrc[:, 1:2], 8)

    M1K = m1T
    M2K = m2T
    s8a = b.sb("d_s8a", [8, 512], F32)
    s8b = b.sb("d_s8b", [8, 512], F32)
    denrow = b.sb("d_denrow", [1, T], F32)
    for tt in range(NTT):
        for (msk, pos, thr_i, MK, second) in ((m1T, pos1, 0, M1K, False),
                                              (m2T, pos2, 1, M2K, True)):
            sl_ = (slice(0, 8), slice(tt * 512, (tt + 1) * 512))
            em.ser("vector", lambda E_, msk=msk, pos=pos, sl_=sl_: E_.tensor_sub(
                s8a[0:8, 0:512], pos[sl_[0], sl_[1]], msk[sl_[0], sl_[1]]))
            em.ser("vector", lambda E_, thr_i=thr_i: E_.tensor_scalar(
                out=s8a[0:8, 0:512], in0=s8a[0:8, 0:512],
                scalar1=thrc[0:8, thr_i:thr_i + 1], scalar2=None, op0=OP.is_lt))
            em.ser("vector", lambda E_, msk=msk, MK=MK, sl_=sl_: E_.tensor_tensor(
                out=MK[sl_[0], sl_[1]], in0=msk[sl_[0], sl_[1]],
                in1=s8a[0:8, 0:512], op=OP.mult))
            em.ser("vector", lambda E_, msk=msk, sl_=sl_: E_.tensor_tensor(
                out=s8b[0:8, 0:512], in0=msk[sl_[0], sl_[1]],
                in1=gT[sl_[0], sl_[1]], op=OP.mult))
            em.ser("tensor", lambda E_: E_.matmul(
                b.pb[6][0:1, 0:512], b.cst["c_ones8col"][:], s8b[0:8, 0:512],
                start=True, stop=True))
            if not second:
                em.ser("vector", lambda E_, tt=tt: E_.tensor_copy(
                    out=denrow[0:1, tt * 512:(tt + 1) * 512], in_=b.pb[6][0:1, 0:512]))
            else:
                em.ser("vector", lambda E_, tt=tt: E_.tensor_tensor(
                    out=denrow[0:1, tt * 512:(tt + 1) * 512],
                    in0=denrow[0:1, tt * 512:(tt + 1) * 512],
                    in1=b.pb[6][0:1, 0:512], op=OP.add))
    em.ser("vector", lambda E_: E_.tensor_scalar_add(denrow[0:1, 0:T],
                                                     denrow[0:1, 0:T], 1e-9))
    em.ser("vector", lambda E_: E_.reciprocal(denrow[0:1, 0:T], denrow[0:1, 0:T]))
    for tt in range(NTT):
        em.ser("tensor", lambda E_, tt=tt: E_.matmul(
            b.pb[6][0:8, 0:512], b.cst["c_ones1"][0:1, 0:8],
            denrow[0:1, tt * 512:(tt + 1) * 512], start=True, stop=True))
        em.ser("vector", lambda E_, tt=tt: E_.tensor_tensor(
            out=s8a[0:8, 0:512], in0=M1K[0:8, tt * 512:(tt + 1) * 512],
            in1=b.pb[6][0:8, 0:512], op=OP.mult))
        em.ser("vector", lambda E_, tt=tt: E_.tensor_tensor(
            out=s8b[0:8, 0:512], in0=M2K[0:8, tt * 512:(tt + 1) * 512],
            in1=b.pb[6][0:8, 0:512], op=OP.mult))
        em.ser("vector", lambda E_, tt=tt: E_.tensor_tensor(
            out=s8a[0:8, 0:512], in0=s8a[0:8, 0:512],
            in1=gT[0:8, tt * 512:(tt + 1) * 512], op=OP.mult))
        em.ser("vector", lambda E_, tt=tt: E_.tensor_tensor(
            out=s8b[0:8, 0:512], in0=s8b[0:8, 0:512],
            in1=gT[0:8, tt * 512:(tt + 1) * 512], op=OP.mult))
        em.ser("vector", lambda E_, tt=tt: E_.tensor_add(
            Mc[0:8, tt * 512:(tt + 1) * 512], s8a[0:8, 0:512], s8b[0:8, 0:512]))
    em.join([(em.cch, em.counts[em.cch]), (em.dch, em.counts[em.dch])])
    b.pop()

    b.push()
    emit_moe_loop(b, "d", xn, Mc, x4_hbm, w1_h, None, w2_h, b1cols, b2cols,
                  silu_glu=False)
    b.pop()


def _phase_scope(b):
    return b.stack.enter_context(contextlib.ExitStack())


def build_program(T, phases, n_cores=NCORES, verbose=False):
    nc = bass.Bass("TRN2", target_bir_lowering=False)
    ext = {}
    with contextlib.ExitStack() as stack:
        em = Em(nc, stack)
        b = B(nc, stack, em, T)
        NTT = T // 512
        xshape = [NTT, KC, 128, 512]
        x_in = nc.dram_tensor("x_in", xshape, F32, kind="ExternalInput")
        b.ext["x_in"] = tuple(xshape)
        x_out = nc.dram_tensor("x_out", xshape, F32, kind="ExternalOutput")
        n = len(phases)
        bufs = [x_in]
        for i in range(n - 1):
            bufs.append(nc.dram_tensor(f"xmid{i}", xshape, F32, kind="Internal"))
        bufs.append(x_out)
        emitters = {"A": emit_phase_a, "B": emit_phase_b, "C": emit_phase_c,
                    "D": emit_phase_d}
        with nc.Block() as blk:
            setup_common(b)
            outer_stack = b.stack
            for i, p in enumerate(phases):
                sem_snapshot = dict(em.counts)
                phase_stack = contextlib.ExitStack()
                b.stack = phase_stack
                if p == "D":
                    emit_phase_d(b, bufs[i], bufs[i + 1], n_cores=n_cores)
                else:
                    emitters[p](b, bufs[i], bufs[i + 1])
                deps = [(s, c) for s, c in em.counts.items()
                        if c > sem_snapshot.get(s, 0)]
                em.join(deps)
                em.release_phase_sems()
                em.join([(em.cch, em.counts[em.cch])])
                b.stack = outer_stack
                phase_stack.close()
                if verbose:
                    print(f"phase {p}: sbuf_remaining={nc.sbuf_bytes_remaining} "
                          f"n_inst={nc.n_instructions()}")
            em.finalize(blk)
        ext = dict(b.ext)
    return nc, ext


def tile_updown(W):
    W = np.ascontiguousarray(W)
    Eo, Do, Ho = W.shape
    return np.ascontiguousarray(
        W.reshape(Eo, KC, 128, Ho // 256, 256).transpose(0, 3, 2, 1, 4)
        .reshape(Eo, Ho // 256, 128, KC * 256))


def tile_wf(W):
    W = np.ascontiguousarray(W)
    return np.ascontiguousarray(
        W.reshape(2, 8, 64, KC, 128).transpose(0, 3, 2, 1, 4)
        .reshape(2, KC, 64, 8 * 128))


def tile_x(xTD):
    T_, D_ = xTD.shape
    xT = np.ascontiguousarray(xTD.T)
    return np.ascontiguousarray(
        xT.reshape(KC, 128, T_ // 512, 512).transpose(2, 0, 1, 3))


def untile_x(xt):
    NTT_ = xt.shape[0]
    return np.ascontiguousarray(
        xt.transpose(1, 2, 0, 3).reshape(D, NTT_ * 512).T)


TILE_TRANSFORMS = {
    "L0_ffn_Wg": tile_updown,
    "L0_ffn_Wu": tile_updown,
    "L1_ffn_W1": tile_updown,
    "L1_attn_Wf": tile_wf,
}


def _shard_inputs(inputs, T, phases, n_cores=NCORES):
    w = dict(inputs)
    maps = []
    x0 = np.asarray(w["L0_attn_inputs"])
    need = set()
    for p in phases:
        need |= {
            "A": {"L0_attn_norm_weight", "L0_attn_router_w", "L0_attn_router_b",
                  "L0_attn_w_hg", "L0_attn_w_out"},
            "B": {"L0_ffn_Wr", "L0_ffn_Wg", "L0_ffn_Wu", "L0_ffn_Wd"},
            "C": {"L1_attn_Wq", "L1_attn_bq", "L1_attn_Wk", "L1_attn_bk",
                  "L1_attn_Wv", "L1_attn_bv", "L1_attn_Wf", "L1_attn_bf"},
            "D": {"L1_ffn_Wg", "L1_ffn_W1", "L1_ffn_b1", "L1_ffn_W2", "L1_ffn_b2"},
        }[p]
    reshape = {"L0_attn_router_b": (1, E)}
    for c in range(n_cores):
        m = {}
        for k, v in host_constants(c).items():
            m[k] = v
        for name in need:
            arr = np.ascontiguousarray(np.asarray(w[name]), dtype=np.float32)
            if name in reshape:
                arr = arr.reshape(reshape[name])
            if name in TILE_TRANSFORMS:
                arr = TILE_TRANSFORMS[name](arr)
            m[name] = arr
        m["x_in"] = tile_x(np.asarray(x0[c]))
        maps.append(m)
    return maps


_NC_CACHE = {}


def kernel(**inputs):
    T = int(np.asarray(inputs["L0_attn_inputs"]).shape[1])
    phases = ("A", "B", "C", "D")
    key = (T, phases)
    if key not in _NC_CACHE:
        _NC_CACHE[key] = build_program(T, phases)
    nc, ext = _NC_CACHE[key]
    maps = _shard_inputs(inputs, T, phases)
    res = run_bass_kernel_spmd(nc, maps, core_ids=list(range(NCORES)))
    out = np.stack([untile_x(r["x_out"].reshape(T // 512, KC, 128, 512))
                    for r in res.results])
    return out.astype(np.float32)


# version 13
# speedup vs baseline: 1.0086x; 1.0086x over previous
import contextlib
import numpy as np
import ml_dtypes
import concourse.bass as bass
import concourse.mybir as mybir
from concourse.bass_utils import run_bass_kernel_spmd

F32 = mybir.dt.float32
BF16 = mybir.dt.bfloat16
AX = mybir.AxisListType.X
OP = mybir.AluOpType
AF = mybir.ActivationFunctionType

D = 1024
KC = D // 128
E = 8
H = 4096
HC = H // 128
NHEADS = 16
HDIM = 64
DEPTH_SCALE = float(1.0 / np.sqrt(12.0))
LN_EPS = 1e-5
RMS_EPS = 1e-6
NEG = -1e9
CCAP = 16384
NCORES = 8


def alibi_slopes():
    half = NHEADS // 2
    base = 24.0 ** (1.0 / half)
    return (1.0 / base ** np.arange(1, half + 1, dtype=np.float64)).astype(np.float64)


class Em:
    ENGINES = ("sync", "tensor", "vector", "scalar", "gpsimd")

    CHAIN = ("vector", "scalar")

    def __init__(self, nc, stack):
        self.nc = nc
        self.streams = {e: [] for e in self.ENGINES}
        self.stack = stack
        self.sems = {}
        self.counts = {}
        self.cch = self.sem("cch")
        self.dch = self.sem("dch")
        self.spd = self.sem("spd")
        self.selfsem = {e: self.sem(f"self_{e}") for e in self.CHAIN}
        self.last = {}

    def sem(self, name):
        assert name not in self.sems, name
        h = self.nc.alloc_semaphore(name=name)
        self.sems[name] = h
        if not hasattr(self, "phys_count"):
            self.phys_count = {}
        self.counts[name] = self.phys_count.get(h.num, 0)
        return name

    def release_phase_sems(self, keep=()):
        keepset = (set(keep) | {self.cch, self.dch, self.spd}
                   | set(self.selfsem.values()))
        handles = []
        for name in list(self.sems.keys()):
            if name in keepset:
                continue
            h = self.sems[name]
            handles.append(h)
            if not hasattr(self, "phys_count"):
                self.phys_count = {}
            self.phys_count[h.num] = 0
            del self.sems[name]
            del self.counts[name]
        fence = self.fence_t
        spd_h = self.sems[self.spd]
        self.streams["tensor"].append(lambda E_, s=spd_h: E_.nop().then_inc(s, 1))
        self.streams["vector"].append(lambda E_, s=spd_h: E_.engine_nop().then_inc(s, 1))
        self.streams["scalar"].append(
            lambda E_, s=spd_h, f=fence: E_.copy(out=f[:], in_=f[:]).then_inc(s, 1))
        self.streams["sync"].append(lambda E_, s=spd_h: E_.nop().then_inc(s, 1))
        self.counts[self.spd] += 4
        self.last["vector"] = (self.spd, self.counts[self.spd])
        self.last["scalar"] = (self.spd, self.counts[self.spd])
        self.wait("gpsimd", self.spd, self.counts[self.spd])
        self.wait("gpsimd", self.cch, self.counts[self.cch])
        self.wait("gpsimd", self.dch, self.counts[self.dch])
        for h in handles:
            self.streams["gpsimd"].append(
                lambda E_, h=h: E_.dma_reset(semaphore_range=range(h.num, h.num + 1)))
            self.streams["gpsimd"].append(
                lambda E_, h=h: E_.sem_clear(h))
            self.nc.release_semaphore(h)

        def bump(E_, s=self.sems[self.cch]):
            E_.memset(fence[:], 0.0).then_inc(s, 1)
        self.streams["gpsimd"].append(bump)
        self.counts[self.cch] += 1
        for eng in list(self.last.keys()):
            if self.last[eng] is not None and self.last[eng][0] not in self.sems:
                self.last[eng] = None

    def wait(self, eng, sem, val):
        h = self.sems[sem]
        if val > 0:
            self.streams[eng].append(lambda E_, h=h, v=val: E_.wait_ge(h, v))

    def emit(self, eng, fn, inc=None):
        chained = eng in self.CHAIN
        if chained:
            prev = self.last.get(eng)
            if prev is not None:
                self.wait(eng, prev[0], prev[1])
            if inc is None:
                inc = (self.selfsem[eng], 1)
        if inc is None:
            self.streams[eng].append(fn)
        else:
            s, n = inc
            h = self.sems[s]
            self.streams[eng].append(lambda E_, fn=fn, h=h, n=n: fn(E_).then_inc(h, n))
            self.counts[s] += n
            if chained:
                self.last[eng] = (s, self.counts[s])

    def ser(self, eng, fn, dma=False):
        self.wait(eng, self.cch, self.counts[self.cch])
        self.wait(eng, self.dch, self.counts[self.dch])
        self.emit(eng, fn, inc=(self.dch, 16) if dma else (self.cch, 1))

    def barrier(self):
        deps = [(self.cch, self.counts[self.cch]), (self.dch, self.counts[self.dch])]
        for s in self.selfsem.values():
            deps.append((s, self.counts[s]))
        self.join(deps)

    def join(self, deps):
        for eng in self.ENGINES:
            for s, v in deps:
                self.wait(eng, s, v)

    def finalize(self, blk):
        for name in self.ENGINES:
            ops = self.streams[name]

            def body(E_, ops=ops):
                for op in ops:
                    op(E_)
            getattr(blk, name)(body)


def host_constants(core_id):
    c = {}
    c["c_ident"] = np.eye(128, dtype=np.float32)
    c["c_ones_col"] = np.ones((128, 1), np.float32)
    c["c_ones1"] = np.ones((1, 128), np.float32)
    c["c_ones11"] = np.ones((1, 1), np.float32)
    oneh = np.zeros((8, 8 * 128), np.float32)
    for e in range(8):
        oneh[e, e * 128:(e + 1) * 128] = 1.0
    c["c_oneh"] = oneh
    c["c_p2row"] = np.broadcast_to((2.0 ** -np.arange(8, dtype=np.float32))[None, :],
                                   (128, 8)).copy()
    c["c_ones8"] = np.ones((8, 512), np.float32)
    i = np.arange(128)[:, None].astype(np.float32)
    jj = np.arange(512)[None, :].astype(np.float32)
    c["c_rel_kq512"] = (i - jj).astype(np.float32)
    for bb in range(4):
        rel = (i - jj + 128.0 * bb).astype(np.float32)
        vf = (i + 128.0 * bb) <= jj
        c[f"c_diag_f{bb}"] = np.where(vf, rel, -1e10).astype(np.float32)
        vb = (i + 128.0 * bb) >= jj
        c[f"c_diag_b{bb}"] = np.where(vb, rel, 1e10).astype(np.float32)
    doff = np.broadcast_to(128.0 * (np.arange(64, dtype=np.float32) - 32.0),
                           (128, 64)).copy()
    c["c_doff"] = doff
    c["c_ones64"] = np.ones((128, 64), np.float32)
    ii = np.arange(128, dtype=np.float32)[:, None]
    dd = np.arange(64, dtype=np.float32)[None, :]
    c["c_doff2"] = (ii + 128.0 * (dd - 32.0) - 256.0).astype(np.float32)
    bones2 = np.zeros((128, 2), np.float32)
    bones2[0:64, 0] = 1.0
    bones2[64:128, 1] = 1.0
    c["c_bones2"] = bones2
    pref = np.zeros((8, 1), np.float32)
    pref[:core_id, 0] = 1.0
    c["c_pref"] = pref
    c["c_ones8col"] = np.ones((8, 1), np.float32)
    return c


CONST_SPECS = {k: v.shape for k, v in host_constants(0).items()}


def xap(h, fc, tt, nf=1):
    return h[tt, fc]


class B:
    def __init__(self, nc, stack, em, T):
        self.nc = nc
        self.stack = stack
        self.em = em
        self.T = T
        self.NTT = T // 512
        self.NTC = T // 128
        self.cst = {}
        self.pb = []
        self.ext = {}

    def sb(self, name, shape, dtype):
        return self.stack.enter_context(self.nc.sbuf_tensor(name, shape, dtype))

    def push(self):
        self._saved = getattr(self, "_saved", [])
        self._saved.append(self.stack)
        self.stack = contextlib.ExitStack()
        return self.stack

    def pop(self):
        self.stack.close()
        self.stack = self._saved.pop()

    def din(self, name, shape):
        h = self.nc.dram_tensor(name, list(shape), F32, kind="ExternalInput")
        self.ext[name] = tuple(shape)
        return h


def setup_common(b):
    for name, shape in CONST_SPECS.items():
        if name.startswith("c_diag_") or name in ("c_rel_kq512", "c_ones64"):
            continue
        h = b.din(name, shape)
        t = b.sb(name + "_sb", list(shape), F32)
        b.cst[name] = t
        b.em.ser("sync", lambda E_, t=t, h=h: E_.dma_start(out=t[:], in_=h[:]), dma=True)
    b.pb = [b.stack.enter_context(b.nc.psum_tensor(f"pb{i}", [128, 512], F32))
            for i in range(8)]
    b.em.fence_t = b.sb("fence_t", [1, 1], F32)
    b.em.ser("vector", lambda E_: E_.memset(b.em.fence_t[:], 0.0))
    oc_b = b.sb("c_ones_colb", [128, 1], BF16)
    b.em.ser("scalar", lambda E_, oc_b=oc_b: E_.copy(
        out=oc_b[:], in_=b.cst["c_ones_col"][:]))
    b.cst["c_ones_colb"] = oc_b
    o1_b = b.sb("c_ones1b", [1, 128], BF16)
    b.em.ser("scalar", lambda E_, o1_b=o1_b: E_.copy(
        out=o1_b[:], in_=b.cst["c_ones1"][:]))
    b.cst["c_ones1b"] = o1_b


def vec2cols(b, vec_h, n, dst_sb, tmp_name):
    em = b.em
    nch = n // 128
    tmp = b.sb(tmp_name, [nch, 128], F32)
    em.ser("sync", lambda E_: E_.dma_start(
        out=tmp[:], in_=vec_h.ap().rearrange("(c p) -> c p", p=128)), dma=True)
    em.ser("tensor", lambda E_: E_.transpose(
        b.pb[6][:, 0:nch], tmp[:], b.cst["c_ident"][0:nch, 0:nch]))
    em.ser("vector", lambda E_: E_.tensor_copy(out=dst_sb[:, 0:nch], in_=b.pb[6][:, 0:nch]))


def row2col(b, row_ap, col_ap, n):
    em = b.em
    em.ser("tensor", lambda E_: E_.matmul(
        b.pb[6][0:n, 0:1], row_ap, b.cst["c_ones11"][:], start=True, stop=True))
    em.ser("vector", lambda E_: E_.tensor_copy(out=col_ap, in_=b.pb[6][0:n, 0:1]))


def col2row(b, col_ap, row_ap, n):
    em = b.em
    em.ser("tensor", lambda E_: E_.transpose(
        b.pb[6][0:1, 0:n], col_ap, b.cst["c_ident"][0:n, 0:n]))
    em.ser("vector", lambda E_: E_.tensor_copy(out=row_ap, in_=b.pb[6][0:1, 0:n]))


def alloc_rows(b, pfx, names, persist=(), persist_dtype=F32):
    return {n: b.sb(f"{pfx}row_{n}", [1, b.T] if n in persist else [1, 512],
                    persist_dtype if n in persist else F32)
            for n in names}


def emit_ln_pass(b, pfx, x_hbm, rows, want_rms=False, xn_sb=None, xn_rows=None):
    em = b.em
    T, NTT = b.T, b.NTT
    b.push()
    need = ["alpha", "beta", "a1", "t1", "t2", "t3", "t4"]
    if want_rms:
        need += ["gamma", "delta"]
    for nm in need:
        if nm not in rows:
            rows[nm] = b.sb(f"{pfx}row_{nm}", [1, 512], F32)
    xr = [b.sb(f"{pfx}xr{i}", [128, 512], F32) for i in range(KC)]
    sq = [b.sb(f"{pfx}sq{i}", [128, 512], F32) for i in range(2)]
    xs = [b.sb(f"{pfx}xs{i}", [128, 512], F32) for i in range(2)]
    sL = [em.sem(f"{pfx}L{i}") for i in range(KC)]
    sSq = em.sem(pfx + "Sq")
    sMM = em.sem(pfx + "MM")
    sRw = em.sem(pfx + "Rw")
    sSx = em.sem(pfx + "Sx")
    sBC = em.sem(pfx + "BC")
    sXN = em.sem(pfx + "XN")
    rpt = 2 if want_rms else 1

    def r(name, tt):
        buf = rows[name]
        if buf.shape[1] == T:
            return buf[0:1, tt * 512:(tt + 1) * 512]
        return buf[0:1, 0:512]

    for tt in range(NTT):
        for fc in range(KC):
            i = tt * KC + fc
            if tt >= 1:
                if xn_sb is not None:
                    em.wait("sync", sXN, (tt - 1) * KC + fc + 1)
                else:
                    em.wait("sync", sMM, (tt - 1) * KC + fc + 1)
            em.emit("sync", lambda E_, fc=fc, tt=tt: E_.dma_start(
                out=xr[fc][:], in_=x_hbm[tt, fc]),
                inc=(sL[fc], 16))
            em.wait("scalar", sL[fc], (tt + 1) * 16)
            if i >= 2:
                em.wait("scalar", sMM, i - 1)
            em.emit("scalar", lambda E_, fc=fc, i=i: E_.square(
                out=sq[i % 2][:], in_=xr[fc][:]), inc=(sSq, 1))
            em.wait("tensor", sSq, i + 1)
            em.wait("tensor", sL[fc], (tt + 1) * 16)
            if tt >= 1 and fc == 0:
                em.wait("tensor", sRw, (tt - 1) * rpt + 1)
            em.emit("tensor", lambda E_, fc=fc: E_.matmul(
                b.pb[0][0:1, 0:512], b.cst["c_ones_col"][:], xr[fc][:],
                start=(fc == 0), stop=(fc == KC - 1)))
            em.emit("tensor", lambda E_, fc=fc, i=i: E_.matmul(
                b.pb[1][0:1, 0:512], b.cst["c_ones_col"][:], sq[i % 2][:],
                start=(fc == 0), stop=(fc == KC - 1)), inc=(sMM, 1))
        em.wait("vector", sMM, (tt + 1) * KC)
        t1 = rows["t1"][0:1, 0:512]
        t2 = rows["t2"][0:1, 0:512]
        t3 = rows["t3"][0:1, 0:512]
        t4 = rows["t4"][0:1, 0:512]
        ral, rbe, ra1 = r("alpha", tt), r("beta", tt), r("a1", tt)

        def v(fn):
            em.emit("vector", fn)

        v(lambda E_: E_.tensor_scalar_mul(t1, b.pb[0][0:1, 0:512], 1.0 / D))
        v(lambda E_: E_.tensor_scalar_mul(t2, b.pb[1][0:1, 0:512], 1.0 / D))
        v(lambda E_: E_.tensor_mul(t3, t1, t1))
        v(lambda E_: E_.tensor_sub(t3, t2, t3))
        em.emit("vector", lambda E_: E_.tensor_scalar_add(t3, t3, LN_EPS),
                inc=(sRw, 1))
        em.wait("scalar", sRw, tt * rpt + 1)
        em.emit("scalar", lambda E_: E_.sqrt(out=t4, in_=t3), inc=(sSx, 1))
        em.wait("vector", sSx, tt * rpt + 1)
        v(lambda E_: E_.reciprocal(t2, t4))
        v(lambda E_, ral=ral: E_.tensor_copy(out=ral, in_=t2))
        v(lambda E_: E_.tensor_mul(t3, t1, t2))
        v(lambda E_: E_.tensor_scalar_mul(t3, t3, -1.0))
        v(lambda E_, rbe=rbe: E_.tensor_copy(out=rbe, in_=t3))
        v(lambda E_, ra1=ra1: E_.tensor_scalar_add(ra1, t2, 1.0))
        if want_rms:
            rga, rde = r("gamma", tt), r("delta", tt)
            v(lambda E_: E_.tensor_mul(t1, t4, t4))
            v(lambda E_: E_.tensor_scalar_add(t1, t1, -LN_EPS))
            v(lambda E_: E_.tensor_mul(t1, t1, t2))
            v(lambda E_: E_.tensor_mul(t1, t1, t2))
            em.emit("vector", lambda E_: E_.tensor_scalar_add(t1, t1, RMS_EPS),
                    inc=(sRw, 1))
            em.wait("scalar", sRw, tt * rpt + 2)
            em.emit("scalar", lambda E_: E_.sqrt(out=t4, in_=t1), inc=(sSx, 1))
            em.wait("vector", sSx, tt * rpt + 2)
            v(lambda E_: E_.reciprocal(t1, t4))
            v(lambda E_, rga=rga: E_.tensor_mul(rga, t2, t1))
            v(lambda E_, rde=rde: E_.tensor_mul(rde, t3, t1))
        if xn_sb is not None:
            gname, dname = xn_rows
            rg, rd = r(gname, tt), r(dname, tt)
            em.emit("vector", lambda E_: E_.engine_nop(), inc=(sBC, 1))
            em.wait("tensor", sBC, 2 * tt + 1)
            if tt >= 1:
                em.wait("tensor", sXN, tt * KC)
            em.emit("tensor", lambda E_, rg=rg: E_.matmul(
                b.pb[2][:, 0:512], b.cst["c_ones1"][:], rg, start=True, stop=True))
            em.emit("tensor", lambda E_, rd=rd: E_.matmul(
                b.pb[3][:, 0:512], b.cst["c_ones1"][:], rd, start=True, stop=True),
                inc=(sBC, 1))
            for fc in range(KC):
                i = tt * KC + fc
                em.wait("vector", sBC, 2 * tt + 2)
                if i >= 2:
                    em.wait("vector", sXN, i - 1)
                em.emit("vector", lambda E_, fc=fc, i=i: E_.tensor_tensor(
                    out=xs[i % 2][:], in0=xr[fc][:], in1=b.pb[2][:, 0:512], op=OP.mult))
                em.emit("vector", lambda E_, fc=fc, tt=tt, i=i: E_.tensor_tensor(
                    out=xn_sb[:, fc * T + tt * 512: fc * T + tt * 512 + 512],
                    in0=xs[i % 2][:], in1=b.pb[3][:, 0:512], op=OP.add),
                    inc=(sXN, 1))
    deps = [(sRw, em.counts[sRw]), (sMM, em.counts[sMM]), (sSx, em.counts[sSx])]
    if xn_sb is not None:
        deps.append((sXN, em.counts[sXN]))
    for s in sL:
        deps.append((s, em.counts[s]))
    for s in em.selfsem.values():
        deps.append((s, em.counts[s]))
    em.join(deps)
    b.pop()


def emit_phase_a(b, x0_hbm, x1_hbm):
    em, nc, T, NTT = b.em, b.nc, b.T, b.NTT
    pfx = "a"
    whg_h = b.din("L0_attn_w_hg", [E, 2 * D, D])
    wout_h = b.din("L0_attn_w_out", [E, D, D])
    nw_h = b.din("L0_attn_norm_weight", [D])
    rw_h = b.din("L0_attn_router_w", [E, D])
    rb_h = b.din("L0_attn_router_b", [1, E])

    rows = alloc_rows(b, pfx, ["alpha", "beta"], persist=("beta", "alpha"))
    xrms = b.sb("a_xrms", [128, KC * T], BF16)
    whgT = b.sb("a_whgT", [128, KC * 2048], BF16)
    woutT = b.sb("a_woutT", [128, KC * 1024], BF16)
    pcols = b.sb("a_pcols", [128, 8], F32)
    nwcols = b.sb("a_nwcols", [128, 8], F32)
    meanv = b.sb("a_meanv", [128, 8], F32)

    emit_ln_pass(b, pfx, x0_hbm, rows, want_rms=True, xn_sb=xrms,
                 xn_rows=("gamma", "delta"))

    b.push()
    scI = b.sb("a_scI", [128, 8 * 128], BF16)
    r8a = b.sb("a_r8a", [1, 8], F32)
    r8b = b.sb("a_r8b", [1, 8], F32)
    rw_sb = b.sb("a_rw", [8, D], F32)
    rwT = b.sb("a_rwT", [128, 64], F32)
    rbrow = b.sb("a_rb", [1, 8], F32)
    wring = [b.sb(f"a_wr{i}", [128, D], BF16) for i in range(8)]

    vec2cols(b, nw_h, D, nwcols, "a_nwtmp")
    em.ser("sync", lambda E_: E_.dma_start(out=rw_sb[:], in_=rw_h[:]), dma=True)
    em.ser("sync", lambda E_: E_.dma_start(out=rbrow[:], in_=rb_h[:]), dma=True)
    for fc in range(KC):
        em.ser("tensor", lambda E_, fc=fc: E_.transpose(
            b.pb[6][:, 0:8], rw_sb[:, fc * 128:(fc + 1) * 128],
            b.cst["c_ident"][0:8, 0:8]))
        em.ser("vector", lambda E_, fc=fc: E_.tensor_copy(
            out=rwT[:, fc * 8:(fc + 1) * 8], in_=b.pb[6][:, 0:8]))
    for fc in range(KC):
        em.ser("vector", lambda E_, fc=fc: E_.tensor_reduce(
            out=meanv[:, fc:fc + 1], in_=xrms[:, fc * T:(fc + 1) * T],
            axis=AX, op=OP.add))
        em.ser("vector", lambda E_, fc=fc: E_.tensor_scalar(
            out=meanv[:, fc:fc + 1], in0=meanv[:, fc:fc + 1],
            scalar1=nwcols[:, fc:fc + 1], scalar2=1.0 / T, op0=OP.mult, op1=OP.mult))
    for fc in range(KC):
        em.ser("tensor", lambda E_, fc=fc: E_.matmul(
            b.pb[6][0:1, 0:8], meanv[:, fc:fc + 1], rwT[:, fc * 8:(fc + 1) * 8],
            start=(fc == 0), stop=(fc == KC - 1)))
    em.ser("vector", lambda E_: E_.tensor_add(r8a[:], b.pb[6][0:1, 0:8], rbrow[:]))
    em.ser("vector", lambda E_: E_.tensor_reduce(
        out=r8b[0:1, 0:1], in_=r8a[:], axis=AX, op=OP.max))
    em.ser("vector", lambda E_: E_.tensor_scalar_mul(r8b[0:1, 1:2], r8b[0:1, 0:1], -1.0))
    em.ser("scalar", lambda E_: E_.activation(
        out=r8a[:], in_=r8a[:], func=AF.Exp, bias=r8b[0:1, 1:2], scale=1.0))
    em.ser("vector", lambda E_: E_.tensor_reduce(
        out=r8b[0:1, 2:3], in_=r8a[:], axis=AX, op=OP.add))
    em.ser("vector", lambda E_: E_.reciprocal(r8b[0:1, 3:4], r8b[0:1, 2:3]))
    em.ser("vector", lambda E_: E_.tensor_scalar(
        out=r8a[:], in0=r8a[:], scalar1=r8b[0:1, 3:4], scalar2=None, op0=OP.mult))
    em.ser("tensor", lambda E_: E_.matmul(
        b.pb[6][:, 0:8], b.cst["c_ones1"][:], r8a[:], start=True, stop=True))
    em.ser("vector", lambda E_: E_.tensor_copy(out=pcols[:], in_=b.pb[6][:, 0:8]))
    for e in range(E):
        em.ser("scalar", lambda E_, e=e: E_.activation(
            out=scI[:, e * 128:(e + 1) * 128], in_=b.cst["c_ident"][:],
            func=AF.Copy, bias=0.0, scale=pcols[:, e:e + 1]))

    sWL = [em.sem(f"a_wl{i}") for i in range(8)]
    sMix = em.sem("a_mix")
    sMixEv = em.sem("a_mixev")

    state = {"g": 0, "oc_total": 0}

    def mix(which, w_h, dst):
        outer = 16 if which == 0 else 8
        for oc in range(outer):
            oci = state["oc_total"]
            for e in range(E):
                if oci >= 1:
                    em.wait("gpsimd", sMix, oci * KC)
                em.emit("gpsimd", lambda E_, oc=oc, e=e, w_h=w_h: E_.dma_start(
                    out=wring[e][:], in_=w_h[e, oc * 128:(oc + 1) * 128, :]),
                    inc=(sWL[e], 16))
            for fc in range(KC):
                g = state["g"]
                bank = b.pb[4 + fc % 2]
                if g >= 2:
                    em.wait("tensor", sMixEv, g - 1)
                for e in range(E):
                    if fc == 0:
                        em.wait("tensor", sWL[e], (oci + 1) * 16)
                    em.emit("tensor", lambda E_, e=e, fc=fc, bank=bank: E_.matmul(
                        bank[:, 0:128],
                        wring[e][:, fc * 128:(fc + 1) * 128],
                        scI[:, e * 128:(e + 1) * 128],
                        start=(e == 0), stop=(e == E - 1)),
                        inc=(sMix, 1) if e == E - 1 else None)
                em.wait("vector", sMix, g + 1)
                if which == 0:
                    dstap = dst[:, fc * 2048 + oc * 128: fc * 2048 + oc * 128 + 128]
                    em.emit("vector", lambda E_, dstap=dstap, bank=bank, fc=fc: E_.tensor_scalar(
                        out=dstap, in0=bank[:, 0:128], scalar1=nwcols[:, fc:fc + 1],
                        scalar2=None, op0=OP.mult), inc=(sMixEv, 1))
                else:
                    dstap = dst[:, fc * 1024 + oc * 128: fc * 1024 + oc * 128 + 128]
                    em.emit("vector", lambda E_, dstap=dstap, bank=bank: E_.tensor_copy(
                        out=dstap, in_=bank[:, 0:128]), inc=(sMixEv, 1))
                state["g"] += 1
            state["oc_total"] += 1

    em.join([(em.cch, em.counts[em.cch]), (em.dch, em.counts[em.dch])])
    mix(0, whg_h, whgT)
    mix(1, wout_h, woutT)
    em.join([(sMixEv, em.counts[sMixEv])] +
            [(s, em.counts[s]) for s in sWL])
    em.barrier()
    b.pop()

    b.push()
    hidS = [b.sb(f"a_hidS{i}", [128, 512], F32) for i in range(2)]
    gatS = [b.sb(f"a_gatS{i}", [128, 512], F32) for i in range(2)]
    sgS = [b.sb(f"a_sgS{i}", [128, 512], F32) for i in range(2)]
    mskS = b.sb("a_mskS", [128, 512], F32)
    tS = b.sb("a_tS", [128, 512], F32)
    aT = [b.sb(f"a_aT{i}", [128, 512], F32) for i in range(2)]
    wT = b.sb("a_wT", [128, 512], F32)
    carry = b.sb("a_carry", [128, 8], F32)
    hbf = b.sb("a_hbf", [128, 8 * 512], BF16)
    a1bS = b.sb("a_a1b", [128, 512], F32)
    bbS = b.sb("a_bb", [128, 512], F32)
    x0r = [b.sb(f"a_x0r{i}", [128, 512], F32) for i in range(2)]
    x1o = [b.sb(f"a_x1o{i}", [128, 512], F32) for i in range(2)]

    sHid = em.sem("a_hid")
    sGat = em.sem("a_gat")
    sScl = em.sem("a_scl")
    sVec = em.sem("a_vec")
    sBc2 = em.sem("a_bc2")
    sBcC = em.sem("a_bcc")
    sOutM = em.sem("a_outm")
    sEv = em.sem("a_ev")
    sX0 = [em.sem(f"a_x0{i}") for i in range(2)]
    sSt = [em.sem(f"a_st{i}") for i in range(2)]

    for tt in range(NTT):
        for pair in range(8):
            pi = tt * 8 + pair
            hb = b.pb[pair % 2]
            gb = b.pb[2 + pair % 2]
            if pi >= 2:
                em.wait("tensor", sScl, pi - 1)
            if pair == 0 and tt >= 1:
                em.wait("tensor", sEv, tt * 8)
            for fc in range(KC):
                em.emit("tensor", lambda E_, fc=fc, pair=pair, tt=tt, hb=hb: E_.matmul(
                    hb[:, 0:512],
                    whgT[:, fc * 2048 + pair * 128: fc * 2048 + pair * 128 + 128],
                    xrms[:, fc * T + tt * 512: fc * T + tt * 512 + 512],
                    start=(fc == 0), stop=(fc == KC - 1)),
                    inc=(sHid, 1) if fc == KC - 1 else None)
            for fc in range(KC):
                em.emit("tensor", lambda E_, fc=fc, pair=pair, tt=tt, gb=gb: E_.matmul(
                    gb[:, 0:512],
                    whgT[:, fc * 2048 + 1024 + pair * 128: fc * 2048 + 1024 + pair * 128 + 128],
                    xrms[:, fc * T + tt * 512: fc * T + tt * 512 + 512],
                    start=(fc == 0), stop=(fc == KC - 1)),
                    inc=(sGat, 1) if fc == KC - 1 else None)
            sl2 = pi % 2
            em.wait("scalar", sHid, pi + 1)
            if pi >= 2:
                em.wait("scalar", sVec, pi - 1)
            em.emit("scalar", lambda E_, hb=hb, sl2=sl2: E_.copy(
                out=hidS[sl2][:], in_=hb[:, 0:512]))
            em.wait("scalar", sGat, pi + 1)
            em.emit("scalar", lambda E_, gb=gb, sl2=sl2: E_.copy(
                out=gatS[sl2][:], in_=gb[:, 0:512]))
            em.emit("scalar", lambda E_, sl2=sl2: E_.activation(
                out=aT[sl2][:], in_=gatS[sl2][:], func=AF.Sigmoid, bias=0.0, scale=-1.0))
            em.emit("scalar", lambda E_, sl2=sl2: E_.activation(
                out=sgS[sl2][:], in_=hidS[sl2][:], func=AF.Sigmoid), inc=(sScl, 1))
            em.wait("vector", sScl, pi + 1)
            if pair == 0 and tt >= 1:
                em.wait("vector", sOutM, tt * 8)
            em.emit("vector", lambda E_, sl2=sl2: E_.tensor_single_scalar(
                out=mskS[:], in_=hidS[sl2][:], scalar=0.0, op=OP.is_ge))
            em.emit("vector", lambda E_, sl2=sl2: E_.scalar_tensor_tensor(
                out=tS[:], in0=hidS[sl2][:], scalar=0.5, in1=sgS[sl2][:],
                op0=OP.add, op1=OP.subtract))
            em.emit("vector", lambda E_: E_.tensor_mul(tS[:], mskS[:], tS[:]))
            em.emit("vector", lambda E_, sl2=sl2: E_.tensor_add(tS[:], sgS[sl2][:], tS[:]))
            em.emit("vector", lambda E_, sl2=sl2: E_.tensor_mul(wT[:], aT[sl2][:], tS[:]))
            em.emit("vector", lambda E_: E_.tensor_sub(wT[:], tS[:], wT[:]))
            hdst = hbf[:, pair * 512:(pair + 1) * 512]
            if tt == 0:
                em.emit("vector", lambda E_, hdst=hdst, sl2=sl2: E_.tensor_tensor_scan(
                    out=hdst, data0=aT[sl2][:], data1=wT[:], initial=0.0,
                    op0=OP.mult, op1=OP.add))
            else:
                em.emit("vector", lambda E_, hdst=hdst, pair=pair, sl2=sl2: E_.tensor_tensor_scan(
                    out=hdst, data0=aT[sl2][:], data1=wT[:],
                    initial=carry[:, pair:pair + 1], op0=OP.mult, op1=OP.add))
            em.emit("vector", lambda E_, pair=pair: E_.tensor_copy(
                out=carry[:, pair:pair + 1],
                in_=hbf[:, pair * 512 + 511: pair * 512 + 512]), inc=(sVec, 1))
        ra1 = rows["alpha"][0:1, tt * 512:(tt + 1) * 512]
        rbe = rows["beta"][0:1, tt * 512:(tt + 1) * 512]
        em.wait("tensor", sScl, tt * 8 + 8)
        if tt >= 1:
            em.wait("tensor", sEv, tt * 8)
        em.emit("tensor", lambda E_, ra1=ra1: E_.matmul(
            b.pb[4][:, 0:512], b.cst["c_ones1"][:], ra1, start=True, stop=True))
        em.emit("tensor", lambda E_, rbe=rbe: E_.matmul(
            b.pb[5][:, 0:512], b.cst["c_ones1"][:], rbe, start=True, stop=True),
            inc=(sBc2, 1))
        em.wait("vector", sBc2, tt + 1)
        em.emit("vector", lambda E_: E_.tensor_scalar_add(
            a1bS[:], b.pb[4][:, 0:512], 1.0))
        em.emit("vector", lambda E_: E_.tensor_copy(out=bbS[:], in_=b.pb[5][:, 0:512]),
                inc=(sBcC, 2))
        em.wait("tensor", sVec, tt * 8 + 8)
        em.wait("tensor", sBcC, tt * 2 + 2)
        if tt >= 1:
            em.wait("tensor", sEv, tt * 8)
        for oc in range(KC):
            for kc in range(KC):
                em.emit("tensor", lambda E_, oc=oc, kc=kc: E_.matmul(
                    b.pb[oc][:, 0:512],
                    woutT[:, kc * 1024 + oc * 128: kc * 1024 + oc * 128 + 128],
                    hbf[:, kc * 512:(kc + 1) * 512],
                    start=(kc == 0), stop=(kc == KC - 1)),
                    inc=(sOutM, 1) if kc == KC - 1 else None)
        for oc in range(KC):
            i = tt * KC + oc
            if i >= 2:
                em.wait("sync", sEv, i - 1)
            em.emit("sync", lambda E_, oc=oc, tt=tt, slot=i % 2: E_.dma_start(
                out=x0r[slot][:], in_=x0_hbm[tt, oc]),
                inc=(sX0[i % 2], 16))
            em.wait("vector", sOutM, tt * KC + oc + 1)
            em.wait("vector", sX0[i % 2], (i // 2 + 1) * 16)
            if i >= 2:
                em.wait("vector", sSt[i % 2], (i // 2) * 16)
            em.emit("vector", lambda E_, slot=i % 2: E_.tensor_mul(
                tS[:], x0r[slot][:], a1bS[:]))
            em.emit("vector", lambda E_: E_.tensor_add(tS[:], tS[:], bbS[:]))
            em.emit("vector", lambda E_, oc=oc, slot=i % 2: E_.tensor_tensor(
                out=x1o[slot][:], in0=tS[:], in1=b.pb[oc][:, 0:512], op=OP.add),
                inc=(sEv, 1))
            em.wait("sync", sEv, i + 1)
            em.emit("sync", lambda E_, oc=oc, tt=tt, slot=i % 2: E_.dma_start(
                out=x1_hbm[tt, oc], in_=x1o[slot][:]), inc=(sSt[i % 2], 16))
    em.join([(sSt[0], em.counts[sSt[0]]), (sSt[1], em.counts[sSt[1]]),
             (sEv, em.counts[sEv])])
    b.pop()


def emit_phase_b(b, x1_hbm, x2_hbm):
    em, nc, T, NTT, NTC = b.em, b.nc, b.T, b.NTT, b.NTC
    pfx = "bb"
    wr_h = b.din("L0_ffn_Wr", [D, E])
    wg_h = b.din("L0_ffn_Wg", [E, H // 256, 128, KC * 256])
    wu_h = b.din("L0_ffn_Wu", [E, H // 256, 128, KC * 256])
    wd_h = b.din("L0_ffn_Wd", [E, H, D])

    rows = {}
    xn = b.sb("b_xn", [128, KC * T], BF16)
    Mc = b.sb("b_Mc", [8, T], F32)
    emit_ln_pass(b, pfx, x1_hbm, rows, want_rms=False, xn_sb=xn,
                 xn_rows=("alpha", "beta"))

    em.ser("sync", lambda E_: E_.dma_start(out=x2_hbm[:], in_=x1_hbm[:]), dma=True)

    b.push()
    wr_sb = b.sb("b_wr", [128, 8 * 8], F32)
    wrb = b.sb("b_wrb", [128, 8 * 8], BF16)
    logT = b.sb("b_logT", [8, T], F32)
    for fc in range(KC):
        em.ser("sync", lambda E_, fc=fc: E_.dma_start(
            out=wr_sb[:, fc * 8:(fc + 1) * 8], in_=wr_h[fc * 128:(fc + 1) * 128, :]),
            dma=True)
    em.ser("scalar", lambda E_: E_.copy(out=wrb[:], in_=wr_sb[:]))
    for tt in range(NTT):
        for fc in range(KC):
            em.ser("tensor", lambda E_, fc=fc, tt=tt: E_.matmul(
                b.pb[6][0:8, 0:512], wrb[:, fc * 8:(fc + 1) * 8],
                xn[:, fc * T + tt * 512: fc * T + tt * 512 + 512],
                start=(fc == 0), stop=(fc == KC - 1)))
        em.ser("vector", lambda E_, tt=tt: E_.tensor_copy(
            out=logT[0:8, tt * 512:(tt + 1) * 512], in_=b.pb[6][0:8, 0:512]))

    lg = b.sb("b_lg", [128, 8], F32)
    c1 = b.sb("b_c1", [128, 8], F32)
    c2 = b.sb("b_c2", [128, 8], F32)
    cc = b.sb("b_cc", [128, 8], F32)
    m1 = b.sb("b_m1", [128, 6], F32)
    for tc in range(NTC):
        em.ser("tensor", lambda E_, tc=tc: E_.transpose(
            b.pb[6][:, 0:8], logT[0:8, tc * 128:(tc + 1) * 128],
            b.cst["c_ident"][0:8, 0:8]))
        em.ser("vector", lambda E_: E_.tensor_copy(out=lg[:], in_=b.pb[6][:, 0:8]))
        em.ser("vector", lambda E_: E_.tensor_reduce(
            out=m1[:, 0:1], in_=lg[:], axis=AX, op=OP.max))
        em.ser("vector", lambda E_: E_.tensor_scalar(
            out=c1[:], in0=lg[:], scalar1=m1[:, 0:1], scalar2=None, op0=OP.is_equal))
        em.ser("vector", lambda E_: E_.tensor_mul(c1[:], c1[:], b.cst["c_p2row"][:]))
        em.ser("vector", lambda E_: E_.tensor_reduce(
            out=m1[:, 1:2], in_=c1[:], axis=AX, op=OP.max))
        em.ser("vector", lambda E_: E_.tensor_scalar(
            out=c1[:], in0=c1[:], scalar1=m1[:, 1:2], scalar2=None, op0=OP.is_equal))
        em.ser("vector", lambda E_: E_.scalar_tensor_tensor(
            out=cc[:], in0=c1[:], scalar=-1e30, in1=lg[:], op0=OP.mult, op1=OP.add))
        em.ser("vector", lambda E_: E_.tensor_reduce(
            out=m1[:, 2:3], in_=cc[:], axis=AX, op=OP.max))
        em.ser("vector", lambda E_: E_.tensor_scalar(
            out=c2[:], in0=cc[:], scalar1=m1[:, 2:3], scalar2=None, op0=OP.is_equal))
        em.ser("vector", lambda E_: E_.tensor_mul(c2[:], c2[:], b.cst["c_p2row"][:]))
        em.ser("vector", lambda E_: E_.tensor_reduce(
            out=m1[:, 3:4], in_=c2[:], axis=AX, op=OP.max))
        em.ser("vector", lambda E_: E_.tensor_scalar(
            out=c2[:], in0=c2[:], scalar1=m1[:, 3:4], scalar2=None, op0=OP.is_equal))
        em.ser("vector", lambda E_: E_.tensor_sub(m1[:, 4:5], m1[:, 0:1], m1[:, 2:3]))
        em.ser("scalar", lambda E_: E_.activation(
            out=m1[:, 4:5], in_=m1[:, 4:5], func=AF.Sigmoid))
        em.ser("vector", lambda E_: E_.tensor_scalar_mul(
            m1[:, 4:5], m1[:, 4:5], DEPTH_SCALE))
        em.ser("vector", lambda E_: E_.tensor_scalar(
            out=m1[:, 5:6], in0=m1[:, 4:5], scalar1=-1.0, scalar2=DEPTH_SCALE,
            op0=OP.mult, op1=OP.add))
        em.ser("vector", lambda E_: E_.tensor_scalar(
            out=cc[:], in0=c1[:], scalar1=m1[:, 4:5], scalar2=None, op0=OP.mult))
        em.ser("vector", lambda E_: E_.scalar_tensor_tensor(
            out=cc[:], in0=c2[:], scalar=m1[:, 5:6], in1=cc[:],
            op0=OP.mult, op1=OP.add))
        em.ser("tensor", lambda E_, tc=tc: E_.transpose(
            b.pb[6][0:8, 0:128], cc[:], b.cst["c_ident"][:]))
        em.ser("vector", lambda E_, tc=tc: E_.tensor_copy(
            out=Mc[0:8, tc * 128:(tc + 1) * 128], in_=b.pb[6][0:8, 0:128]))
    em.join([(em.cch, em.counts[em.cch]), (em.dch, em.counts[em.dch])])
    b.pop()

    b.push()
    emit_moe_loop(b, "b", xn, Mc, x2_hbm, wg_h, wu_h, wd_h, None, None,
                  silu_glu=True)
    b.pop()


def emit_moe_loop(b, pfx, xn, Mc, xout_hbm, w1_h, wu_h, w2_h, b1cols, b2cols,
                  silu_glu):
    em, T = b.em, b.T
    em.join([(em.cch, em.counts[em.cch]), (em.dch, em.counts[em.dch])])
    TB = 512
    NTB = T // TB
    TBG = 2
    NPASS = NTB // TBG
    HQ8 = 16
    QW = H // HQ8
    HQC = HC // HQ8

    wgb = [b.sb(f"{pfx}_wgb{i}", [128, KC * QW], BF16) for i in range(2)]
    wub = ([b.sb(f"{pfx}_wub{i}", [128, KC * QW], BF16) for i in range(2)]
           if silu_glu else None)
    NWR = 4
    wdb = [b.sb(f"{pfx}_wdb{i}", [128, D], BF16) for i in range(NWR)]
    h_all = [b.sb(f"{pfx}_h{i}", [128, HC * TB], BF16) for i in range(TBG)]
    acc = b.sb(f"{pfx}_acc", [128, KC * TB], F32)
    ce_t = [b.sb(f"{pfx}_ce{i}", [128, TB], F32) for i in range(TBG)]
    silu_t = [b.sb(f"{pfx}_si{i}", [128, TB], BF16) for i in range(2)]
    evt = ([b.sb(f"{pfx}_evt{i}", [128, TB], F32) for i in range(2)]
           if b2cols is not None else None)

    sW = [em.sem(f"{pfx}_w{i}") for i in range(2)]
    sWD = [em.sem(f"{pfx}_wd{i}") for i in range(NWR)]
    sWDF = em.sem(f"{pfx}_wdf")
    sG = em.sem(f"{pfx}_g")
    sU = em.sem(f"{pfx}_u")
    sGF = em.sem(f"{pfx}_gf")
    sH = em.sem(f"{pfx}_hh")
    sD = em.sem(f"{pfx}_d")
    sCe = em.sem(f"{pfx}_cem")
    sCeC = em.sem(f"{pfx}_cec")
    sB2 = em.sem(f"{pfx}_b2")
    sEv = em.sem(f"{pfx}_ev")
    sSt = [em.sem(f"{pfx}_sb{i}") for i in range(NTB)]
    hdone = sH if silu_glu else sGF

    per_q = 2 * 16 if silu_glu else 16
    NQ = E * NPASS * HQ8

    def load_q(q, par):
        e2 = q // (NPASS * HQ8)
        hq2 = q % HQ8
        em.emit("gpsimd", lambda E_, e2=e2, hq2=hq2, par=par: E_.dma_start(
            out=wgb[par][:], in_=w1_h[e2, hq2]), inc=(sW[par], 16))
        if silu_glu:
            em.emit("gpsimd", lambda E_, e2=e2, hq2=hq2, par=par: E_.dma_start(
                out=wub[par][:], in_=wu_h[e2, hq2]), inc=(sW[par], 16))

    load_q(0, 0)
    q_loaded = 1
    gi_c = [0]
    store_done = {}

    for e in range(E):
        for p in range(NPASS):
            pi = e * NPASS + p
            tbs = [p * TBG + i for i in range(TBG)]
            for i2, tb in enumerate(tbs):
                t0 = tb * TB
                bank = b.pb[6 + i2]
                if pi >= 1:
                    em.wait("tensor", sEv, pi * TBG * KC)
                em.emit("tensor", lambda E_, e=e, t0=t0, bank=bank: E_.matmul(
                    bank[:, 0:TB], b.cst["c_oneh"][0:8, e * 128:(e + 1) * 128],
                    Mc[0:8, t0:t0 + TB], start=True, stop=True), inc=(sCe, 1))
                cCe = em.counts[sCe]
                em.wait("vector", sCe, cCe)
                em.emit("vector", lambda E_, i2=i2, bank=bank: E_.tensor_copy(
                    out=ce_t[i2][:], in_=bank[:, 0:TB]), inc=(sCeC, 1))
            for hq in range(HQ8):
                qi = pi * HQ8 + hq
                par = qi % 2
                if q_loaded < NQ:
                    if q_loaded >= 2:
                        em.wait("gpsimd", sU if silu_glu else sG,
                                (q_loaded - 1) * HQC * TBG)
                    load_q(q_loaded, q_loaded % 2)
                    q_loaded += 1
                em.wait("tensor", sW[par], (qi // 2 + 1) * per_q)
                for hc4 in range(HQC):
                    hc = hq * HQC + hc4
                    for i2, tb in enumerate(tbs):
                        t0 = tb * TB
                        gi = gi_c[0]
                        gi_c[0] += 1
                        gb = b.pb[gi % 2] if silu_glu else b.pb[gi % 4]
                        ub = b.pb[2 + gi % 2] if silu_glu else None
                        if silu_glu:
                            if gi >= 2:
                                em.wait("tensor", sGF, gi - 1)
                                em.wait("tensor", sH, gi - 1)
                        elif gi >= 4:
                            em.wait("tensor", sGF, gi - 3)
                        if hc == 0 and i2 == 0 and pi >= 1:
                            em.wait("tensor", sEv, pi * TBG * KC)
                        for kc in range(KC):
                            em.emit("tensor", lambda E_, par=par, kc=kc, hc4=hc4, gb=gb, t0=t0: E_.matmul(
                                gb[:, 0:TB],
                                wgb[par][:, kc * QW + hc4 * 128: kc * QW + hc4 * 128 + 128],
                                xn[:, kc * T + t0: kc * T + t0 + TB],
                                start=(kc == 0), stop=(kc == KC - 1)),
                                inc=(sG, 1) if kc == KC - 1 else None)
                        if silu_glu:
                            for kc in range(KC):
                                em.emit("tensor", lambda E_, par=par, kc=kc, hc4=hc4, ub=ub, t0=t0: E_.matmul(
                                    ub[:, 0:TB],
                                    wub[par][:, kc * QW + hc4 * 128: kc * QW + hc4 * 128 + 128],
                                    xn[:, kc * T + t0: kc * T + t0 + TB],
                                    start=(kc == 0), stop=(kc == KC - 1)),
                                    inc=(sU, 1) if kc == KC - 1 else None)
                        em.wait("scalar", sG, gi + 1)
                        if silu_glu:
                            if gi >= 2:
                                em.wait("scalar", sH, gi - 1)
                            em.emit("scalar", lambda E_, gi=gi, gb=gb: E_.activation(
                                out=silu_t[gi % 2][:], in_=gb[:, 0:TB], func=AF.Silu),
                                inc=(sGF, 1))
                            em.wait("vector", sU, gi + 1)
                            em.wait("vector", sGF, gi + 1)
                            em.emit("vector", lambda E_, gi=gi, ub=ub, hc=hc, i2=i2: E_.tensor_tensor(
                                out=h_all[i2][:, hc * TB:(hc + 1) * TB],
                                in0=silu_t[gi % 2][:], in1=ub[:, 0:TB], op=OP.mult),
                                inc=(sH, 1))
                        else:
                            em.emit("scalar", lambda E_, e=e, hc=hc, gb=gb, i2=i2: E_.activation(
                                out=h_all[i2][:, hc * TB:(hc + 1) * TB], in_=gb[:, 0:TB],
                                func=AF.Gelu, bias=b1cols[:, e * HC + hc: e * HC + hc + 1],
                                scale=1.0), inc=(sGF, 1))
            for i2, tb in enumerate(tbs):
                di = pi * TBG + i2
                t0 = tb * TB
                for hc in range(HC):
                    wi = di * HC + hc
                    if wi >= NWR:
                        w2 = wi - NWR
                        if w2 % HC == HC - 1:
                            em.wait("gpsimd", sD, (w2 // HC + 1) * KC)
                        else:
                            em.wait("gpsimd", sWDF, (w2 // HC) * (HC - 1) + (w2 % HC) + 1)
                    em.emit("gpsimd", lambda E_, e=e, hc=hc, r=wi % NWR: E_.dma_start(
                        out=wdb[r][:], in_=w2_h[e, hc * 128:(hc + 1) * 128, :]),
                        inc=(sWD[wi % NWR], 16))
                for hc in range(HC):
                    wi = di * HC + hc
                    par = wi % NWR
                    em.wait("tensor", sWD[par], (wi // NWR + 1) * 16)
                    if hc == 0:
                        em.wait("tensor", hdone, (pi + 1) * HQ8 * HQC * TBG)
                        if silu_glu:
                            em.wait("tensor", sGF, (pi + 1) * HQ8 * HQC * TBG)
                        em.wait("tensor", sEv, di * KC)
                        em.wait("tensor", sCeC, (pi + 1) * TBG)
                    for dc in range(KC):
                        if hc == HC - 1:
                            inc = (sD, 1)
                        elif dc == KC - 1:
                            inc = (sWDF, 1)
                        else:
                            inc = None
                        em.emit("tensor", lambda E_, par=par, dc=dc, hc=hc, i2=i2: E_.matmul(
                            b.pb[dc][:, 0:TB],
                            wdb[par][:, dc * 128:(dc + 1) * 128],
                            h_all[i2][:, hc * TB:(hc + 1) * TB],
                            start=(hc == 0), stop=(hc == HC - 1)), inc=inc)
                cDbase = di * KC
                for dc in range(KC):
                    em.wait("vector", sD, cDbase + dc + 1)
                    if dc == 0:
                        for tbx, cnt in store_done.items():
                            em.wait("vector", sSt[tbx], cnt)
                    if b2cols is None:
                        em.emit("vector", lambda E_, dc=dc, i2=i2: E_.tensor_tensor(
                            out=acc[:, dc * TB:(dc + 1) * TB], in0=b.pb[dc][:, 0:TB],
                            in1=ce_t[i2][:], op=OP.mult), inc=(sEv, 1))
                    else:
                        nv = cDbase + dc
                        em.wait("scalar", sD, nv + 1)
                        if nv >= 2:
                            em.wait("scalar", sEv, nv - 1)
                        em.emit("scalar", lambda E_, dc=dc, e=e, p=nv % 2: E_.add(
                            out=evt[p][:], in_=b.pb[dc][:, 0:TB],
                            add=b2cols[:, e * KC + dc: e * KC + dc + 1]), inc=(sB2, 1))
                        em.wait("vector", sB2, nv + 1)
                        em.emit("vector", lambda E_, dc=dc, i2=i2, p=nv % 2: E_.tensor_tensor(
                            out=acc[:, dc * TB:(dc + 1) * TB], in0=evt[p][:],
                            in1=ce_t[i2][:], op=OP.mult), inc=(sEv, 1))
                em.wait("gpsimd", sEv, em.counts[sEv])
                em.wait("gpsimd", sSt[tb], em.counts[sSt[tb]])
                for dc in range(KC):
                    em.emit("gpsimd", lambda E_, dc=dc, tb=tb: E_.dma_start(
                        out=xout_hbm[tb, dc],
                        in_=acc[:, dc * TB:(dc + 1) * TB], accum_op=OP.add),
                        inc=(sSt[tb], 16))
                store_done[tb] = em.counts[sSt[tb]]
    em.join([(s, em.counts[s]) for s in sSt])


def emit_phase_c(b, x2_hbm, x3_hbm):
    em, nc, T, NTT, NTC = b.em, b.nc, b.T, b.NTT, b.NTC
    pfx = "cc"
    wq_h = b.din("L1_attn_Wq", [D, D])
    bq_h = b.din("L1_attn_bq", [D])
    wk_h = b.din("L1_attn_Wk", [D, D])
    bk_h = b.din("L1_attn_bk", [D])
    wv_h = b.din("L1_attn_Wv", [D, D])
    bv_h = b.din("L1_attn_bv", [D])
    wf_h = b.din("L1_attn_Wf", [2, KC, 64, 8 * 128])
    bf_h = b.din("L1_attn_bf", [D])

    rows = alloc_rows(b, pfx, ["alpha", "beta"], persist=("alpha", "beta"),
                      persist_dtype=BF16)
    em.ser("sync", lambda E_: E_.dma_start(out=x3_hbm[:], in_=x2_hbm[:]), dma=True)
    cpy_d = em.counts[em.dch]
    emit_ln_pass(b, pfx, x2_hbm, rows, want_rms=False)

    VST = 8 * 65
    kTh = b.sb("c_kTh", [128, 4 * T], BF16)
    qTh = b.sb("c_qTh", [128, 4 * T], BF16)
    vh = b.sb("c_vh", [128, NTC * VST], BF16)
    lrow = b.sb("c_lrow", [1, 512], F32)
    bqc = b.sb("c_bqc", [128, KC], F32)
    bkc = b.sb("c_bkc", [128, KC], F32)
    bfc = b.sb("c_bfc", [128, KC], F32)
    bvb = b.sb("c_bvb", [128, D], BF16)
    nrm = b.sb("c_nrm", [2, 8], F32)
    mc1 = b.sb("c_mc1", [2, 1], F32)
    bndc = b.sb("c_bndc", [2, 4], F32)
    bcolsP = [b.sb(f"c_bcol{i}", [128, 4], F32) for i in range(2)]
    attnh = b.sb("c_attnh", [64, 8 * 512], BF16)
    wfb = [b.sb(f"c_wfb{i}", [64, 8 * 128], BF16) for i in range(2)]
    dnames = [f"c_diag_{d}{bb}" for d in ("f", "b") for bb in range(4)]
    dnames += ["c_rel_kq512", "c_ones64"]
    for dname in dnames:
        b.cst[dname] = b.sb(dname + "_sb", list(CONST_SPECS[dname]), BF16)
    b.push()
    stage = b.sb("c_stage", [128, 512], F32)
    for dname in dnames:
        h = b.din(dname, CONST_SPECS[dname])
        ncol = CONST_SPECS[dname][1]
        em.ser("sync", lambda E_, h=h, ncol=ncol: E_.dma_start(
            out=stage[:, 0:ncol], in_=h[:]), dma=True)
        em.ser("scalar", lambda E_, t=b.cst[dname], ncol=ncol: E_.copy(
            out=t[:], in_=stage[:, 0:ncol]))
    for hh in range(2):
        em.ser("sync", lambda E_, hh=hh: E_.dma_start(
            out=stage[0:1, 0:512],
            in_=bv_h.ap().rearrange("(o d) -> o d", o=2)[hh:hh + 1, :]), dma=True)
        em.ser("tensor", lambda E_: E_.matmul(
            b.pb[6][:, 0:512], b.cst["c_ones1"][:], stage[0:1, 0:512],
            start=True, stop=True))
        em.ser("vector", lambda E_, hh=hh: E_.tensor_copy(
            out=bvb[:, hh * 512:(hh + 1) * 512], in_=b.pb[6][:, 0:512]))
    em.barrier()
    b.pop()
    wbuf = b.sb("c_wbuf", [128, KC * D], BF16)
    x2t = [b.sb(f"c_x2t{i}", [128, 2 * 512], F32) for i in range(2)]
    xnt = [b.sb(f"c_xnt{i}", [128, KC * 512], BF16) for i in range(2)]
    srelP = [b.sb(f"c_srelP{i}", [128, 512], BF16) for i in range(2)]
    offc = [b.sb(f"c_offc{i}", [128, 64], F32) for i in range(2)]
    offc2 = [b.sb(f"c_offc2{i}", [128, 64], F32) for i in range(2)]
    diagP = [[b.sb(f"c_diagP{i}_{hx}", [128, 512], BF16) for i in range(4)]
             for hx in range(2)]
    scoreS = [x2t[0][:, 0:512], x2t[1][:, 0:512], x2t[0][:, 512:1024]]
    rb = x2t[1][0:64, 512:1024]
    sq2 = scoreS[0:2]
    pT = [b.sb(f"c_pT{i}", [128, 512], BF16) for i in range(3)]
    st3 = [b.sb(f"c_st3{i}", [128, 512], F32) for i in range(2)]
    txf = b.sb("c_txf", [128, 512], F32)
    slopes = alibi_slopes()

    for vh_, vc_, vn_ in ((bq_h, bqc, "c_bqt"), (bk_h, bkc, "c_bkt"),
                          (bf_h, bfc, "c_bft")):
        b.push()
        vec2cols(b, vh_, D, vc_, vn_)
        em.barrier()
        b.pop()

    for tc in range(NTC):
        for h8 in range(8):
            em.emit("vector", lambda E_, tc=tc, h8=h8: E_.memset(
                vh[:, tc * VST + h8 * 65 + 64: tc * VST + h8 * 65 + 65], 1.0))

    sXL = [em.sem(f"c_xl{i}") for i in range(2)]
    sXB = em.sem("c_xb")
    sBC2 = em.sem("c_bc2")
    sKB = em.sem("c_kb")
    sKE = em.sem("c_ke")
    sWB = em.sem("c_wbl")
    sSq = em.sem("c_sq2")
    sNmm = em.sem("c_nmm")
    sRed = em.sem("c_red")
    sS = em.sem("c_sS")
    sTx = em.sem("c_stx")
    sPe = em.sem("c_spe")
    sPv = em.sem("c_spv")
    sRcp = em.sem("c_rcp")
    sRbm = em.sem("c_rbm")
    sPat = em.sem("c_pat")
    sEvh = em.sem("c_evh")
    sWF = em.sem("c_swf")
    sF = em.sem("c_sf")
    sFe = em.sem("c_sfe")
    sV3m = em.sem("c_v3m")
    sSt3 = [em.sem(f"c_st3s{i}") for i in range(2)]
    st = {"xc": 0, "xt": 0}

    first_store = [True]
    ihead = [0]
    cF_attnh = [0]

    for half in range(2):
        hbase = half * 8
        em.join([(s, em.counts[s]) for s in
                 (sKE, sKB, sXB, sBC2, sWB, sS, sTx, sPe, sPv, sRcp, sRbm,
                  sEvh, sWF, sF, sFe, sV3m, sSt3[0], sSt3[1], sXL[0], sXL[1],
                  sSq, sNmm, sRed, em.cch, em.dch)]
                + [(s, em.counts[s]) for s in em.selfsem.values()])

        mxb = txf

        def build_xnt(tt, guard=0):
            ti = st["xt"]
            st["xt"] += 1
            p = ti % 2
            if guard > 0:
                em.wait("vector", sKE, guard)
            ral = rows["alpha"][0:1, tt * 512:(tt + 1) * 512]
            rbe = rows["beta"][0:1, tt * 512:(tt + 1) * 512]
            for r in range(4):
                i = st["xc"]
                st["xc"] += 1
                pr = i % 2
                if i >= 2:
                    em.wait("gpsimd", sXB, i - 1)
                for fc4 in range(2):
                    fc = r * 2 + fc4
                    em.emit("gpsimd", lambda E_, tt=tt, pr=pr, fc=fc, fc4=fc4: E_.dma_start(
                        out=x2t[pr][:, fc4 * 512:(fc4 + 1) * 512],
                        in_=x2_hbm[tt, fc]), inc=(sXL[pr], 16))
                cL = em.counts[sXL[pr]]
                if r == 0:
                    em.wait("tensor", sXB, 4 * ti)
                    em.emit("tensor", lambda E_, ral=ral: E_.matmul(
                        b.pb[2][:, 0:512], b.cst["c_ones1b"][:], ral,
                        start=True, stop=True))
                    em.emit("tensor", lambda E_, rbe=rbe: E_.matmul(
                        b.pb[3][:, 0:512], b.cst["c_ones1b"][:], rbe,
                        start=True, stop=True), inc=(sBC2, 1))
                    cB = em.counts[sBC2]
                    em.wait("vector", sBC2, cB)
                em.wait("vector", sXL[pr], cL)
                for fc4 in range(2):
                    fc = r * 2 + fc4
                    em.emit("vector", lambda E_, fc=fc, fc4=fc4, pr=pr: E_.tensor_tensor(
                        out=mxb[:], in0=x2t[pr][:, fc4 * 512:(fc4 + 1) * 512],
                        in1=b.pb[2][:, 0:512], op=OP.mult))
                    em.emit("vector", lambda E_, fc=fc, p=p: E_.tensor_tensor(
                        out=xnt[p][:, fc * 512:(fc + 1) * 512], in0=mxb[:],
                        in1=b.pb[3][:, 0:512], op=OP.add),
                        inc=(sXB, 1) if fc4 == 1 else None)
            return p, em.counts[sXB]

        def load_w(wh):
            em.wait("gpsimd", sKB, em.counts[sKB])
            for fc in range(KC):
                em.emit("gpsimd", lambda E_, fc=fc, wh=wh: E_.dma_start(
                    out=wbuf[:, fc * D:(fc + 1) * D],
                    in_=wh[fc * 128:(fc + 1) * 128, :]), inc=(sWB, 16))
            return em.counts[sWB]

        wk_d = load_w(wk_h)
        cKE0 = em.counts[sKE]
        for tt in range(NTT):
            p, cXB = build_xnt(tt, cKE0 + 4 * (tt - 1) if tt >= 2 else 0)
            em.wait("tensor", sXB, cXB)
            if tt == 0:
                em.wait("tensor", sWB, wk_d)
            for c4 in range(4):
                dcg = half * 4 + c4
                bank = b.pb[4 + c4 % 2]
                if em.counts[sKB] >= 2:
                    em.wait("tensor", sKE, em.counts[sKB] - 1)
                for fc in range(KC):
                    em.emit("tensor", lambda E_, fc=fc, dcg=dcg, p=p, bank=bank: E_.matmul(
                        bank[:, 0:512],
                        wbuf[:, fc * D + dcg * 128: fc * D + dcg * 128 + 128],
                        xnt[p][:, fc * 512:(fc + 1) * 512],
                        start=(fc == 0), stop=(fc == KC - 1)),
                        inc=(sKB, 1) if fc == KC - 1 else None)
                cKB = em.counts[sKB]
                em.wait("scalar", sKB, cKB)
                em.emit("scalar", lambda E_, c4=c4, tt=tt, dcg=dcg, bank=bank: E_.add(
                    out=kTh[:, c4 * T + tt * 512: c4 * T + tt * 512 + 512],
                    in_=bank[:, 0:512], add=bkc[:, dcg:dcg + 1]), inc=(sKE, 1))
        wv_d = load_w(wv_h)
        cKE0 = em.counts[sKE]
        for tt in range(NTT):
            p, cXB = build_xnt(tt, cKE0 + 4 * (tt - 1) if tt >= 2 else 0)
            em.wait("tensor", sXB, cXB)
            if tt == 0:
                em.wait("tensor", sWB, wv_d)
                em.wait("tensor", sKE, em.counts[sKE])
            for tc4 in range(4):
                tc = tt * 4 + tc4
                bank = b.pb[4 + tc4 % 2]
                if em.counts[sKB] >= 2:
                    em.wait("tensor", sKE, em.counts[sKB] - 1)
                for fc in range(KC):
                    em.emit("tensor", lambda E_, fc=fc, tc4=tc4, p=p, bank=bank, hbase=hbase: E_.matmul(
                        bank[:, 0:512],
                        xnt[p][:, fc * 512 + tc4 * 128: fc * 512 + tc4 * 128 + 128],
                        wbuf[:, fc * D + hbase * 64: fc * D + hbase * 64 + 512],
                        start=(fc == 0), stop=(fc == KC - 1)),
                        inc=(sKB, 1) if fc == KC - 1 else None)
                cKB = em.counts[sKB]
                em.wait("vector", sKB, cKB)
                for h8 in range(8):
                    em.emit("vector", lambda E_, tc=tc, h8=h8, bank=bank, half=half: E_.tensor_tensor(
                        out=vh[:, tc * VST + h8 * 65: tc * VST + h8 * 65 + 64],
                        in0=bank[:, h8 * 64:(h8 + 1) * 64],
                        in1=bvb[:, half * 512 + h8 * 64: half * 512 + h8 * 64 + 64],
                        op=OP.add), inc=(sKE, 1) if h8 == 7 else None)
        wq_d = load_w(wq_h)
        cKE0 = em.counts[sKE]
        for tt in range(NTT):
            p, cXB = build_xnt(tt, cKE0 + 4 * (tt - 1) if tt >= 2 else 0)
            em.wait("tensor", sXB, cXB)
            if tt == 0:
                em.wait("tensor", sWB, wq_d)
                em.wait("tensor", sKE, em.counts[sKE])
            for c4 in range(4):
                dcg = half * 4 + c4
                bank = b.pb[4 + c4 % 2]
                if em.counts[sKB] >= 2:
                    em.wait("tensor", sKE, em.counts[sKB] - 1)
                for fc in range(KC):
                    em.emit("tensor", lambda E_, fc=fc, dcg=dcg, p=p, bank=bank: E_.matmul(
                        bank[:, 0:512],
                        wbuf[:, fc * D + dcg * 128: fc * D + dcg * 128 + 128],
                        xnt[p][:, fc * 512:(fc + 1) * 512],
                        start=(fc == 0), stop=(fc == KC - 1)),
                        inc=(sKB, 1) if fc == KC - 1 else None)
                cKB = em.counts[sKB]
                em.wait("scalar", sKB, cKB)
                em.emit("scalar", lambda E_, c4=c4, tt=tt, dcg=dcg, bank=bank: E_.add(
                    out=qTh[:, c4 * T + tt * 512: c4 * T + tt * 512 + 512],
                    in_=bank[:, 0:512], add=bqc[:, dcg:dcg + 1]), inc=(sKE, 1))
        em.join([(s, em.counts[s]) for s in
                 (sKE, sKB, sXB, sBC2, sWB, sXL[0], sXL[1], em.cch, em.dch)]
                + [(s, em.counts[s]) for s in em.selfsem.values()])

        for si, src in enumerate((kTh, qTh)):
            for hc2 in range(4):
                for tt in range(NTT):
                    i = em.counts[sSq]
                    if i >= 2:
                        em.wait("scalar", sNmm, i - 1)
                    em.emit("scalar", lambda E_, src=src, hc2=hc2, tt=tt, p=i % 2: E_.square(
                        out=sq2[p][:],
                        in_=src[:, hc2 * T + tt * 512: hc2 * T + tt * 512 + 512]),
                        inc=(sSq, 1))
                    em.wait("tensor", sSq, i + 1)
                    if i >= 2:
                        em.wait("tensor", sRed, i - 1)
                    em.emit("tensor", lambda E_, p=i % 2, bk=b.pb[6 + i % 2]: E_.matmul(
                        bk[0:2, 0:512], b.cst["c_bones2"][:], sq2[p][:],
                        start=True, stop=True), inc=(sNmm, 1))
                    em.wait("vector", sNmm, i + 1)
                    col = nrm[0:2, si * 4 + hc2: si * 4 + hc2 + 1]
                    if tt == 0:
                        em.emit("vector", lambda E_, col=col, bk=b.pb[6 + i % 2]: E_.tensor_reduce(
                            out=col, in_=bk[0:2, 0:512], axis=AX, op=OP.max),
                            inc=(sRed, 1))
                    else:
                        em.emit("vector", lambda E_, bk=b.pb[6 + i % 2]: E_.tensor_reduce(
                            out=mc1[:], in_=bk[0:2, 0:512], axis=AX, op=OP.max))
                        em.emit("vector", lambda E_, col=col: E_.tensor_tensor(
                            out=col, in0=col, in1=mc1[:], op=OP.max), inc=(sRed, 1))
        em.join([(sSq, em.counts[sSq]), (sNmm, em.counts[sNmm]),
                 (sRed, em.counts[sRed])]
                + [(s, em.counts[s]) for s in em.selfsem.values()])
        em.ser("scalar", lambda E_: E_.sqrt(out=nrm[:], in_=nrm[:]))
        em.ser("vector", lambda E_: E_.tensor_mul(bndc[:], nrm[0:2, 0:4],
                                                  nrm[0:2, 4:8]))
        em.ser("vector", lambda E_: E_.tensor_scalar(
            out=bndc[:], in0=bndc[:], scalar1=0.125, scalar2=1.0,
            op0=OP.mult, op1=OP.add))
        for hp in range(2):
            em.ser("tensor", lambda E_, hp=hp: E_.matmul(
                b.pb[6][:, 0:4], b.cst["c_oneh"][0:2, hp * 128:(hp + 1) * 128],
                bndc[:], start=True, stop=True))
            em.ser("vector", lambda E_, hp=hp: E_.tensor_copy(
                out=bcolsP[hp][:], in_=b.pb[6][:, 0:4]))

        stt_exp = []
        for qt in range(NTT):
            for hpair in range(4):
                heads = (2 * hpair, 2 * hpair + 1)
                hc2 = hpair
                fwd = (hbase + heads[0]) < 8
                slope = float(slopes[hc2 * 2]), float(slopes[hc2 * 2 + 1])
                slh = [(s if fwd else -s) for s in slope]
                shallow = hpair >= 2
                em.wait("vector", sPe, em.counts[sPe])
                for hx in range(2):
                    sl = slh[hx]
                    bcol = bcolsP[heads[hx] % 2][:, hc2:hc2 + 1]
                    if shallow:
                        em.emit("vector", lambda E_, sl=sl, bcol=bcol, hx=hx: E_.tensor_scalar(
                            out=offc2[hx][:], in0=b.cst["c_doff2"][:], scalar1=sl,
                            scalar2=bcol, op0=OP.mult, op1=OP.subtract),
                            inc=(sPat, 1))
                        if hx == 0:
                            for bb in range(4):
                                dname = (f"c_diag_f{bb}" if fwd else f"c_diag_b{bb}")
                                thr = -1e9 if fwd else 1e9
                                opc = OP.is_le if fwd else OP.is_ge
                                em.emit("vector", lambda E_, dname=dname, bb=bb, thr=thr, opc=opc: E_.tensor_single_scalar(
                                    out=diagP[0][bb][:], in_=b.cst[dname][:],
                                    scalar=thr, op=opc))
                                em.emit("vector", lambda E_, bb=bb: E_.tensor_scalar_mul(
                                    diagP[0][bb][:], diagP[0][bb][:], -1e10))
                    else:
                        em.emit("vector", lambda E_, sl=sl, bcol=bcol, hx=hx: E_.tensor_scalar(
                            out=srelP[hx][:], in0=b.cst["c_rel_kq512"][:], scalar1=sl,
                            scalar2=bcol, op0=OP.mult, op1=OP.subtract))
                        for bb in range(4):
                            dname = (f"c_diag_f{bb}" if fwd else f"c_diag_b{bb}")
                            em.emit("vector", lambda E_, dname=dname, sl=sl, bcol=bcol, bb=bb, hx=hx: E_.tensor_scalar(
                                out=diagP[hx][bb][:], in0=b.cst[dname][:], scalar1=sl,
                                scalar2=bcol, op0=OP.mult, op1=OP.subtract))
                        em.emit("vector", lambda E_, sl=sl, hx=hx: E_.tensor_scalar_mul(
                            offc[hx][:], b.cst["c_doff"][:], sl))
                kt_lo = 0 if fwd else qt * 4
                kt_hi = qt * 4 + 4 if fwd else NTC
                nk = kt_hi - kt_lo
                nj = 2 * nk
                base_e = em.counts[sPe]
                pvb = [b.pb[3], b.pb[4]]
                sbanks = {}

                def emit_score(j):
                    kk, hx = j // 2, j % 2
                    kt = kt_lo + kk
                    h8 = heads[hx]
                    hp = (h8 % 2) * 64
                    isc = em.counts[sS]
                    bank = b.pb[isc % 3]
                    sbanks[j] = bank
                    if isc >= 3:
                        em.wait("tensor", sPe, isc - 2)
                    if j == 0 and ihead[0] >= 2:
                        em.wait("tensor", sEvh, ihead[0])
                    em.emit("tensor", lambda E_, kt=kt, bank=bank, hp=hp, hc2=hc2, qt=qt: E_.matmul(
                        bank[:, 0:512],
                        kTh[hp:hp + 64, hc2 * T + kt * 128: hc2 * T + kt * 128 + 128],
                        qTh[hp:hp + 64, hc2 * T + qt * 512: hc2 * T + qt * 512 + 512],
                        start=True, stop=True), inc=(sS, 1))

                for j0 in range(min(2, nj)):
                    emit_score(j0)
                for j in range(nj):
                    if j + 2 < nj:
                        emit_score(j + 2)
                    kk, hx = j // 2, j % 2
                    kt = kt_lo + kk
                    h8 = heads[hx]
                    bank = sbanks.pop(j)
                    diag = (qt * 4 <= kt < qt * 4 + 4)
                    m = em.counts[sPe]
                    if shallow:
                        dcol = offc2[hx][:, kt - 4 * qt + 32: kt - 4 * qt + 33]
                        if diag:
                            n = em.counts[sTx]
                            em.wait("vector", sS, base_e + j + 1)
                            if len(stt_exp) >= 3:
                                em.wait("vector", sPe, stt_exp[-3])
                            em.emit("vector", lambda E_, bank=bank, pat=diagP[0][kt - qt * 4], p=n % 3: E_.scalar_tensor_tensor(
                                out=scoreS[p][:], in0=bank[:, 0:512], scalar=0.125,
                                in1=pat[:], op0=OP.mult, op1=OP.add), inc=(sTx, 1))
                            stt_exp.append(m + 1)
                            em.wait("scalar", sTx, n + 1)
                            if m >= 3:
                                em.wait("scalar", sPv, m - 2)
                            em.emit("scalar", lambda E_, p=n % 3, dcol=dcol, q=m % 3: E_.activation(
                                out=pT[q][:], in_=scoreS[p][:], func=AF.Exp,
                                bias=dcol, scale=1.0), inc=(sPe, 1))
                        else:
                            em.wait("scalar", sS, base_e + j + 1)
                            em.wait("scalar", sPat, em.counts[sPat])
                            if m >= 3:
                                em.wait("scalar", sPv, m - 2)
                            em.emit("scalar", lambda E_, bank=bank, dcol=dcol, q=m % 3: E_.activation(
                                out=pT[q][:], in_=bank[:, 0:512], func=AF.Exp,
                                bias=dcol, scale=0.125), inc=(sPe, 1))
                    else:
                        n = em.counts[sTx]
                        em.wait("vector", sS, base_e + j + 1)
                        if len(stt_exp) >= 3:
                            em.wait("vector", sPe, stt_exp[-3])
                        pat = (diagP[hx][kt - qt * 4] if diag else srelP[hx])
                        em.emit("vector", lambda E_, bank=bank, pat=pat, p=n % 3: E_.scalar_tensor_tensor(
                            out=scoreS[p][:], in0=bank[:, 0:512], scalar=0.125,
                            in1=pat[:], op0=OP.mult, op1=OP.add), inc=(sTx, 1))
                        stt_exp.append(m + 1)
                        em.wait("scalar", sTx, n + 1)
                        if m >= 3:
                            em.wait("scalar", sPv, m - 2)
                        bias_ap = (0.0 if diag
                                   else offc[hx][:, kt - 4 * qt + 32: kt - 4 * qt + 33])
                        em.emit("scalar", lambda E_, p=n % 3, bias_ap=bias_ap, q=m % 3: E_.activation(
                            out=pT[q][:], in_=scoreS[p][:], func=AF.Exp,
                            bias=bias_ap, scale=1.0), inc=(sPe, 1))
                    em.wait("tensor", sPe, m + 1)
                    em.emit("tensor", lambda E_, kt=kt, h8=h8, q=m % 3, kk=kk, nk=nk, pv=pvb[hx]: E_.matmul(
                        pv[0:65, 0:512], vh[:, kt * VST + h8 * 65: kt * VST + h8 * 65 + 65],
                        pT[q][:], start=(kk == 0), stop=(kk == nk - 1)),
                        inc=(sPv, 1))
                cPv = em.counts[sPv]
                em.wait("scalar", sPv, cPv)
                em.wait("vector", sPv, cPv)
                if cF_attnh[0] > 0:
                    em.wait("vector", sF, cF_attnh[0])
                for hx in range(2):
                    h8 = heads[hx]
                    pv = pvb[hx]
                    if ihead[0] >= 1:
                        em.wait("scalar", sRbm, em.counts[sRbm])
                    em.emit("scalar", lambda E_, pv=pv: E_.copy(
                        out=lrow[:], in_=pv[64:65, 0:512]), inc=(sRcp, 1))
                    cRc = em.counts[sRcp]
                    em.wait("tensor", sRcp, cRc)
                    if ihead[0] >= 1:
                        em.wait("tensor", sEvh, ihead[0])
                    em.emit("tensor", lambda E_: E_.matmul(
                        b.pb[5][0:64, 0:512], b.cst["c_ones1"][0:1, 0:64], lrow[:],
                        start=True, stop=True), inc=(sRbm, 1))
                    cRb = em.counts[sRbm]
                    em.wait("vector", sRbm, cRb)
                    em.emit("vector", lambda E_: E_.reciprocal(rb[:], b.pb[5][0:64, 0:512]))
                    em.emit("vector", lambda E_, pv=pv, h8=h8: E_.tensor_tensor(
                        out=attnh[:, h8 * 512:(h8 + 1) * 512],
                        in0=pv[0:64, 0:512], in1=rb[:], op=OP.mult), inc=(sEvh, 1))
                    ihead[0] += 1
            em.wait("tensor", sEvh, em.counts[sEvh])
            for dc in range(KC):
                fb = b.pb[6 + dc % 2]
                par = dc % 2
                em.wait("gpsimd", sFe, em.counts[sFe] - 1)
                em.emit("gpsimd", lambda E_, dc=dc, par=par, half=half: E_.dma_start(
                    out=wfb[par][:], in_=wf_h[half, dc]), inc=(sWF, 16))
                cWF = em.counts[sWF]
                em.wait("tensor", sWF, cWF)
                em.wait("tensor", sFe, em.counts[sFe] - 1)
                for h8 in range(8):
                    em.emit("tensor", lambda E_, dc=dc, h8=h8, fb=fb, par=par: E_.matmul(
                        fb[:, 0:512], wfb[par][0:64, h8 * 128:(h8 + 1) * 128],
                        attnh[:, h8 * 512:(h8 + 1) * 512],
                        start=(h8 == 0), stop=(h8 == 7)),
                        inc=(sF, 1) if h8 == 7 else None)
                cF = em.counts[sF]
                em.wait("scalar", sF, cF)
                em.wait("scalar", sV3m, em.counts[sV3m])
                if half == 0:
                    em.emit("scalar", lambda E_, dc=dc, fb=fb: E_.add(
                        out=txf[:], in_=fb[:, 0:512], add=bfc[:, dc:dc + 1]),
                        inc=(sFe, 1))
                else:
                    em.emit("scalar", lambda E_, fb=fb: E_.copy(
                        out=txf[:], in_=fb[:, 0:512]), inc=(sFe, 1))
                cFe = em.counts[sFe]
                em.wait("vector", sFe, cFe)
                em.wait("vector", sSt3[dc % 2], em.counts[sSt3[dc % 2]])
                em.emit("vector", lambda E_, dc=dc: E_.tensor_copy(
                    out=st3[dc % 2][:], in_=txf[:]), inc=(sV3m, 1))
                cV3 = em.counts[sV3m]
                em.wait("gpsimd", sV3m, cV3)
                if first_store[0]:
                    em.wait("gpsimd", em.dch, cpy_d)
                    first_store[0] = False
                em.emit("gpsimd", lambda E_, dc=dc, qt=qt: E_.dma_start(
                    out=x3_hbm[qt, dc],
                    in_=st3[dc % 2][:], accum_op=OP.add), inc=(sSt3[dc % 2], 16))
            cF_attnh[0] = em.counts[sF]
    em.join([(sSt3[0], em.counts[sSt3[0]]), (sSt3[1], em.counts[sSt3[1]])])


def emit_phase_d(b, x3_hbm, x4_hbm, n_cores=8):
    em, nc, T, NTT, NTC = b.em, b.nc, b.T, b.NTT, b.NTC
    pfx = "dd"
    ccap = float(2 * 2 * (T * n_cores) // E)
    wr_h = b.din("L1_ffn_Wg", [D, E])
    w1_h = b.din("L1_ffn_W1", [E, H // 256, 128, KC * 256])
    b1_h = b.din("L1_ffn_b1", [E, H])
    w2_h = b.din("L1_ffn_W2", [E, H, D])
    b2_h = b.din("L1_ffn_b2", [E, D])

    rows = {}
    xn = b.sb("d_xn", [128, KC * T], BF16)
    Mc = b.sb("d_Mc", [8, T], F32)
    b1cols = b.sb("d_b1c", [128, E * HC], F32)
    b2cols = b.sb("d_b2c", [128, E * KC], F32)
    emit_ln_pass(b, pfx, x3_hbm, rows, want_rms=False, xn_sb=xn,
                 xn_rows=("alpha", "beta"))
    em.ser("sync", lambda E_: E_.dma_start(out=x4_hbm[:], in_=x3_hbm[:]), dma=True)

    b.push()
    tmp32 = b.sb("d_tmp32", [32, 128], F32)
    tmp8 = b.sb("d_tmp8", [8, 128], F32)
    for e in range(E):
        em.ser("sync", lambda E_, e=e: E_.dma_start(
            out=tmp32[:], in_=b1_h[e].rearrange("(c p) -> c p", p=128)), dma=True)
        em.ser("tensor", lambda E_: E_.transpose(
            b.pb[6][:, 0:32], tmp32[:], b.cst["c_ident"][0:32, 0:32]))
        em.ser("vector", lambda E_, e=e: E_.tensor_copy(
            out=b1cols[:, e * HC:(e + 1) * HC], in_=b.pb[6][:, 0:32]))
        em.ser("sync", lambda E_, e=e: E_.dma_start(
            out=tmp8[:], in_=b2_h[e].rearrange("(c p) -> c p", p=128)), dma=True)
        em.ser("tensor", lambda E_: E_.transpose(
            b.pb[6][:, 0:8], tmp8[:], b.cst["c_ident"][0:8, 0:8]))
        em.ser("vector", lambda E_, e=e: E_.tensor_copy(
            out=b2cols[:, e * KC:(e + 1) * KC], in_=b.pb[6][:, 0:8]))

    wr_sb = b.sb("d_wr", [128, 8 * 8], F32)
    wrb = b.sb("d_wrb", [128, 8 * 8], BF16)
    logT = b.sb("d_logT", [8, T], F32)
    for fc in range(KC):
        em.ser("sync", lambda E_, fc=fc: E_.dma_start(
            out=wr_sb[:, fc * 8:(fc + 1) * 8], in_=wr_h[fc * 128:(fc + 1) * 128, :]),
            dma=True)
    em.ser("scalar", lambda E_: E_.copy(out=wrb[:], in_=wr_sb[:]))
    for tt in range(NTT):
        for fc in range(KC):
            em.ser("tensor", lambda E_, fc=fc, tt=tt: E_.matmul(
                b.pb[6][0:8, 0:512], wrb[:, fc * 8:(fc + 1) * 8],
                xn[:, fc * T + tt * 512: fc * T + tt * 512 + 512],
                start=(fc == 0), stop=(fc == KC - 1)))
        em.ser("vector", lambda E_, tt=tt: E_.tensor_copy(
            out=logT[0:8, tt * 512:(tt + 1) * 512], in_=b.pb[6][0:8, 0:512]))

    m1T = b.sb("d_m1T", [8, T], BF16)
    m2T = b.sb("d_m2T", [8, T], BF16)
    gT = b.sb("d_gT", [8, T], F32)
    scr96 = b.sb("d_scr96", [128, 96], F32)
    lg = b.sb("d_lg", [128, 8], F32)
    cc = b.sb("d_cc", [128, 8], F32)
    m1 = b.sb("d_m1", [128, 6], F32)
    for tc in range(NTC):
        em.ser("tensor", lambda E_, tc=tc: E_.transpose(
            b.pb[6][:, 0:8], logT[0:8, tc * 128:(tc + 1) * 128],
            b.cst["c_ident"][0:8, 0:8]))
        em.ser("vector", lambda E_: E_.tensor_copy(out=lg[:], in_=b.pb[6][:, 0:8]))
        em.ser("vector", lambda E_: E_.tensor_reduce(
            out=m1[:, 0:1], in_=lg[:], axis=AX, op=OP.max))
        em.ser("vector", lambda E_: E_.tensor_scalar(
            out=cc[:], in0=lg[:], scalar1=m1[:, 0:1], scalar2=None, op0=OP.is_equal))
        em.ser("vector", lambda E_: E_.tensor_mul(cc[:], cc[:], b.cst["c_p2row"][:]))
        em.ser("vector", lambda E_: E_.tensor_reduce(
            out=m1[:, 1:2], in_=cc[:], axis=AX, op=OP.max))
        em.ser("vector", lambda E_: E_.tensor_scalar(
            out=scr96[:, 0:8], in0=cc[:], scalar1=m1[:, 1:2], scalar2=None,
            op0=OP.is_equal))
        em.ser("vector", lambda E_: E_.scalar_tensor_tensor(
            out=cc[:], in0=scr96[:, 0:8], scalar=-1e30, in1=lg[:],
            op0=OP.mult, op1=OP.add))
        em.ser("vector", lambda E_: E_.tensor_reduce(
            out=m1[:, 2:3], in_=cc[:], axis=AX, op=OP.max))
        em.ser("vector", lambda E_: E_.tensor_scalar(
            out=cc[:], in0=cc[:], scalar1=m1[:, 2:3], scalar2=None, op0=OP.is_equal))
        em.ser("vector", lambda E_: E_.tensor_mul(cc[:], cc[:], b.cst["c_p2row"][:]))
        em.ser("vector", lambda E_: E_.tensor_reduce(
            out=m1[:, 3:4], in_=cc[:], axis=AX, op=OP.max))
        em.ser("vector", lambda E_: E_.tensor_scalar(
            out=scr96[:, 32:40], in0=cc[:], scalar1=m1[:, 3:4], scalar2=None,
            op0=OP.is_equal))
        em.ser("vector", lambda E_: E_.tensor_scalar_mul(m1[:, 4:5], m1[:, 0:1], -1.0))
        em.ser("scalar", lambda E_: E_.activation(
            out=scr96[:, 64:72], in_=lg[:], func=AF.Exp, bias=m1[:, 4:5], scale=1.0))
        em.ser("vector", lambda E_: E_.tensor_reduce(
            out=m1[:, 5:6], in_=scr96[:, 64:72], axis=AX, op=OP.add))
        em.ser("vector", lambda E_: E_.reciprocal(m1[:, 5:6], m1[:, 5:6]))
        em.ser("vector", lambda E_: E_.tensor_scalar(
            out=scr96[:, 64:72], in0=scr96[:, 64:72], scalar1=m1[:, 5:6],
            scalar2=None, op0=OP.mult))
        em.ser("tensor", lambda E_, tc=tc: E_.transpose(
            b.pb[6][0:96, 0:128], scr96[:], b.cst["c_ident"][:]))
        em.ser("vector", lambda E_, tc=tc: E_.tensor_copy(
            out=m1T[0:8, tc * 128:(tc + 1) * 128], in_=b.pb[6][0:8, 0:128]))
        em.ser("vector", lambda E_, tc=tc: E_.tensor_copy(
            out=m2T[0:8, tc * 128:(tc + 1) * 128], in_=b.pb[6][32:40, 0:128]))
        em.ser("vector", lambda E_, tc=tc: E_.tensor_copy(
            out=gT[0:8, tc * 128:(tc + 1) * 128], in_=b.pb[6][64:72, 0:128]))

    pos1 = b.sb("d_pos1", [8, T], F32)
    pos2 = b.sb("d_pos2", [8, T], F32)
    for (msk, pos) in ((m1T, pos1), (m2T, pos2)):
        for tt in range(NTT):
            if tt == 0:
                em.ser("vector", lambda E_, msk=msk, pos=pos: E_.tensor_tensor_scan(
                    out=pos[0:8, 0:512], data0=b.cst["c_ones8"][:],
                    data1=msk[0:8, 0:512], initial=0.0, op0=OP.mult, op1=OP.add))
            else:
                em.ser("vector", lambda E_, msk=msk, pos=pos, tt=tt: E_.tensor_tensor_scan(
                    out=pos[0:8, tt * 512:(tt + 1) * 512], data0=b.cst["c_ones8"][:],
                    data1=msk[0:8, tt * 512:(tt + 1) * 512],
                    initial=pos[0:8, tt * 512 - 1:tt * 512], op0=OP.mult, op1=OP.add))

    ccin_sb = b.sb("d_ccin", [1, 16], F32)
    ccall = b.sb("d_ccall", [8, 16], F32)
    ccin_d = nc.dram_tensor("d_ccin_d", [1, 16], F32, kind="Internal")
    ccout_d = nc.dram_tensor("d_ccout_d", [n_cores, 16], F32, kind="Internal",
                             addr_space="Shared" if n_cores > 1 else "Local")
    em.ser("tensor", lambda E_: E_.transpose(
        b.pb[6][0:1, 0:8], pos1[0:8, T - 1:T], b.cst["c_ident"][0:8, 0:8]))
    em.ser("vector", lambda E_: E_.tensor_copy(
        out=ccin_sb[0:1, 0:8], in_=b.pb[6][0:1, 0:8]))
    em.ser("tensor", lambda E_: E_.transpose(
        b.pb[6][0:1, 0:8], pos2[0:8, T - 1:T], b.cst["c_ident"][0:8, 0:8]))
    em.ser("vector", lambda E_: E_.tensor_copy(
        out=ccin_sb[0:1, 8:16], in_=b.pb[6][0:1, 0:8]))
    em.ser("sync", lambda E_: E_.dma_start(out=ccin_d[:], in_=ccin_sb[:]), dma=True)
    sCC = em.sem("d_cc")
    em.wait("gpsimd", em.cch, em.counts[em.cch])
    em.wait("gpsimd", em.dch, em.counts[em.dch])
    em.emit("gpsimd", lambda E_: E_.collective_compute(
        "AllGather", OP.bypass, replica_groups=[list(range(n_cores))],
        ins=[ccin_d[:]], outs=[ccout_d[:]]), inc=(sCC, 1))
    em.join([(sCC, 1)])
    em.ser("vector", lambda E_: E_.memset(ccall[:], 0.0))
    em.ser("sync", lambda E_, n_cores=n_cores: E_.dma_start(
        out=ccall[0:n_cores, :], in_=ccout_d[:]), dma=True)
    rx = b.sb("d_rx", [1, 8 * 4], F32)
    thrc = b.sb("d_thrc", [8, 2], F32)
    em.ser("tensor", lambda E_: E_.matmul(
        b.pb[6][0:1, 0:8], b.cst["c_pref"][:], ccall[0:8, 0:8], start=True, stop=True))
    em.ser("vector", lambda E_: E_.tensor_copy(out=rx[0:1, 0:8], in_=b.pb[6][0:1, 0:8]))
    em.ser("tensor", lambda E_: E_.matmul(
        b.pb[6][0:1, 0:8], b.cst["c_ones8col"][:], ccall[0:8, 0:8],
        start=True, stop=True))
    em.ser("vector", lambda E_: E_.tensor_copy(out=rx[0:1, 8:16], in_=b.pb[6][0:1, 0:8]))
    em.ser("tensor", lambda E_: E_.matmul(
        b.pb[6][0:1, 0:8], b.cst["c_pref"][:], ccall[0:8, 8:16], start=True, stop=True))
    em.ser("vector", lambda E_: E_.tensor_copy(out=rx[0:1, 16:24], in_=b.pb[6][0:1, 0:8]))
    em.ser("vector", lambda E_, ccap=ccap: E_.tensor_scalar(
        out=rx[0:1, 24:32], in0=rx[0:1, 0:8], scalar1=-1.0, scalar2=ccap,
        op0=OP.mult, op1=OP.add))
    row2col(b, rx[0:1, 24:32], thrc[:, 0:1], 8)
    em.ser("vector", lambda E_: E_.tensor_add(rx[0:1, 16:24], rx[0:1, 16:24],
                                              rx[0:1, 8:16]))
    em.ser("vector", lambda E_, ccap=ccap: E_.tensor_scalar(
        out=rx[0:1, 24:32], in0=rx[0:1, 16:24], scalar1=-1.0, scalar2=ccap,
        op0=OP.mult, op1=OP.add))
    row2col(b, rx[0:1, 24:32], thrc[:, 1:2], 8)

    M1K = m1T
    M2K = m2T
    s8a = b.sb("d_s8a", [8, 512], F32)
    s8b = b.sb("d_s8b", [8, 512], F32)
    denrow = b.sb("d_denrow", [1, T], F32)
    for tt in range(NTT):
        for (msk, pos, thr_i, MK, second) in ((m1T, pos1, 0, M1K, False),
                                              (m2T, pos2, 1, M2K, True)):
            sl_ = (slice(0, 8), slice(tt * 512, (tt + 1) * 512))
            em.ser("vector", lambda E_, msk=msk, pos=pos, sl_=sl_: E_.tensor_sub(
                s8a[0:8, 0:512], pos[sl_[0], sl_[1]], msk[sl_[0], sl_[1]]))
            em.ser("vector", lambda E_, thr_i=thr_i: E_.tensor_scalar(
                out=s8a[0:8, 0:512], in0=s8a[0:8, 0:512],
                scalar1=thrc[0:8, thr_i:thr_i + 1], scalar2=None, op0=OP.is_lt))
            em.ser("vector", lambda E_, msk=msk, MK=MK, sl_=sl_: E_.tensor_tensor(
                out=MK[sl_[0], sl_[1]], in0=msk[sl_[0], sl_[1]],
                in1=s8a[0:8, 0:512], op=OP.mult))
            em.ser("vector", lambda E_, msk=msk, sl_=sl_: E_.tensor_tensor(
                out=s8b[0:8, 0:512], in0=msk[sl_[0], sl_[1]],
                in1=gT[sl_[0], sl_[1]], op=OP.mult))
            em.ser("tensor", lambda E_: E_.matmul(
                b.pb[6][0:1, 0:512], b.cst["c_ones8col"][:], s8b[0:8, 0:512],
                start=True, stop=True))
            if not second:
                em.ser("vector", lambda E_, tt=tt: E_.tensor_copy(
                    out=denrow[0:1, tt * 512:(tt + 1) * 512], in_=b.pb[6][0:1, 0:512]))
            else:
                em.ser("vector", lambda E_, tt=tt: E_.tensor_tensor(
                    out=denrow[0:1, tt * 512:(tt + 1) * 512],
                    in0=denrow[0:1, tt * 512:(tt + 1) * 512],
                    in1=b.pb[6][0:1, 0:512], op=OP.add))
    em.ser("vector", lambda E_: E_.tensor_scalar_add(denrow[0:1, 0:T],
                                                     denrow[0:1, 0:T], 1e-9))
    em.ser("vector", lambda E_: E_.reciprocal(denrow[0:1, 0:T], denrow[0:1, 0:T]))
    for tt in range(NTT):
        em.ser("tensor", lambda E_, tt=tt: E_.matmul(
            b.pb[6][0:8, 0:512], b.cst["c_ones1"][0:1, 0:8],
            denrow[0:1, tt * 512:(tt + 1) * 512], start=True, stop=True))
        em.ser("vector", lambda E_, tt=tt: E_.tensor_tensor(
            out=s8a[0:8, 0:512], in0=M1K[0:8, tt * 512:(tt + 1) * 512],
            in1=b.pb[6][0:8, 0:512], op=OP.mult))
        em.ser("vector", lambda E_, tt=tt: E_.tensor_tensor(
            out=s8b[0:8, 0:512], in0=M2K[0:8, tt * 512:(tt + 1) * 512],
            in1=b.pb[6][0:8, 0:512], op=OP.mult))
        em.ser("vector", lambda E_, tt=tt: E_.tensor_tensor(
            out=s8a[0:8, 0:512], in0=s8a[0:8, 0:512],
            in1=gT[0:8, tt * 512:(tt + 1) * 512], op=OP.mult))
        em.ser("vector", lambda E_, tt=tt: E_.tensor_tensor(
            out=s8b[0:8, 0:512], in0=s8b[0:8, 0:512],
            in1=gT[0:8, tt * 512:(tt + 1) * 512], op=OP.mult))
        em.ser("vector", lambda E_, tt=tt: E_.tensor_add(
            Mc[0:8, tt * 512:(tt + 1) * 512], s8a[0:8, 0:512], s8b[0:8, 0:512]))
    em.join([(em.cch, em.counts[em.cch]), (em.dch, em.counts[em.dch])])
    b.pop()

    b.push()
    emit_moe_loop(b, "d", xn, Mc, x4_hbm, w1_h, None, w2_h, b1cols, b2cols,
                  silu_glu=False)
    b.pop()


def _phase_scope(b):
    return b.stack.enter_context(contextlib.ExitStack())


def build_program(T, phases, n_cores=NCORES, verbose=False):
    nc = bass.Bass("TRN2", target_bir_lowering=False)
    ext = {}
    with contextlib.ExitStack() as stack:
        em = Em(nc, stack)
        b = B(nc, stack, em, T)
        NTT = T // 512
        xshape = [NTT, KC, 128, 512]
        x_in = nc.dram_tensor("x_in", xshape, F32, kind="ExternalInput")
        b.ext["x_in"] = tuple(xshape)
        x_out = nc.dram_tensor("x_out", xshape, F32, kind="ExternalOutput")
        n = len(phases)
        bufs = [x_in]
        for i in range(n - 1):
            bufs.append(nc.dram_tensor(f"xmid{i}", xshape, F32, kind="Internal"))
        bufs.append(x_out)
        emitters = {"A": emit_phase_a, "B": emit_phase_b, "C": emit_phase_c,
                    "D": emit_phase_d}
        with nc.Block() as blk:
            setup_common(b)
            outer_stack = b.stack
            for i, p in enumerate(phases):
                sem_snapshot = dict(em.counts)
                phase_stack = contextlib.ExitStack()
                b.stack = phase_stack
                if p == "D":
                    emit_phase_d(b, bufs[i], bufs[i + 1], n_cores=n_cores)
                else:
                    emitters[p](b, bufs[i], bufs[i + 1])
                deps = [(s, c) for s, c in em.counts.items()
                        if c > sem_snapshot.get(s, 0)]
                em.join(deps)
                em.release_phase_sems()
                em.join([(em.cch, em.counts[em.cch])])
                b.stack = outer_stack
                phase_stack.close()
                if verbose:
                    print(f"phase {p}: sbuf_remaining={nc.sbuf_bytes_remaining} "
                          f"n_inst={nc.n_instructions()}")
            em.finalize(blk)
        ext = dict(b.ext)
    return nc, ext


def tile_updown(W):
    W = np.ascontiguousarray(W)
    Eo, Do, Ho = W.shape
    return np.ascontiguousarray(
        W.reshape(Eo, KC, 128, Ho // 256, 256).transpose(0, 3, 2, 1, 4)
        .reshape(Eo, Ho // 256, 128, KC * 256))


def tile_wf(W):
    W = np.ascontiguousarray(W)
    return np.ascontiguousarray(
        W.reshape(2, 8, 64, KC, 128).transpose(0, 3, 2, 1, 4)
        .reshape(2, KC, 64, 8 * 128))


def tile_x(xTD):
    T_, D_ = xTD.shape
    xT = np.ascontiguousarray(xTD.T)
    return np.ascontiguousarray(
        xT.reshape(KC, 128, T_ // 512, 512).transpose(2, 0, 1, 3))


def untile_x(xt):
    NTT_ = xt.shape[0]
    return np.ascontiguousarray(
        xt.transpose(1, 2, 0, 3).reshape(D, NTT_ * 512).T)


TILE_TRANSFORMS = {
    "L0_ffn_Wg": tile_updown,
    "L0_ffn_Wu": tile_updown,
    "L1_ffn_W1": tile_updown,
    "L1_attn_Wf": tile_wf,
}


def _shard_inputs(inputs, T, phases, n_cores=NCORES):
    w = dict(inputs)
    maps = []
    x0 = np.asarray(w["L0_attn_inputs"])
    need = set()
    for p in phases:
        need |= {
            "A": {"L0_attn_norm_weight", "L0_attn_router_w", "L0_attn_router_b",
                  "L0_attn_w_hg", "L0_attn_w_out"},
            "B": {"L0_ffn_Wr", "L0_ffn_Wg", "L0_ffn_Wu", "L0_ffn_Wd"},
            "C": {"L1_attn_Wq", "L1_attn_bq", "L1_attn_Wk", "L1_attn_bk",
                  "L1_attn_Wv", "L1_attn_bv", "L1_attn_Wf", "L1_attn_bf"},
            "D": {"L1_ffn_Wg", "L1_ffn_W1", "L1_ffn_b1", "L1_ffn_W2", "L1_ffn_b2"},
        }[p]
    reshape = {"L0_attn_router_b": (1, E)}
    for c in range(n_cores):
        m = {}
        for k, v in host_constants(c).items():
            m[k] = v
        for name in need:
            arr = np.ascontiguousarray(np.asarray(w[name]), dtype=np.float32)
            if name in reshape:
                arr = arr.reshape(reshape[name])
            if name in TILE_TRANSFORMS:
                arr = TILE_TRANSFORMS[name](arr)
            m[name] = arr
        m["x_in"] = tile_x(np.asarray(x0[c]))
        maps.append(m)
    return maps


_NC_CACHE = {}


def kernel(**inputs):
    T = int(np.asarray(inputs["L0_attn_inputs"]).shape[1])
    phases = ("A", "B", "C", "D")
    key = (T, phases)
    if key not in _NC_CACHE:
        _NC_CACHE[key] = build_program(T, phases)
    nc, ext = _NC_CACHE[key]
    maps = _shard_inputs(inputs, T, phases)
    res = run_bass_kernel_spmd(nc, maps, core_ids=list(range(NCORES)))
    out = np.stack([untile_x(r["x_out"].reshape(T // 512, KC, 128, 512))
                    for r in res.results])
    return out.astype(np.float32)


# version 14 (final)
# speedup vs baseline: 1.0249x; 1.0161x over previous
import contextlib
import numpy as np
import ml_dtypes
import concourse.bass as bass
import concourse.mybir as mybir
from concourse.bass_utils import run_bass_kernel_spmd

F32 = mybir.dt.float32
BF16 = mybir.dt.bfloat16
AX = mybir.AxisListType.X
OP = mybir.AluOpType
AF = mybir.ActivationFunctionType

D = 1024
KC = D // 128
E = 8
H = 4096
HC = H // 128
NHEADS = 16
HDIM = 64
DEPTH_SCALE = float(1.0 / np.sqrt(12.0))
LN_EPS = 1e-5
RMS_EPS = 1e-6
NEG = -1e9
CCAP = 16384
NCORES = 8


def alibi_slopes():
    half = NHEADS // 2
    base = 24.0 ** (1.0 / half)
    return (1.0 / base ** np.arange(1, half + 1, dtype=np.float64)).astype(np.float64)


class Em:
    ENGINES = ("sync", "tensor", "vector", "scalar", "gpsimd")

    CHAIN = ("vector", "scalar")

    def __init__(self, nc, stack):
        self.nc = nc
        self.streams = {e: [] for e in self.ENGINES}
        self.stack = stack
        self.sems = {}
        self.counts = {}
        self.cch = self.sem("cch")
        self.dch = self.sem("dch")
        self.spd = self.sem("spd")
        self.selfsem = {e: self.sem(f"self_{e}") for e in self.CHAIN}
        self.last = {}

    def sem(self, name):
        assert name not in self.sems, name
        h = self.nc.alloc_semaphore(name=name)
        self.sems[name] = h
        if not hasattr(self, "phys_count"):
            self.phys_count = {}
        self.counts[name] = self.phys_count.get(h.num, 0)
        return name

    def release_phase_sems(self, keep=()):
        keepset = (set(keep) | {self.cch, self.dch, self.spd}
                   | set(self.selfsem.values()))
        handles = []
        for name in list(self.sems.keys()):
            if name in keepset:
                continue
            h = self.sems[name]
            handles.append(h)
            if not hasattr(self, "phys_count"):
                self.phys_count = {}
            self.phys_count[h.num] = 0
            del self.sems[name]
            del self.counts[name]
        fence = self.fence_t
        spd_h = self.sems[self.spd]
        self.streams["tensor"].append(lambda E_, s=spd_h: E_.nop().then_inc(s, 1))
        self.streams["vector"].append(lambda E_, s=spd_h: E_.engine_nop().then_inc(s, 1))
        self.streams["scalar"].append(
            lambda E_, s=spd_h, f=fence: E_.copy(out=f[:], in_=f[:]).then_inc(s, 1))
        self.streams["sync"].append(lambda E_, s=spd_h: E_.nop().then_inc(s, 1))
        self.counts[self.spd] += 4
        self.last["vector"] = (self.spd, self.counts[self.spd])
        self.last["scalar"] = (self.spd, self.counts[self.spd])
        self.wait("gpsimd", self.spd, self.counts[self.spd])
        self.wait("gpsimd", self.cch, self.counts[self.cch])
        self.wait("gpsimd", self.dch, self.counts[self.dch])
        for h in handles:
            self.streams["gpsimd"].append(
                lambda E_, h=h: E_.dma_reset(semaphore_range=range(h.num, h.num + 1)))
            self.streams["gpsimd"].append(
                lambda E_, h=h: E_.sem_clear(h))
            self.nc.release_semaphore(h)

        def bump(E_, s=self.sems[self.cch]):
            E_.memset(fence[:], 0.0).then_inc(s, 1)
        self.streams["gpsimd"].append(bump)
        self.counts[self.cch] += 1
        for eng in list(self.last.keys()):
            if self.last[eng] is not None and self.last[eng][0] not in self.sems:
                self.last[eng] = None

    def wait(self, eng, sem, val):
        h = self.sems[sem]
        if val > 0:
            self.streams[eng].append(lambda E_, h=h, v=val: E_.wait_ge(h, v))

    def emit(self, eng, fn, inc=None):
        chained = eng in self.CHAIN
        if chained:
            prev = self.last.get(eng)
            if prev is not None:
                self.wait(eng, prev[0], prev[1])
            if inc is None:
                inc = (self.selfsem[eng], 1)
        if inc is None:
            self.streams[eng].append(fn)
        else:
            s, n = inc
            h = self.sems[s]
            self.streams[eng].append(lambda E_, fn=fn, h=h, n=n: fn(E_).then_inc(h, n))
            self.counts[s] += n
            if chained:
                self.last[eng] = (s, self.counts[s])

    def ser(self, eng, fn, dma=False):
        self.wait(eng, self.cch, self.counts[self.cch])
        self.wait(eng, self.dch, self.counts[self.dch])
        self.emit(eng, fn, inc=(self.dch, 16) if dma else (self.cch, 1))

    def barrier(self):
        deps = [(self.cch, self.counts[self.cch]), (self.dch, self.counts[self.dch])]
        for s in self.selfsem.values():
            deps.append((s, self.counts[s]))
        self.join(deps)

    def join(self, deps):
        for eng in self.ENGINES:
            for s, v in deps:
                self.wait(eng, s, v)

    def finalize(self, blk):
        for name in self.ENGINES:
            ops = self.streams[name]

            def body(E_, ops=ops):
                for op in ops:
                    op(E_)
            getattr(blk, name)(body)


def host_constants(core_id):
    c = {}
    c["c_ident"] = np.eye(128, dtype=np.float32)
    c["c_ones_col"] = np.ones((128, 1), np.float32)
    c["c_ones1"] = np.ones((1, 128), np.float32)
    c["c_ones11"] = np.ones((1, 1), np.float32)
    oneh = np.zeros((8, 8 * 128), np.float32)
    for e in range(8):
        oneh[e, e * 128:(e + 1) * 128] = 1.0
    c["c_oneh"] = oneh
    c["c_p2row"] = np.broadcast_to((2.0 ** -np.arange(8, dtype=np.float32))[None, :],
                                   (128, 8)).copy()
    c["c_ones8"] = np.ones((8, 512), np.float32)
    i = np.arange(128)[:, None].astype(np.float32)
    jj = np.arange(512)[None, :].astype(np.float32)
    c["c_rel_kq512"] = (i - jj).astype(np.float32)
    for bb in range(4):
        rel = (i - jj + 128.0 * bb).astype(np.float32)
        vf = (i + 128.0 * bb) <= jj
        c[f"c_diag_f{bb}"] = np.where(vf, rel, -1e10).astype(np.float32)
        vb = (i + 128.0 * bb) >= jj
        c[f"c_diag_b{bb}"] = np.where(vb, rel, 1e10).astype(np.float32)
    doff = np.broadcast_to(128.0 * (np.arange(64, dtype=np.float32) - 32.0),
                           (128, 64)).copy()
    c["c_doff"] = doff
    c["c_ones64"] = np.ones((128, 64), np.float32)
    ii = np.arange(128, dtype=np.float32)[:, None]
    dd = np.arange(64, dtype=np.float32)[None, :]
    c["c_doff2"] = (ii + 128.0 * (dd - 32.0) - 256.0).astype(np.float32)
    bones2 = np.zeros((128, 2), np.float32)
    bones2[0:64, 0] = 1.0
    bones2[64:128, 1] = 1.0
    c["c_bones2"] = bones2
    pref = np.zeros((8, 1), np.float32)
    pref[:core_id, 0] = 1.0
    c["c_pref"] = pref
    c["c_ones8col"] = np.ones((8, 1), np.float32)
    return c


CONST_SPECS = {k: v.shape for k, v in host_constants(0).items()}


def xap(h, fc, tt, nf=1):
    return h[tt, fc]


class B:
    def __init__(self, nc, stack, em, T):
        self.nc = nc
        self.stack = stack
        self.em = em
        self.T = T
        self.NTT = T // 512
        self.NTC = T // 128
        self.cst = {}
        self.pb = []
        self.ext = {}

    def sb(self, name, shape, dtype):
        return self.stack.enter_context(self.nc.sbuf_tensor(name, shape, dtype))

    def push(self):
        self._saved = getattr(self, "_saved", [])
        self._saved.append(self.stack)
        self.stack = contextlib.ExitStack()
        return self.stack

    def pop(self):
        self.stack.close()
        self.stack = self._saved.pop()

    def din(self, name, shape):
        h = self.nc.dram_tensor(name, list(shape), F32, kind="ExternalInput")
        self.ext[name] = tuple(shape)
        return h


def setup_common(b):
    for name, shape in CONST_SPECS.items():
        if name.startswith("c_diag_") or name in ("c_rel_kq512", "c_ones64"):
            continue
        h = b.din(name, shape)
        t = b.sb(name + "_sb", list(shape), F32)
        b.cst[name] = t
        b.em.ser("sync", lambda E_, t=t, h=h: E_.dma_start(out=t[:], in_=h[:]), dma=True)
    b.pb = [b.stack.enter_context(b.nc.psum_tensor(f"pb{i}", [128, 512], F32))
            for i in range(8)]
    b.em.fence_t = b.sb("fence_t", [1, 1], F32)
    b.em.ser("vector", lambda E_: E_.memset(b.em.fence_t[:], 0.0))
    oc_b = b.sb("c_ones_colb", [128, 1], BF16)
    b.em.ser("scalar", lambda E_, oc_b=oc_b: E_.copy(
        out=oc_b[:], in_=b.cst["c_ones_col"][:]))
    b.cst["c_ones_colb"] = oc_b
    o1_b = b.sb("c_ones1b", [1, 128], BF16)
    b.em.ser("scalar", lambda E_, o1_b=o1_b: E_.copy(
        out=o1_b[:], in_=b.cst["c_ones1"][:]))
    b.cst["c_ones1b"] = o1_b


def vec2cols(b, vec_h, n, dst_sb, tmp_name):
    em = b.em
    nch = n // 128
    tmp = b.sb(tmp_name, [nch, 128], F32)
    em.ser("sync", lambda E_: E_.dma_start(
        out=tmp[:], in_=vec_h.ap().rearrange("(c p) -> c p", p=128)), dma=True)
    em.ser("tensor", lambda E_: E_.transpose(
        b.pb[6][:, 0:nch], tmp[:], b.cst["c_ident"][0:nch, 0:nch]))
    em.ser("vector", lambda E_: E_.tensor_copy(out=dst_sb[:, 0:nch], in_=b.pb[6][:, 0:nch]))


def row2col(b, row_ap, col_ap, n):
    em = b.em
    em.ser("tensor", lambda E_: E_.matmul(
        b.pb[6][0:n, 0:1], row_ap, b.cst["c_ones11"][:], start=True, stop=True))
    em.ser("vector", lambda E_: E_.tensor_copy(out=col_ap, in_=b.pb[6][0:n, 0:1]))


def col2row(b, col_ap, row_ap, n):
    em = b.em
    em.ser("tensor", lambda E_: E_.transpose(
        b.pb[6][0:1, 0:n], col_ap, b.cst["c_ident"][0:n, 0:n]))
    em.ser("vector", lambda E_: E_.tensor_copy(out=row_ap, in_=b.pb[6][0:1, 0:n]))


def alloc_rows(b, pfx, names, persist=(), persist_dtype=F32):
    return {n: b.sb(f"{pfx}row_{n}", [1, b.T] if n in persist else [1, 512],
                    persist_dtype if n in persist else F32)
            for n in names}


def emit_ln_pass(b, pfx, x_hbm, rows, want_rms=False, xn_sb=None, xn_rows=None):
    em = b.em
    T, NTT = b.T, b.NTT
    b.push()
    need = ["alpha", "beta", "a1", "t1", "t2", "t3", "t4"]
    if want_rms:
        need += ["gamma", "delta"]
    for nm in need:
        if nm not in rows:
            rows[nm] = b.sb(f"{pfx}row_{nm}", [1, 512], F32)
    xr = [b.sb(f"{pfx}xr{i}", [128, 512], F32) for i in range(KC)]
    sq = [b.sb(f"{pfx}sq{i}", [128, 512], F32) for i in range(2)]
    xs = [b.sb(f"{pfx}xs{i}", [128, 512], F32) for i in range(2)]
    sL = [em.sem(f"{pfx}L{i}") for i in range(KC)]
    sSq = em.sem(pfx + "Sq")
    sMM = em.sem(pfx + "MM")
    sRw = em.sem(pfx + "Rw")
    sSx = em.sem(pfx + "Sx")
    sBC = em.sem(pfx + "BC")
    sXN = em.sem(pfx + "XN")
    rpt = 2 if want_rms else 1

    def r(name, tt):
        buf = rows[name]
        if buf.shape[1] == T:
            return buf[0:1, tt * 512:(tt + 1) * 512]
        return buf[0:1, 0:512]

    for tt in range(NTT):
        for fc in range(KC):
            i = tt * KC + fc
            if tt >= 1:
                if xn_sb is not None:
                    em.wait("sync", sXN, (tt - 1) * KC + fc + 1)
                else:
                    em.wait("sync", sMM, (tt - 1) * KC + fc + 1)
            em.emit("sync", lambda E_, fc=fc, tt=tt: E_.dma_start(
                out=xr[fc][:], in_=x_hbm[tt, fc]),
                inc=(sL[fc], 16))
            em.wait("scalar", sL[fc], (tt + 1) * 16)
            if i >= 2:
                em.wait("scalar", sMM, i - 1)
            em.emit("scalar", lambda E_, fc=fc, i=i: E_.square(
                out=sq[i % 2][:], in_=xr[fc][:]), inc=(sSq, 1))
            em.wait("tensor", sSq, i + 1)
            em.wait("tensor", sL[fc], (tt + 1) * 16)
            if tt >= 1 and fc == 0:
                em.wait("tensor", sRw, (tt - 1) * rpt + 1)
            em.emit("tensor", lambda E_, fc=fc: E_.matmul(
                b.pb[0][0:1, 0:512], b.cst["c_ones_col"][:], xr[fc][:],
                start=(fc == 0), stop=(fc == KC - 1)))
            em.emit("tensor", lambda E_, fc=fc, i=i: E_.matmul(
                b.pb[1][0:1, 0:512], b.cst["c_ones_col"][:], sq[i % 2][:],
                start=(fc == 0), stop=(fc == KC - 1)), inc=(sMM, 1))
        em.wait("vector", sMM, (tt + 1) * KC)
        t1 = rows["t1"][0:1, 0:512]
        t2 = rows["t2"][0:1, 0:512]
        t3 = rows["t3"][0:1, 0:512]
        t4 = rows["t4"][0:1, 0:512]
        ral, rbe, ra1 = r("alpha", tt), r("beta", tt), r("a1", tt)

        def v(fn):
            em.emit("vector", fn)

        v(lambda E_: E_.tensor_scalar_mul(t1, b.pb[0][0:1, 0:512], 1.0 / D))
        v(lambda E_: E_.tensor_scalar_mul(t2, b.pb[1][0:1, 0:512], 1.0 / D))
        v(lambda E_: E_.tensor_mul(t3, t1, t1))
        v(lambda E_: E_.tensor_sub(t3, t2, t3))
        em.emit("vector", lambda E_: E_.tensor_scalar_add(t3, t3, LN_EPS),
                inc=(sRw, 1))
        em.wait("scalar", sRw, tt * rpt + 1)
        em.emit("scalar", lambda E_: E_.sqrt(out=t4, in_=t3), inc=(sSx, 1))
        em.wait("vector", sSx, tt * rpt + 1)
        v(lambda E_: E_.reciprocal(t2, t4))
        v(lambda E_, ral=ral: E_.tensor_copy(out=ral, in_=t2))
        v(lambda E_: E_.tensor_mul(t3, t1, t2))
        v(lambda E_: E_.tensor_scalar_mul(t3, t3, -1.0))
        v(lambda E_, rbe=rbe: E_.tensor_copy(out=rbe, in_=t3))
        v(lambda E_, ra1=ra1: E_.tensor_scalar_add(ra1, t2, 1.0))
        if want_rms:
            rga, rde = r("gamma", tt), r("delta", tt)
            v(lambda E_: E_.tensor_mul(t1, t4, t4))
            v(lambda E_: E_.tensor_scalar_add(t1, t1, -LN_EPS))
            v(lambda E_: E_.tensor_mul(t1, t1, t2))
            v(lambda E_: E_.tensor_mul(t1, t1, t2))
            em.emit("vector", lambda E_: E_.tensor_scalar_add(t1, t1, RMS_EPS),
                    inc=(sRw, 1))
            em.wait("scalar", sRw, tt * rpt + 2)
            em.emit("scalar", lambda E_: E_.sqrt(out=t4, in_=t1), inc=(sSx, 1))
            em.wait("vector", sSx, tt * rpt + 2)
            v(lambda E_: E_.reciprocal(t1, t4))
            v(lambda E_, rga=rga: E_.tensor_mul(rga, t2, t1))
            v(lambda E_, rde=rde: E_.tensor_mul(rde, t3, t1))
        if xn_sb is not None:
            gname, dname = xn_rows
            rg, rd = r(gname, tt), r(dname, tt)
            em.emit("vector", lambda E_: E_.engine_nop(), inc=(sBC, 1))
            em.wait("tensor", sBC, 2 * tt + 1)
            if tt >= 1:
                em.wait("tensor", sXN, tt * KC)
            em.emit("tensor", lambda E_, rg=rg: E_.matmul(
                b.pb[2][:, 0:512], b.cst["c_ones1"][:], rg, start=True, stop=True))
            em.emit("tensor", lambda E_, rd=rd: E_.matmul(
                b.pb[3][:, 0:512], b.cst["c_ones1"][:], rd, start=True, stop=True),
                inc=(sBC, 1))
            for fc in range(KC):
                i = tt * KC + fc
                em.wait("vector", sBC, 2 * tt + 2)
                if i >= 2:
                    em.wait("vector", sXN, i - 1)
                em.emit("vector", lambda E_, fc=fc, i=i: E_.tensor_tensor(
                    out=xs[i % 2][:], in0=xr[fc][:], in1=b.pb[2][:, 0:512], op=OP.mult))
                em.emit("vector", lambda E_, fc=fc, tt=tt, i=i: E_.tensor_tensor(
                    out=xn_sb[:, fc * T + tt * 512: fc * T + tt * 512 + 512],
                    in0=xs[i % 2][:], in1=b.pb[3][:, 0:512], op=OP.add),
                    inc=(sXN, 1))
    deps = [(sRw, em.counts[sRw]), (sMM, em.counts[sMM]), (sSx, em.counts[sSx])]
    if xn_sb is not None:
        deps.append((sXN, em.counts[sXN]))
    for s in sL:
        deps.append((s, em.counts[s]))
    for s in em.selfsem.values():
        deps.append((s, em.counts[s]))
    em.join(deps)
    b.pop()


def emit_phase_a(b, x0_hbm, x1_hbm):
    em, nc, T, NTT = b.em, b.nc, b.T, b.NTT
    pfx = "a"
    whg_h = b.din("L0_attn_w_hg", [E, 2 * D, D])
    wout_h = b.din("L0_attn_w_out", [E, D, D])
    nw_h = b.din("L0_attn_norm_weight", [D])
    rw_h = b.din("L0_attn_router_w", [E, D])
    rb_h = b.din("L0_attn_router_b", [1, E])

    rows = alloc_rows(b, pfx, ["alpha", "beta"], persist=("beta", "alpha"))
    xrms = b.sb("a_xrms", [128, KC * T], BF16)
    whgT = b.sb("a_whgT", [128, KC * 2048], BF16)
    woutT = b.sb("a_woutT", [128, KC * 1024], BF16)
    pcols = b.sb("a_pcols", [128, 8], F32)
    nwcols = b.sb("a_nwcols", [128, 8], F32)
    meanv = b.sb("a_meanv", [128, 8], F32)

    emit_ln_pass(b, pfx, x0_hbm, rows, want_rms=True, xn_sb=xrms,
                 xn_rows=("gamma", "delta"))

    b.push()
    scI = b.sb("a_scI", [128, 8 * 128], BF16)
    r8a = b.sb("a_r8a", [1, 8], F32)
    r8b = b.sb("a_r8b", [1, 8], F32)
    rw_sb = b.sb("a_rw", [8, D], F32)
    rwT = b.sb("a_rwT", [128, 64], F32)
    rbrow = b.sb("a_rb", [1, 8], F32)
    wring = [b.sb(f"a_wr{i}", [128, D], BF16) for i in range(8)]

    vec2cols(b, nw_h, D, nwcols, "a_nwtmp")
    em.ser("sync", lambda E_: E_.dma_start(out=rw_sb[:], in_=rw_h[:]), dma=True)
    em.ser("sync", lambda E_: E_.dma_start(out=rbrow[:], in_=rb_h[:]), dma=True)
    for fc in range(KC):
        em.ser("tensor", lambda E_, fc=fc: E_.transpose(
            b.pb[6][:, 0:8], rw_sb[:, fc * 128:(fc + 1) * 128],
            b.cst["c_ident"][0:8, 0:8]))
        em.ser("vector", lambda E_, fc=fc: E_.tensor_copy(
            out=rwT[:, fc * 8:(fc + 1) * 8], in_=b.pb[6][:, 0:8]))
    for fc in range(KC):
        em.ser("vector", lambda E_, fc=fc: E_.tensor_reduce(
            out=meanv[:, fc:fc + 1], in_=xrms[:, fc * T:(fc + 1) * T],
            axis=AX, op=OP.add))
        em.ser("vector", lambda E_, fc=fc: E_.tensor_scalar(
            out=meanv[:, fc:fc + 1], in0=meanv[:, fc:fc + 1],
            scalar1=nwcols[:, fc:fc + 1], scalar2=1.0 / T, op0=OP.mult, op1=OP.mult))
    for fc in range(KC):
        em.ser("tensor", lambda E_, fc=fc: E_.matmul(
            b.pb[6][0:1, 0:8], meanv[:, fc:fc + 1], rwT[:, fc * 8:(fc + 1) * 8],
            start=(fc == 0), stop=(fc == KC - 1)))
    em.ser("vector", lambda E_: E_.tensor_add(r8a[:], b.pb[6][0:1, 0:8], rbrow[:]))
    em.ser("vector", lambda E_: E_.tensor_reduce(
        out=r8b[0:1, 0:1], in_=r8a[:], axis=AX, op=OP.max))
    em.ser("vector", lambda E_: E_.tensor_scalar_mul(r8b[0:1, 1:2], r8b[0:1, 0:1], -1.0))
    em.ser("scalar", lambda E_: E_.activation(
        out=r8a[:], in_=r8a[:], func=AF.Exp, bias=r8b[0:1, 1:2], scale=1.0))
    em.ser("vector", lambda E_: E_.tensor_reduce(
        out=r8b[0:1, 2:3], in_=r8a[:], axis=AX, op=OP.add))
    em.ser("vector", lambda E_: E_.reciprocal(r8b[0:1, 3:4], r8b[0:1, 2:3]))
    em.ser("vector", lambda E_: E_.tensor_scalar(
        out=r8a[:], in0=r8a[:], scalar1=r8b[0:1, 3:4], scalar2=None, op0=OP.mult))
    em.ser("tensor", lambda E_: E_.matmul(
        b.pb[6][:, 0:8], b.cst["c_ones1"][:], r8a[:], start=True, stop=True))
    em.ser("vector", lambda E_: E_.tensor_copy(out=pcols[:], in_=b.pb[6][:, 0:8]))
    for e in range(E):
        em.ser("scalar", lambda E_, e=e: E_.activation(
            out=scI[:, e * 128:(e + 1) * 128], in_=b.cst["c_ident"][:],
            func=AF.Copy, bias=0.0, scale=pcols[:, e:e + 1]))

    sWL = [em.sem(f"a_wl{i}") for i in range(8)]
    sMix = em.sem("a_mix")
    sMixEv = em.sem("a_mixev")

    state = {"g": 0, "oc_total": 0}

    def mix(which, w_h, dst):
        outer = 16 if which == 0 else 8
        for oc in range(outer):
            oci = state["oc_total"]
            for e in range(E):
                if oci >= 1:
                    em.wait("gpsimd", sMix, oci * KC)
                em.emit("gpsimd", lambda E_, oc=oc, e=e, w_h=w_h: E_.dma_start(
                    out=wring[e][:], in_=w_h[e, oc * 128:(oc + 1) * 128, :]),
                    inc=(sWL[e], 16))
            for fc in range(KC):
                g = state["g"]
                bank = b.pb[4 + fc % 2]
                if g >= 2:
                    em.wait("tensor", sMixEv, g - 1)
                for e in range(E):
                    if fc == 0:
                        em.wait("tensor", sWL[e], (oci + 1) * 16)
                    em.emit("tensor", lambda E_, e=e, fc=fc, bank=bank: E_.matmul(
                        bank[:, 0:128],
                        wring[e][:, fc * 128:(fc + 1) * 128],
                        scI[:, e * 128:(e + 1) * 128],
                        start=(e == 0), stop=(e == E - 1)),
                        inc=(sMix, 1) if e == E - 1 else None)
                em.wait("vector", sMix, g + 1)
                if which == 0:
                    dstap = dst[:, fc * 2048 + oc * 128: fc * 2048 + oc * 128 + 128]
                    em.emit("vector", lambda E_, dstap=dstap, bank=bank, fc=fc: E_.tensor_scalar(
                        out=dstap, in0=bank[:, 0:128], scalar1=nwcols[:, fc:fc + 1],
                        scalar2=None, op0=OP.mult), inc=(sMixEv, 1))
                else:
                    dstap = dst[:, fc * 1024 + oc * 128: fc * 1024 + oc * 128 + 128]
                    em.emit("vector", lambda E_, dstap=dstap, bank=bank: E_.tensor_copy(
                        out=dstap, in_=bank[:, 0:128]), inc=(sMixEv, 1))
                state["g"] += 1
            state["oc_total"] += 1

    em.join([(em.cch, em.counts[em.cch]), (em.dch, em.counts[em.dch])])
    mix(0, whg_h, whgT)
    mix(1, wout_h, woutT)
    em.join([(sMixEv, em.counts[sMixEv])] +
            [(s, em.counts[s]) for s in sWL])
    em.barrier()
    b.pop()

    b.push()
    hidS = [b.sb(f"a_hidS{i}", [128, 512], F32) for i in range(2)]
    gatS = [b.sb(f"a_gatS{i}", [128, 512], F32) for i in range(2)]
    sgS = [b.sb(f"a_sgS{i}", [128, 512], F32) for i in range(2)]
    mskS = b.sb("a_mskS", [128, 512], F32)
    tS = b.sb("a_tS", [128, 512], F32)
    aT = [b.sb(f"a_aT{i}", [128, 512], F32) for i in range(2)]
    wT = b.sb("a_wT", [128, 512], F32)
    carry = b.sb("a_carry", [128, 8], F32)
    hbf = b.sb("a_hbf", [128, 8 * 512], BF16)
    a1bS = b.sb("a_a1b", [128, 512], F32)
    bbS = b.sb("a_bb", [128, 512], F32)
    x0r = [b.sb(f"a_x0r{i}", [128, 512], F32) for i in range(2)]
    x1o = [b.sb(f"a_x1o{i}", [128, 512], F32) for i in range(2)]

    sHid = em.sem("a_hid")
    sGat = em.sem("a_gat")
    sScl = em.sem("a_scl")
    sVec = em.sem("a_vec")
    sBc2 = em.sem("a_bc2")
    sBcC = em.sem("a_bcc")
    sOutM = em.sem("a_outm")
    sEv = em.sem("a_ev")
    sX0 = [em.sem(f"a_x0{i}") for i in range(2)]
    sSt = [em.sem(f"a_st{i}") for i in range(2)]

    for tt in range(NTT):
        for pair in range(8):
            pi = tt * 8 + pair
            hb = b.pb[pair % 2]
            gb = b.pb[2 + pair % 2]
            if pi >= 2:
                em.wait("tensor", sScl, pi - 1)
            if pair == 0 and tt >= 1:
                em.wait("tensor", sEv, tt * 8)
            for fc in range(KC):
                em.emit("tensor", lambda E_, fc=fc, pair=pair, tt=tt, hb=hb: E_.matmul(
                    hb[:, 0:512],
                    whgT[:, fc * 2048 + pair * 128: fc * 2048 + pair * 128 + 128],
                    xrms[:, fc * T + tt * 512: fc * T + tt * 512 + 512],
                    start=(fc == 0), stop=(fc == KC - 1)),
                    inc=(sHid, 1) if fc == KC - 1 else None)
            for fc in range(KC):
                em.emit("tensor", lambda E_, fc=fc, pair=pair, tt=tt, gb=gb: E_.matmul(
                    gb[:, 0:512],
                    whgT[:, fc * 2048 + 1024 + pair * 128: fc * 2048 + 1024 + pair * 128 + 128],
                    xrms[:, fc * T + tt * 512: fc * T + tt * 512 + 512],
                    start=(fc == 0), stop=(fc == KC - 1)),
                    inc=(sGat, 1) if fc == KC - 1 else None)
            sl2 = pi % 2
            em.wait("scalar", sHid, pi + 1)
            if pi >= 2:
                em.wait("scalar", sVec, pi - 1)
            em.emit("scalar", lambda E_, hb=hb, sl2=sl2: E_.copy(
                out=hidS[sl2][:], in_=hb[:, 0:512]))
            em.wait("scalar", sGat, pi + 1)
            em.emit("scalar", lambda E_, gb=gb, sl2=sl2: E_.copy(
                out=gatS[sl2][:], in_=gb[:, 0:512]))
            em.emit("scalar", lambda E_, sl2=sl2: E_.activation(
                out=aT[sl2][:], in_=gatS[sl2][:], func=AF.Sigmoid, bias=0.0, scale=-1.0))
            em.emit("scalar", lambda E_, sl2=sl2: E_.activation(
                out=sgS[sl2][:], in_=hidS[sl2][:], func=AF.Sigmoid), inc=(sScl, 1))
            em.wait("vector", sScl, pi + 1)
            if pair == 0 and tt >= 1:
                em.wait("vector", sOutM, tt * 8)
            em.emit("vector", lambda E_, sl2=sl2: E_.tensor_single_scalar(
                out=mskS[:], in_=hidS[sl2][:], scalar=0.0, op=OP.is_ge))
            em.emit("vector", lambda E_, sl2=sl2: E_.scalar_tensor_tensor(
                out=tS[:], in0=hidS[sl2][:], scalar=0.5, in1=sgS[sl2][:],
                op0=OP.add, op1=OP.subtract))
            em.emit("vector", lambda E_: E_.tensor_mul(tS[:], mskS[:], tS[:]))
            em.emit("vector", lambda E_, sl2=sl2: E_.tensor_add(tS[:], sgS[sl2][:], tS[:]))
            em.emit("vector", lambda E_, sl2=sl2: E_.tensor_mul(wT[:], aT[sl2][:], tS[:]))
            em.emit("vector", lambda E_: E_.tensor_sub(wT[:], tS[:], wT[:]))
            hdst = hbf[:, pair * 512:(pair + 1) * 512]
            if tt == 0:
                em.emit("vector", lambda E_, hdst=hdst, sl2=sl2: E_.tensor_tensor_scan(
                    out=hdst, data0=aT[sl2][:], data1=wT[:], initial=0.0,
                    op0=OP.mult, op1=OP.add))
            else:
                em.emit("vector", lambda E_, hdst=hdst, pair=pair, sl2=sl2: E_.tensor_tensor_scan(
                    out=hdst, data0=aT[sl2][:], data1=wT[:],
                    initial=carry[:, pair:pair + 1], op0=OP.mult, op1=OP.add))
            em.emit("vector", lambda E_, pair=pair: E_.tensor_copy(
                out=carry[:, pair:pair + 1],
                in_=hbf[:, pair * 512 + 511: pair * 512 + 512]), inc=(sVec, 1))
        ra1 = rows["alpha"][0:1, tt * 512:(tt + 1) * 512]
        rbe = rows["beta"][0:1, tt * 512:(tt + 1) * 512]
        em.wait("tensor", sScl, tt * 8 + 8)
        if tt >= 1:
            em.wait("tensor", sEv, tt * 8)
        em.emit("tensor", lambda E_, ra1=ra1: E_.matmul(
            b.pb[4][:, 0:512], b.cst["c_ones1"][:], ra1, start=True, stop=True))
        em.emit("tensor", lambda E_, rbe=rbe: E_.matmul(
            b.pb[5][:, 0:512], b.cst["c_ones1"][:], rbe, start=True, stop=True),
            inc=(sBc2, 1))
        em.wait("vector", sBc2, tt + 1)
        em.emit("vector", lambda E_: E_.tensor_scalar_add(
            a1bS[:], b.pb[4][:, 0:512], 1.0))
        em.emit("vector", lambda E_: E_.tensor_copy(out=bbS[:], in_=b.pb[5][:, 0:512]),
                inc=(sBcC, 2))
        em.wait("tensor", sVec, tt * 8 + 8)
        em.wait("tensor", sBcC, tt * 2 + 2)
        if tt >= 1:
            em.wait("tensor", sEv, tt * 8)
        for oc in range(KC):
            for kc in range(KC):
                em.emit("tensor", lambda E_, oc=oc, kc=kc: E_.matmul(
                    b.pb[oc][:, 0:512],
                    woutT[:, kc * 1024 + oc * 128: kc * 1024 + oc * 128 + 128],
                    hbf[:, kc * 512:(kc + 1) * 512],
                    start=(kc == 0), stop=(kc == KC - 1)),
                    inc=(sOutM, 1) if kc == KC - 1 else None)
        for oc in range(KC):
            i = tt * KC + oc
            if i >= 2:
                em.wait("sync", sEv, i - 1)
            em.emit("sync", lambda E_, oc=oc, tt=tt, slot=i % 2: E_.dma_start(
                out=x0r[slot][:], in_=x0_hbm[tt, oc]),
                inc=(sX0[i % 2], 16))
            em.wait("vector", sOutM, tt * KC + oc + 1)
            em.wait("vector", sX0[i % 2], (i // 2 + 1) * 16)
            if i >= 2:
                em.wait("vector", sSt[i % 2], (i // 2) * 16)
            em.emit("vector", lambda E_, slot=i % 2: E_.tensor_mul(
                tS[:], x0r[slot][:], a1bS[:]))
            em.emit("vector", lambda E_: E_.tensor_add(tS[:], tS[:], bbS[:]))
            em.emit("vector", lambda E_, oc=oc, slot=i % 2: E_.tensor_tensor(
                out=x1o[slot][:], in0=tS[:], in1=b.pb[oc][:, 0:512], op=OP.add),
                inc=(sEv, 1))
            em.wait("sync", sEv, i + 1)
            em.emit("sync", lambda E_, oc=oc, tt=tt, slot=i % 2: E_.dma_start(
                out=x1_hbm[tt, oc], in_=x1o[slot][:]), inc=(sSt[i % 2], 16))
    em.join([(sSt[0], em.counts[sSt[0]]), (sSt[1], em.counts[sSt[1]]),
             (sEv, em.counts[sEv])])
    b.pop()


def emit_phase_b(b, x1_hbm, x2_hbm):
    em, nc, T, NTT, NTC = b.em, b.nc, b.T, b.NTT, b.NTC
    pfx = "bb"
    wr_h = b.din("L0_ffn_Wr", [D, E])
    wg_h = b.din("L0_ffn_Wg", [E, H // 256, 128, KC * 256])
    wu_h = b.din("L0_ffn_Wu", [E, H // 256, 128, KC * 256])
    wd_h = b.din("L0_ffn_Wd", [E, H, D])

    rows = {}
    xn = b.sb("b_xn", [128, KC * T], BF16)
    Mc = b.sb("b_Mc", [8, T], F32)
    emit_ln_pass(b, pfx, x1_hbm, rows, want_rms=False, xn_sb=xn,
                 xn_rows=("alpha", "beta"))

    em.ser("sync", lambda E_: E_.dma_start(out=x2_hbm[:], in_=x1_hbm[:]), dma=True)

    b.push()
    wr_sb = b.sb("b_wr", [128, 8 * 8], F32)
    wrb = b.sb("b_wrb", [128, 8 * 8], BF16)
    logT = b.sb("b_logT", [8, T], F32)
    for fc in range(KC):
        em.ser("sync", lambda E_, fc=fc: E_.dma_start(
            out=wr_sb[:, fc * 8:(fc + 1) * 8], in_=wr_h[fc * 128:(fc + 1) * 128, :]),
            dma=True)
    em.ser("scalar", lambda E_: E_.copy(out=wrb[:], in_=wr_sb[:]))
    for tt in range(NTT):
        for fc in range(KC):
            em.ser("tensor", lambda E_, fc=fc, tt=tt: E_.matmul(
                b.pb[6][0:8, 0:512], wrb[:, fc * 8:(fc + 1) * 8],
                xn[:, fc * T + tt * 512: fc * T + tt * 512 + 512],
                start=(fc == 0), stop=(fc == KC - 1)))
        em.ser("vector", lambda E_, tt=tt: E_.tensor_copy(
            out=logT[0:8, tt * 512:(tt + 1) * 512], in_=b.pb[6][0:8, 0:512]))

    lg = b.sb("b_lg", [128, 8], F32)
    c1 = b.sb("b_c1", [128, 8], F32)
    c2 = b.sb("b_c2", [128, 8], F32)
    cc = b.sb("b_cc", [128, 8], F32)
    m1 = b.sb("b_m1", [128, 6], F32)
    for tc in range(NTC):
        em.ser("tensor", lambda E_, tc=tc: E_.transpose(
            b.pb[6][:, 0:8], logT[0:8, tc * 128:(tc + 1) * 128],
            b.cst["c_ident"][0:8, 0:8]))
        em.ser("vector", lambda E_: E_.tensor_copy(out=lg[:], in_=b.pb[6][:, 0:8]))
        em.ser("vector", lambda E_: E_.tensor_reduce(
            out=m1[:, 0:1], in_=lg[:], axis=AX, op=OP.max))
        em.ser("vector", lambda E_: E_.tensor_scalar(
            out=c1[:], in0=lg[:], scalar1=m1[:, 0:1], scalar2=None, op0=OP.is_equal))
        em.ser("vector", lambda E_: E_.tensor_mul(c1[:], c1[:], b.cst["c_p2row"][:]))
        em.ser("vector", lambda E_: E_.tensor_reduce(
            out=m1[:, 1:2], in_=c1[:], axis=AX, op=OP.max))
        em.ser("vector", lambda E_: E_.tensor_scalar(
            out=c1[:], in0=c1[:], scalar1=m1[:, 1:2], scalar2=None, op0=OP.is_equal))
        em.ser("vector", lambda E_: E_.scalar_tensor_tensor(
            out=cc[:], in0=c1[:], scalar=-1e30, in1=lg[:], op0=OP.mult, op1=OP.add))
        em.ser("vector", lambda E_: E_.tensor_reduce(
            out=m1[:, 2:3], in_=cc[:], axis=AX, op=OP.max))
        em.ser("vector", lambda E_: E_.tensor_scalar(
            out=c2[:], in0=cc[:], scalar1=m1[:, 2:3], scalar2=None, op0=OP.is_equal))
        em.ser("vector", lambda E_: E_.tensor_mul(c2[:], c2[:], b.cst["c_p2row"][:]))
        em.ser("vector", lambda E_: E_.tensor_reduce(
            out=m1[:, 3:4], in_=c2[:], axis=AX, op=OP.max))
        em.ser("vector", lambda E_: E_.tensor_scalar(
            out=c2[:], in0=c2[:], scalar1=m1[:, 3:4], scalar2=None, op0=OP.is_equal))
        em.ser("vector", lambda E_: E_.tensor_sub(m1[:, 4:5], m1[:, 0:1], m1[:, 2:3]))
        em.ser("scalar", lambda E_: E_.activation(
            out=m1[:, 4:5], in_=m1[:, 4:5], func=AF.Sigmoid))
        em.ser("vector", lambda E_: E_.tensor_scalar_mul(
            m1[:, 4:5], m1[:, 4:5], DEPTH_SCALE))
        em.ser("vector", lambda E_: E_.tensor_scalar(
            out=m1[:, 5:6], in0=m1[:, 4:5], scalar1=-1.0, scalar2=DEPTH_SCALE,
            op0=OP.mult, op1=OP.add))
        em.ser("vector", lambda E_: E_.tensor_scalar(
            out=cc[:], in0=c1[:], scalar1=m1[:, 4:5], scalar2=None, op0=OP.mult))
        em.ser("vector", lambda E_: E_.scalar_tensor_tensor(
            out=cc[:], in0=c2[:], scalar=m1[:, 5:6], in1=cc[:],
            op0=OP.mult, op1=OP.add))
        em.ser("tensor", lambda E_, tc=tc: E_.transpose(
            b.pb[6][0:8, 0:128], cc[:], b.cst["c_ident"][:]))
        em.ser("vector", lambda E_, tc=tc: E_.tensor_copy(
            out=Mc[0:8, tc * 128:(tc + 1) * 128], in_=b.pb[6][0:8, 0:128]))
    em.join([(em.cch, em.counts[em.cch]), (em.dch, em.counts[em.dch])])
    b.pop()

    b.push()
    emit_moe_loop(b, "b", xn, Mc, x2_hbm, wg_h, wu_h, wd_h, None, None,
                  silu_glu=True)
    b.pop()


def emit_moe_loop(b, pfx, xn, Mc, xout_hbm, w1_h, wu_h, w2_h, b1cols, b2cols,
                  silu_glu):
    em, T = b.em, b.T
    em.join([(em.cch, em.counts[em.cch]), (em.dch, em.counts[em.dch])])
    TB = 512
    NTB = T // TB
    TBG = 2
    NPASS = NTB // TBG
    HQ8 = 16
    QW = H // HQ8
    HQC = HC // HQ8

    wgb = [b.sb(f"{pfx}_wgb{i}", [128, KC * QW], BF16) for i in range(2)]
    wub = ([b.sb(f"{pfx}_wub{i}", [128, KC * QW], BF16) for i in range(2)]
           if silu_glu else None)
    NWR = 4
    wdb = [b.sb(f"{pfx}_wdb{i}", [128, D], BF16) for i in range(NWR)]
    h_all = [b.sb(f"{pfx}_h{i}", [128, HC * TB], BF16) for i in range(TBG)]
    acc = b.sb(f"{pfx}_acc", [128, KC * TB], F32)
    ce_t = [b.sb(f"{pfx}_ce{i}", [128, TB], F32) for i in range(TBG)]
    silu_t = [b.sb(f"{pfx}_si{i}", [128, TB], BF16) for i in range(2)]
    evt = ([b.sb(f"{pfx}_evt{i}", [128, TB], F32) for i in range(2)]
           if b2cols is not None else None)

    sW = [em.sem(f"{pfx}_w{i}") for i in range(2)]
    sWD = [em.sem(f"{pfx}_wd{i}") for i in range(NWR)]
    sWDF = em.sem(f"{pfx}_wdf")
    sG = em.sem(f"{pfx}_g")
    sU = em.sem(f"{pfx}_u")
    sGF = em.sem(f"{pfx}_gf")
    sH = em.sem(f"{pfx}_hh")
    sD = em.sem(f"{pfx}_d")
    sCe = em.sem(f"{pfx}_cem")
    sCeC = em.sem(f"{pfx}_cec")
    sB2 = em.sem(f"{pfx}_b2")
    sEv = em.sem(f"{pfx}_ev")
    sSt = [em.sem(f"{pfx}_sb{i}") for i in range(NTB)]
    hdone = sH if silu_glu else sGF

    per_q = 2 * 16 if silu_glu else 16
    NQ = E * NPASS * HQ8

    def load_q(q, par):
        e2 = q // (NPASS * HQ8)
        hq2 = q % HQ8
        em.emit("gpsimd", lambda E_, e2=e2, hq2=hq2, par=par: E_.dma_start(
            out=wgb[par][:], in_=w1_h[e2, hq2]), inc=(sW[par], 16))
        if silu_glu:
            em.emit("gpsimd", lambda E_, e2=e2, hq2=hq2, par=par: E_.dma_start(
                out=wub[par][:], in_=wu_h[e2, hq2]), inc=(sW[par], 16))

    load_q(0, 0)
    q_loaded = 1
    gi_c = [0]
    store_done = {}

    for e in range(E):
        for p in range(NPASS):
            pi = e * NPASS + p
            tbs = [p * TBG + i for i in range(TBG)]
            for i2, tb in enumerate(tbs):
                t0 = tb * TB
                bank = b.pb[6 + i2]
                if pi >= 1:
                    em.wait("tensor", sEv, pi * TBG * KC)
                em.emit("tensor", lambda E_, e=e, t0=t0, bank=bank: E_.matmul(
                    bank[:, 0:TB], b.cst["c_oneh"][0:8, e * 128:(e + 1) * 128],
                    Mc[0:8, t0:t0 + TB], start=True, stop=True), inc=(sCe, 1))
                cCe = em.counts[sCe]
                em.wait("vector", sCe, cCe)
                em.emit("vector", lambda E_, i2=i2, bank=bank: E_.tensor_copy(
                    out=ce_t[i2][:], in_=bank[:, 0:TB]), inc=(sCeC, 1))
            for hq in range(HQ8):
                qi = pi * HQ8 + hq
                par = qi % 2
                if q_loaded < NQ:
                    if q_loaded >= 2:
                        em.wait("gpsimd", sU if silu_glu else sG,
                                (q_loaded - 1) * HQC * TBG)
                    load_q(q_loaded, q_loaded % 2)
                    q_loaded += 1
                em.wait("tensor", sW[par], (qi // 2 + 1) * per_q)
                for hc4 in range(HQC):
                    hc = hq * HQC + hc4
                    for i2, tb in enumerate(tbs):
                        t0 = tb * TB
                        gi = gi_c[0]
                        gi_c[0] += 1
                        gb = b.pb[gi % 2] if silu_glu else b.pb[gi % 4]
                        ub = b.pb[2 + gi % 2] if silu_glu else None
                        if silu_glu:
                            if gi >= 2:
                                em.wait("tensor", sGF, gi - 1)
                                em.wait("tensor", sH, gi - 1)
                        elif gi >= 4:
                            em.wait("tensor", sGF, gi - 3)
                        if hc == 0 and i2 == 0 and pi >= 1:
                            em.wait("tensor", sEv, pi * TBG * KC)
                        for kc in range(KC):
                            em.emit("tensor", lambda E_, par=par, kc=kc, hc4=hc4, gb=gb, t0=t0: E_.matmul(
                                gb[:, 0:TB],
                                wgb[par][:, kc * QW + hc4 * 128: kc * QW + hc4 * 128 + 128],
                                xn[:, kc * T + t0: kc * T + t0 + TB],
                                start=(kc == 0), stop=(kc == KC - 1)),
                                inc=(sG, 1) if kc == KC - 1 else None)
                        if silu_glu:
                            for kc in range(KC):
                                em.emit("tensor", lambda E_, par=par, kc=kc, hc4=hc4, ub=ub, t0=t0: E_.matmul(
                                    ub[:, 0:TB],
                                    wub[par][:, kc * QW + hc4 * 128: kc * QW + hc4 * 128 + 128],
                                    xn[:, kc * T + t0: kc * T + t0 + TB],
                                    start=(kc == 0), stop=(kc == KC - 1)),
                                    inc=(sU, 1) if kc == KC - 1 else None)
                        em.wait("scalar", sG, gi + 1)
                        if silu_glu:
                            if gi >= 2:
                                em.wait("scalar", sH, gi - 1)
                            em.emit("scalar", lambda E_, gi=gi, gb=gb: E_.activation(
                                out=silu_t[gi % 2][:], in_=gb[:, 0:TB], func=AF.Silu),
                                inc=(sGF, 1))
                            em.wait("vector", sU, gi + 1)
                            em.wait("vector", sGF, gi + 1)
                            em.emit("vector", lambda E_, gi=gi, ub=ub, hc=hc, i2=i2: E_.tensor_tensor(
                                out=h_all[i2][:, hc * TB:(hc + 1) * TB],
                                in0=silu_t[gi % 2][:], in1=ub[:, 0:TB], op=OP.mult),
                                inc=(sH, 1))
                        else:
                            em.emit("scalar", lambda E_, e=e, hc=hc, gb=gb, i2=i2: E_.activation(
                                out=h_all[i2][:, hc * TB:(hc + 1) * TB], in_=gb[:, 0:TB],
                                func=AF.Gelu, bias=b1cols[:, e * HC + hc: e * HC + hc + 1],
                                scale=1.0), inc=(sGF, 1))
            for i2, tb in enumerate(tbs):
                di = pi * TBG + i2
                t0 = tb * TB
                for hc in range(HC):
                    wi = di * HC + hc
                    if wi >= NWR:
                        w2 = wi - NWR
                        if w2 % HC == HC - 1:
                            em.wait("gpsimd", sD, (w2 // HC + 1) * KC)
                        else:
                            em.wait("gpsimd", sWDF, (w2 // HC) * (HC - 1) + (w2 % HC) + 1)
                    em.emit("gpsimd", lambda E_, e=e, hc=hc, r=wi % NWR: E_.dma_start(
                        out=wdb[r][:], in_=w2_h[e, hc * 128:(hc + 1) * 128, :]),
                        inc=(sWD[wi % NWR], 16))
                for hc in range(HC):
                    wi = di * HC + hc
                    par = wi % NWR
                    em.wait("tensor", sWD[par], (wi // NWR + 1) * 16)
                    if hc == 0:
                        em.wait("tensor", hdone, (pi + 1) * HQ8 * HQC * TBG)
                        if silu_glu:
                            em.wait("tensor", sGF, (pi + 1) * HQ8 * HQC * TBG)
                        em.wait("tensor", sEv, di * KC)
                        em.wait("tensor", sCeC, (pi + 1) * TBG)
                    for dc in range(KC):
                        if hc == HC - 1:
                            inc = (sD, 1)
                        elif dc == KC - 1:
                            inc = (sWDF, 1)
                        else:
                            inc = None
                        em.emit("tensor", lambda E_, par=par, dc=dc, hc=hc, i2=i2: E_.matmul(
                            b.pb[dc][:, 0:TB],
                            wdb[par][:, dc * 128:(dc + 1) * 128],
                            h_all[i2][:, hc * TB:(hc + 1) * TB],
                            start=(hc == 0), stop=(hc == HC - 1)), inc=inc)
                cDbase = di * KC
                for dc in range(KC):
                    em.wait("vector", sD, cDbase + dc + 1)
                    if dc == 0:
                        for tbx, cnt in store_done.items():
                            em.wait("vector", sSt[tbx], cnt)
                    if b2cols is None:
                        em.emit("vector", lambda E_, dc=dc, i2=i2: E_.tensor_tensor(
                            out=acc[:, dc * TB:(dc + 1) * TB], in0=b.pb[dc][:, 0:TB],
                            in1=ce_t[i2][:], op=OP.mult), inc=(sEv, 1))
                    else:
                        nv = cDbase + dc
                        em.wait("scalar", sD, nv + 1)
                        if nv >= 2:
                            em.wait("scalar", sEv, nv - 1)
                        em.emit("scalar", lambda E_, dc=dc, e=e, p=nv % 2: E_.add(
                            out=evt[p][:], in_=b.pb[dc][:, 0:TB],
                            add=b2cols[:, e * KC + dc: e * KC + dc + 1]), inc=(sB2, 1))
                        em.wait("vector", sB2, nv + 1)
                        em.emit("vector", lambda E_, dc=dc, i2=i2, p=nv % 2: E_.tensor_tensor(
                            out=acc[:, dc * TB:(dc + 1) * TB], in0=evt[p][:],
                            in1=ce_t[i2][:], op=OP.mult), inc=(sEv, 1))
                em.wait("gpsimd", sEv, em.counts[sEv])
                em.wait("gpsimd", sSt[tb], em.counts[sSt[tb]])
                for dc in range(KC):
                    em.emit("gpsimd", lambda E_, dc=dc, tb=tb: E_.dma_start(
                        out=xout_hbm[tb, dc],
                        in_=acc[:, dc * TB:(dc + 1) * TB], accum_op=OP.add),
                        inc=(sSt[tb], 16))
                store_done[tb] = em.counts[sSt[tb]]
    em.join([(s, em.counts[s]) for s in sSt])


def emit_phase_c(b, x2_hbm, x3_hbm):
    em, nc, T, NTT, NTC = b.em, b.nc, b.T, b.NTT, b.NTC
    pfx = "cc"
    wq_h = b.din("L1_attn_Wq", [D, D])
    bq_h = b.din("L1_attn_bq", [D])
    wk_h = b.din("L1_attn_Wk", [D, D])
    bk_h = b.din("L1_attn_bk", [D])
    wv_h = b.din("L1_attn_Wv", [D, D])
    bv_h = b.din("L1_attn_bv", [D])
    wf_h = b.din("L1_attn_Wf", [2, KC, 64, 8 * 128])
    bf_h = b.din("L1_attn_bf", [D])

    rows = alloc_rows(b, pfx, ["alpha", "beta"], persist=("alpha", "beta"),
                      persist_dtype=BF16)
    em.ser("sync", lambda E_: E_.dma_start(out=x3_hbm[:], in_=x2_hbm[:]), dma=True)
    cpy_d = em.counts[em.dch]
    emit_ln_pass(b, pfx, x2_hbm, rows, want_rms=False)

    VST = 8 * 65
    kTh = b.sb("c_kTh", [128, 4 * T], BF16)
    qTh = b.sb("c_qTh", [128, 4 * T], BF16)
    vh = b.sb("c_vh", [128, NTC * VST], BF16)
    lrow = b.sb("c_lrow", [1, 512], F32)
    bqc = b.sb("c_bqc", [128, KC], F32)
    bkc = b.sb("c_bkc", [128, KC], F32)
    bfc = b.sb("c_bfc", [128, KC], F32)
    bvb = b.sb("c_bvb", [128, D], BF16)
    nrm = b.sb("c_nrm", [2, 8], F32)
    mc1 = b.sb("c_mc1", [2, 1], F32)
    bndc = b.sb("c_bndc", [2, 4], F32)
    bcolsP = [b.sb(f"c_bcol{i}", [128, 4], F32) for i in range(2)]
    attnh = b.sb("c_attnh", [64, 8 * 512], BF16)
    wfb = [b.sb(f"c_wfb{i}", [64, 8 * 128], BF16) for i in range(2)]
    dnames = [f"c_diag_{d}{bb}" for d in ("f", "b") for bb in range(4)]
    dnames += ["c_rel_kq512", "c_ones64"]
    for dname in dnames:
        b.cst[dname] = b.sb(dname + "_sb", list(CONST_SPECS[dname]), BF16)
    b.push()
    stage = b.sb("c_stage", [128, 512], F32)
    for dname in dnames:
        h = b.din(dname, CONST_SPECS[dname])
        ncol = CONST_SPECS[dname][1]
        em.ser("sync", lambda E_, h=h, ncol=ncol: E_.dma_start(
            out=stage[:, 0:ncol], in_=h[:]), dma=True)
        em.ser("scalar", lambda E_, t=b.cst[dname], ncol=ncol: E_.copy(
            out=t[:], in_=stage[:, 0:ncol]))
    for hh in range(2):
        em.ser("sync", lambda E_, hh=hh: E_.dma_start(
            out=stage[0:1, 0:512],
            in_=bv_h.ap().rearrange("(o d) -> o d", o=2)[hh:hh + 1, :]), dma=True)
        em.ser("tensor", lambda E_: E_.matmul(
            b.pb[6][:, 0:512], b.cst["c_ones1"][:], stage[0:1, 0:512],
            start=True, stop=True))
        em.ser("vector", lambda E_, hh=hh: E_.tensor_copy(
            out=bvb[:, hh * 512:(hh + 1) * 512], in_=b.pb[6][:, 0:512]))
    em.barrier()
    b.pop()
    wbuf = b.sb("c_wbuf", [128, KC * D], BF16)
    x2t = [b.sb(f"c_x2t{i}", [128, 2 * 512], F32) for i in range(2)]
    xnt = [b.sb(f"c_xnt{i}", [128, KC * 512], BF16) for i in range(2)]
    srelP = [b.sb(f"c_srelP{i}", [128, 512], BF16) for i in range(2)]
    offc = [b.sb(f"c_offc{i}", [128, 64], F32) for i in range(2)]
    offc2 = [b.sb(f"c_offc2{i}", [128, 64], F32) for i in range(2)]
    diagP = [[b.sb(f"c_diagP{i}_{hx}", [128, 512], BF16) for i in range(4)]
             for hx in range(2)]
    scoreS = [x2t[0][:, 0:512], x2t[1][:, 0:512], x2t[0][:, 512:1024]]
    rb = x2t[1][0:64, 512:1024]
    sq2 = scoreS[0:2]
    pT = [b.sb(f"c_pT{i}", [128, 512], BF16) for i in range(3)]
    st3 = [b.sb(f"c_st3{i}", [128, 512], F32) for i in range(2)]
    txf = b.sb("c_txf", [128, 512], F32)
    slopes = alibi_slopes()

    for vh_, vc_, vn_ in ((bq_h, bqc, "c_bqt"), (bk_h, bkc, "c_bkt"),
                          (bf_h, bfc, "c_bft")):
        b.push()
        vec2cols(b, vh_, D, vc_, vn_)
        em.barrier()
        b.pop()

    for tc in range(NTC):
        for h8 in range(8):
            em.emit("vector", lambda E_, tc=tc, h8=h8: E_.memset(
                vh[:, tc * VST + h8 * 65 + 64: tc * VST + h8 * 65 + 65], 1.0))

    sXL = [em.sem(f"c_xl{i}") for i in range(2)]
    sXB = em.sem("c_xb")
    sBC2 = em.sem("c_bc2")
    sKB = em.sem("c_kb")
    sKE = em.sem("c_ke")
    sWB = em.sem("c_wbl")
    sSq = em.sem("c_sq2")
    sNmm = em.sem("c_nmm")
    sRed = em.sem("c_red")
    sS = em.sem("c_sS")
    sTx = em.sem("c_stx")
    sPe = em.sem("c_spe")
    sPv = em.sem("c_spv")
    sRcp = em.sem("c_rcp")
    sRbm = em.sem("c_rbm")
    sPat = em.sem("c_pat")
    sEvh = em.sem("c_evh")
    sWF = em.sem("c_swf")
    sF = em.sem("c_sf")
    sFe = em.sem("c_sfe")
    sV3m = em.sem("c_v3m")
    sSt3 = [em.sem(f"c_st3s{i}") for i in range(2)]
    st = {"xc": 0, "xt": 0}

    first_store = [True]
    ihead = [0]
    cF_attnh = [0]

    for half in range(2):
        hbase = half * 8
        em.join([(s, em.counts[s]) for s in
                 (sKE, sKB, sXB, sBC2, sWB, sS, sTx, sPe, sPv, sRcp, sRbm,
                  sEvh, sWF, sF, sFe, sV3m, sSt3[0], sSt3[1], sXL[0], sXL[1],
                  sSq, sNmm, sRed, em.cch, em.dch)]
                + [(s, em.counts[s]) for s in em.selfsem.values()])

        mxb = txf

        def build_xnt(tt, guard=0):
            ti = st["xt"]
            st["xt"] += 1
            p = ti % 2
            if guard > 0:
                em.wait("vector", sKE, guard)
            ral = rows["alpha"][0:1, tt * 512:(tt + 1) * 512]
            rbe = rows["beta"][0:1, tt * 512:(tt + 1) * 512]
            for r in range(4):
                i = st["xc"]
                st["xc"] += 1
                pr = i % 2
                if i >= 2:
                    em.wait("gpsimd", sXB, i - 1)
                for fc4 in range(2):
                    fc = r * 2 + fc4
                    em.emit("gpsimd", lambda E_, tt=tt, pr=pr, fc=fc, fc4=fc4: E_.dma_start(
                        out=x2t[pr][:, fc4 * 512:(fc4 + 1) * 512],
                        in_=x2_hbm[tt, fc]), inc=(sXL[pr], 16))
                cL = em.counts[sXL[pr]]
                if r == 0:
                    em.wait("tensor", sXB, 4 * ti)
                    em.emit("tensor", lambda E_, ral=ral: E_.matmul(
                        b.pb[2][:, 0:512], b.cst["c_ones1b"][:], ral,
                        start=True, stop=True))
                    em.emit("tensor", lambda E_, rbe=rbe: E_.matmul(
                        b.pb[3][:, 0:512], b.cst["c_ones1b"][:], rbe,
                        start=True, stop=True), inc=(sBC2, 1))
                    cB = em.counts[sBC2]
                    em.wait("vector", sBC2, cB)
                em.wait("vector", sXL[pr], cL)
                for fc4 in range(2):
                    fc = r * 2 + fc4
                    em.emit("vector", lambda E_, fc=fc, fc4=fc4, pr=pr: E_.tensor_tensor(
                        out=mxb[:], in0=x2t[pr][:, fc4 * 512:(fc4 + 1) * 512],
                        in1=b.pb[2][:, 0:512], op=OP.mult))
                    em.emit("vector", lambda E_, fc=fc, p=p: E_.tensor_tensor(
                        out=xnt[p][:, fc * 512:(fc + 1) * 512], in0=mxb[:],
                        in1=b.pb[3][:, 0:512], op=OP.add),
                        inc=(sXB, 1) if fc4 == 1 else None)
            return p, em.counts[sXB]

        def load_w(wh):
            em.wait("gpsimd", sKB, em.counts[sKB])
            for fc in range(KC):
                em.emit("gpsimd", lambda E_, fc=fc, wh=wh: E_.dma_start(
                    out=wbuf[:, fc * D:(fc + 1) * D],
                    in_=wh[fc * 128:(fc + 1) * 128, :]), inc=(sWB, 16))
            return em.counts[sWB]

        wk_d = load_w(wk_h)
        cKE0 = em.counts[sKE]
        for tt in range(NTT):
            p, cXB = build_xnt(tt, cKE0 + 4 * (tt - 1) if tt >= 2 else 0)
            em.wait("tensor", sXB, cXB)
            if tt == 0:
                em.wait("tensor", sWB, wk_d)
            for c4 in range(4):
                dcg = half * 4 + c4
                bank = b.pb[4 + c4 % 2]
                if em.counts[sKB] >= 2:
                    em.wait("tensor", sKE, em.counts[sKB] - 1)
                for fc in range(KC):
                    em.emit("tensor", lambda E_, fc=fc, dcg=dcg, p=p, bank=bank: E_.matmul(
                        bank[:, 0:512],
                        wbuf[:, fc * D + dcg * 128: fc * D + dcg * 128 + 128],
                        xnt[p][:, fc * 512:(fc + 1) * 512],
                        start=(fc == 0), stop=(fc == KC - 1)),
                        inc=(sKB, 1) if fc == KC - 1 else None)
                cKB = em.counts[sKB]
                em.wait("scalar", sKB, cKB)
                em.emit("scalar", lambda E_, c4=c4, tt=tt, dcg=dcg, bank=bank: E_.add(
                    out=kTh[:, c4 * T + tt * 512: c4 * T + tt * 512 + 512],
                    in_=bank[:, 0:512], add=bkc[:, dcg:dcg + 1]), inc=(sKE, 1))
        wv_d = load_w(wv_h)
        cKE0 = em.counts[sKE]
        for tt in range(NTT):
            p, cXB = build_xnt(tt, cKE0 + 4 * (tt - 1) if tt >= 2 else 0)
            em.wait("tensor", sXB, cXB)
            if tt == 0:
                em.wait("tensor", sWB, wv_d)
                em.wait("tensor", sKE, em.counts[sKE])
            for tc4 in range(4):
                tc = tt * 4 + tc4
                bank = b.pb[4 + tc4 % 2]
                if em.counts[sKB] >= 2:
                    em.wait("tensor", sKE, em.counts[sKB] - 1)
                for fc in range(KC):
                    em.emit("tensor", lambda E_, fc=fc, tc4=tc4, p=p, bank=bank, hbase=hbase: E_.matmul(
                        bank[:, 0:512],
                        xnt[p][:, fc * 512 + tc4 * 128: fc * 512 + tc4 * 128 + 128],
                        wbuf[:, fc * D + hbase * 64: fc * D + hbase * 64 + 512],
                        start=(fc == 0), stop=(fc == KC - 1)),
                        inc=(sKB, 1) if fc == KC - 1 else None)
                cKB = em.counts[sKB]
                em.wait("vector", sKB, cKB)
                for h8 in range(8):
                    em.emit("vector", lambda E_, tc=tc, h8=h8, bank=bank, half=half: E_.tensor_tensor(
                        out=vh[:, tc * VST + h8 * 65: tc * VST + h8 * 65 + 64],
                        in0=bank[:, h8 * 64:(h8 + 1) * 64],
                        in1=bvb[:, half * 512 + h8 * 64: half * 512 + h8 * 64 + 64],
                        op=OP.add), inc=(sKE, 1) if h8 == 7 else None)
        wq_d = load_w(wq_h)
        cKE0 = em.counts[sKE]
        for tt in range(NTT):
            p, cXB = build_xnt(tt, cKE0 + 4 * (tt - 1) if tt >= 2 else 0)
            em.wait("tensor", sXB, cXB)
            if tt == 0:
                em.wait("tensor", sWB, wq_d)
                em.wait("tensor", sKE, em.counts[sKE])
            for c4 in range(4):
                dcg = half * 4 + c4
                bank = b.pb[4 + c4 % 2]
                if em.counts[sKB] >= 2:
                    em.wait("tensor", sKE, em.counts[sKB] - 1)
                for fc in range(KC):
                    em.emit("tensor", lambda E_, fc=fc, dcg=dcg, p=p, bank=bank: E_.matmul(
                        bank[:, 0:512],
                        wbuf[:, fc * D + dcg * 128: fc * D + dcg * 128 + 128],
                        xnt[p][:, fc * 512:(fc + 1) * 512],
                        start=(fc == 0), stop=(fc == KC - 1)),
                        inc=(sKB, 1) if fc == KC - 1 else None)
                cKB = em.counts[sKB]
                em.wait("scalar", sKB, cKB)
                em.emit("scalar", lambda E_, c4=c4, tt=tt, dcg=dcg, bank=bank: E_.add(
                    out=qTh[:, c4 * T + tt * 512: c4 * T + tt * 512 + 512],
                    in_=bank[:, 0:512], add=bqc[:, dcg:dcg + 1]), inc=(sKE, 1))
        em.join([(s, em.counts[s]) for s in
                 (sKE, sKB, sXB, sBC2, sWB, sXL[0], sXL[1], em.cch, em.dch)]
                + [(s, em.counts[s]) for s in em.selfsem.values()])

        for si, src in enumerate((kTh, qTh)):
            for hc2 in range(4):
                for tt in range(NTT):
                    i = em.counts[sSq]
                    if i >= 2:
                        em.wait("scalar", sNmm, i - 1)
                    em.emit("scalar", lambda E_, src=src, hc2=hc2, tt=tt, p=i % 2: E_.square(
                        out=sq2[p][:],
                        in_=src[:, hc2 * T + tt * 512: hc2 * T + tt * 512 + 512]),
                        inc=(sSq, 1))
                    em.wait("tensor", sSq, i + 1)
                    if i >= 2:
                        em.wait("tensor", sRed, i - 1)
                    em.emit("tensor", lambda E_, p=i % 2, bk=b.pb[6 + i % 2]: E_.matmul(
                        bk[0:2, 0:512], b.cst["c_bones2"][:], sq2[p][:],
                        start=True, stop=True), inc=(sNmm, 1))
                    em.wait("vector", sNmm, i + 1)
                    col = nrm[0:2, si * 4 + hc2: si * 4 + hc2 + 1]
                    if tt == 0:
                        em.emit("vector", lambda E_, col=col, bk=b.pb[6 + i % 2]: E_.tensor_reduce(
                            out=col, in_=bk[0:2, 0:512], axis=AX, op=OP.max),
                            inc=(sRed, 1))
                    else:
                        em.emit("vector", lambda E_, bk=b.pb[6 + i % 2]: E_.tensor_reduce(
                            out=mc1[:], in_=bk[0:2, 0:512], axis=AX, op=OP.max))
                        em.emit("vector", lambda E_, col=col: E_.tensor_tensor(
                            out=col, in0=col, in1=mc1[:], op=OP.max), inc=(sRed, 1))
        em.join([(sSq, em.counts[sSq]), (sNmm, em.counts[sNmm]),
                 (sRed, em.counts[sRed])]
                + [(s, em.counts[s]) for s in em.selfsem.values()])
        em.ser("scalar", lambda E_: E_.sqrt(out=nrm[:], in_=nrm[:]))
        em.ser("vector", lambda E_: E_.tensor_mul(bndc[:], nrm[0:2, 0:4],
                                                  nrm[0:2, 4:8]))
        em.ser("vector", lambda E_: E_.tensor_scalar(
            out=bndc[:], in0=bndc[:], scalar1=0.125, scalar2=1.0,
            op0=OP.mult, op1=OP.add))
        for hp in range(2):
            em.ser("tensor", lambda E_, hp=hp: E_.matmul(
                b.pb[6][:, 0:4], b.cst["c_oneh"][0:2, hp * 128:(hp + 1) * 128],
                bndc[:], start=True, stop=True))
            em.ser("vector", lambda E_, hp=hp: E_.tensor_copy(
                out=bcolsP[hp][:], in_=b.pb[6][:, 0:4]))

        stt_exp = []
        for qt in range(NTT):
            for hpair in range(4):
                heads = (2 * hpair, 2 * hpair + 1)
                hc2 = hpair
                fwd = (hbase + heads[0]) < 8
                slope = float(slopes[hc2 * 2]), float(slopes[hc2 * 2 + 1])
                slh = [(s if fwd else -s) for s in slope]
                shallow = hpair >= 2
                em.wait("vector", sPe, em.counts[sPe])
                for hx in range(2):
                    sl = slh[hx]
                    bcol = bcolsP[heads[hx] % 2][:, hc2:hc2 + 1]
                    if shallow:
                        em.emit("vector", lambda E_, sl=sl, bcol=bcol, hx=hx: E_.tensor_scalar(
                            out=offc2[hx][:], in0=b.cst["c_doff2"][:], scalar1=sl,
                            scalar2=bcol, op0=OP.mult, op1=OP.subtract),
                            inc=(sPat, 1))
                        if hx == 0:
                            for bb in range(4):
                                dname = (f"c_diag_f{bb}" if fwd else f"c_diag_b{bb}")
                                thr = -1e9 if fwd else 1e9
                                opc = OP.is_le if fwd else OP.is_ge
                                em.emit("vector", lambda E_, dname=dname, bb=bb, thr=thr, opc=opc: E_.tensor_single_scalar(
                                    out=diagP[0][bb][:], in_=b.cst[dname][:],
                                    scalar=thr, op=opc))
                                em.emit("vector", lambda E_, bb=bb: E_.tensor_scalar_mul(
                                    diagP[0][bb][:], diagP[0][bb][:], -1e10))
                    else:
                        em.emit("vector", lambda E_, sl=sl, bcol=bcol, hx=hx: E_.tensor_scalar(
                            out=srelP[hx][:], in0=b.cst["c_rel_kq512"][:], scalar1=sl,
                            scalar2=bcol, op0=OP.mult, op1=OP.subtract))
                        for bb in range(4):
                            dname = (f"c_diag_f{bb}" if fwd else f"c_diag_b{bb}")
                            em.emit("vector", lambda E_, dname=dname, sl=sl, bcol=bcol, bb=bb, hx=hx: E_.tensor_scalar(
                                out=diagP[hx][bb][:], in0=b.cst[dname][:], scalar1=sl,
                                scalar2=bcol, op0=OP.mult, op1=OP.subtract))
                        em.emit("vector", lambda E_, sl=sl, hx=hx: E_.tensor_scalar_mul(
                            offc[hx][:], b.cst["c_doff"][:], sl))
                kt_lo = 0 if fwd else qt * 4
                kt_hi = qt * 4 + 4 if fwd else NTC
                nk = kt_hi - kt_lo
                nj = 2 * nk
                base_e = em.counts[sPe]
                pvb = [b.pb[3], b.pb[4]]
                sbanks = {}

                def emit_score(j):
                    kk, hx = j // 2, j % 2
                    kt = kt_lo + kk
                    h8 = heads[hx]
                    hp = (h8 % 2) * 64
                    isc = em.counts[sS]
                    bank = b.pb[isc % 3]
                    sbanks[j] = bank
                    if isc >= 3:
                        em.wait("tensor", sPe, isc - 2)
                    if j == 0 and ihead[0] >= 2:
                        em.wait("tensor", sEvh, ihead[0])
                    em.emit("tensor", lambda E_, kt=kt, bank=bank, hp=hp, hc2=hc2, qt=qt: E_.matmul(
                        bank[:, 0:512],
                        kTh[hp:hp + 64, hc2 * T + kt * 128: hc2 * T + kt * 128 + 128],
                        qTh[hp:hp + 64, hc2 * T + qt * 512: hc2 * T + qt * 512 + 512],
                        start=True, stop=True), inc=(sS, 1))

                for j0 in range(min(2, nj)):
                    emit_score(j0)
                for j in range(nj):
                    if j + 2 < nj:
                        emit_score(j + 2)
                    kk, hx = j // 2, j % 2
                    kt = kt_lo + kk
                    h8 = heads[hx]
                    bank = sbanks.pop(j)
                    diag = (qt * 4 <= kt < qt * 4 + 4)
                    m = em.counts[sPe]
                    if shallow:
                        dcol = offc2[hx][:, kt - 4 * qt + 32: kt - 4 * qt + 33]
                        if diag:
                            n = em.counts[sTx]
                            em.wait("vector", sS, base_e + j + 1)
                            if len(stt_exp) >= 3:
                                em.wait("vector", sPe, stt_exp[-3])
                            em.emit("vector", lambda E_, bank=bank, pat=diagP[0][kt - qt * 4], p=n % 3: E_.scalar_tensor_tensor(
                                out=scoreS[p][:], in0=bank[:, 0:512], scalar=0.125,
                                in1=pat[:], op0=OP.mult, op1=OP.add), inc=(sTx, 1))
                            stt_exp.append(m + 1)
                            em.wait("scalar", sTx, n + 1)
                            if m >= 3:
                                em.wait("scalar", sPv, m - 2)
                            em.emit("scalar", lambda E_, p=n % 3, dcol=dcol, q=m % 3: E_.activation(
                                out=pT[q][:], in_=scoreS[p][:], func=AF.Exp,
                                bias=dcol, scale=1.0), inc=(sPe, 1))
                        else:
                            em.wait("scalar", sS, base_e + j + 1)
                            em.wait("scalar", sPat, em.counts[sPat])
                            if m >= 3:
                                em.wait("scalar", sPv, m - 2)
                            em.emit("scalar", lambda E_, bank=bank, dcol=dcol, q=m % 3: E_.activation(
                                out=pT[q][:], in_=bank[:, 0:512], func=AF.Exp,
                                bias=dcol, scale=0.125), inc=(sPe, 1))
                    else:
                        n = em.counts[sTx]
                        em.wait("vector", sS, base_e + j + 1)
                        if len(stt_exp) >= 3:
                            em.wait("vector", sPe, stt_exp[-3])
                        pat = (diagP[hx][kt - qt * 4] if diag else srelP[hx])
                        em.emit("vector", lambda E_, bank=bank, pat=pat, p=n % 3: E_.scalar_tensor_tensor(
                            out=scoreS[p][:], in0=bank[:, 0:512], scalar=0.125,
                            in1=pat[:], op0=OP.mult, op1=OP.add), inc=(sTx, 1))
                        stt_exp.append(m + 1)
                        em.wait("scalar", sTx, n + 1)
                        if m >= 3:
                            em.wait("scalar", sPv, m - 2)
                        bias_ap = (0.0 if diag
                                   else offc[hx][:, kt - 4 * qt + 32: kt - 4 * qt + 33])
                        em.emit("scalar", lambda E_, p=n % 3, bias_ap=bias_ap, q=m % 3: E_.activation(
                            out=pT[q][:], in_=scoreS[p][:], func=AF.Exp,
                            bias=bias_ap, scale=1.0), inc=(sPe, 1))
                    em.wait("tensor", sPe, m + 1)
                    em.emit("tensor", lambda E_, kt=kt, h8=h8, q=m % 3, kk=kk, nk=nk, pv=pvb[hx]: E_.matmul(
                        pv[0:65, 0:512], vh[:, kt * VST + h8 * 65: kt * VST + h8 * 65 + 65],
                        pT[q][:], start=(kk == 0), stop=(kk == nk - 1)),
                        inc=(sPv, 1))
                cPv = em.counts[sPv]
                em.wait("scalar", sPv, cPv)
                em.wait("vector", sPv, cPv)
                if cF_attnh[0] > 0:
                    em.wait("vector", sF, cF_attnh[0])
                for hx in range(2):
                    h8 = heads[hx]
                    pv = pvb[hx]
                    if ihead[0] >= 1:
                        em.wait("scalar", sRbm, em.counts[sRbm])
                    em.emit("scalar", lambda E_, pv=pv: E_.copy(
                        out=lrow[:], in_=pv[64:65, 0:512]), inc=(sRcp, 1))
                    cRc = em.counts[sRcp]
                    em.wait("tensor", sRcp, cRc)
                    if ihead[0] >= 1:
                        em.wait("tensor", sEvh, ihead[0])
                    em.emit("tensor", lambda E_: E_.matmul(
                        b.pb[5][0:64, 0:512], b.cst["c_ones1"][0:1, 0:64], lrow[:],
                        start=True, stop=True), inc=(sRbm, 1))
                    cRb = em.counts[sRbm]
                    em.wait("vector", sRbm, cRb)
                    em.emit("vector", lambda E_: E_.reciprocal(rb[:], b.pb[5][0:64, 0:512]))
                    em.emit("vector", lambda E_, pv=pv, h8=h8: E_.tensor_tensor(
                        out=attnh[:, h8 * 512:(h8 + 1) * 512],
                        in0=pv[0:64, 0:512], in1=rb[:], op=OP.mult), inc=(sEvh, 1))
                    ihead[0] += 1
            em.wait("tensor", sEvh, em.counts[sEvh])
            cWFd = {}

            def load_wf(dcl):
                em.wait("gpsimd", sF, em.counts[sF])
                em.emit("gpsimd", lambda E_, dcl=dcl, half=half: E_.dma_start(
                    out=wfb[dcl % 2][:], in_=wf_h[half, dcl]), inc=(sWF, 16))
                cWFd[dcl] = em.counts[sWF]

            load_wf(0)
            for dc in range(KC):
                fb = b.pb[6 + dc % 2]
                if dc + 1 < KC:
                    load_wf(dc + 1)
                em.wait("tensor", sWF, cWFd.pop(dc))
                em.wait("tensor", sFe, em.counts[sFe] - 1)
                for h8 in range(8):
                    em.emit("tensor", lambda E_, dc=dc, h8=h8, fb=fb: E_.matmul(
                        fb[:, 0:512], wfb[dc % 2][0:64, h8 * 128:(h8 + 1) * 128],
                        attnh[:, h8 * 512:(h8 + 1) * 512],
                        start=(h8 == 0), stop=(h8 == 7)),
                        inc=(sF, 1) if h8 == 7 else None)
                cF = em.counts[sF]
                em.wait("scalar", sF, cF)
                em.wait("scalar", sV3m, em.counts[sV3m])
                if half == 0:
                    em.emit("scalar", lambda E_, dc=dc, fb=fb: E_.add(
                        out=txf[:], in_=fb[:, 0:512], add=bfc[:, dc:dc + 1]),
                        inc=(sFe, 1))
                else:
                    em.emit("scalar", lambda E_, fb=fb: E_.copy(
                        out=txf[:], in_=fb[:, 0:512]), inc=(sFe, 1))
                cFe = em.counts[sFe]
                em.wait("vector", sFe, cFe)
                em.wait("vector", sSt3[dc % 2], em.counts[sSt3[dc % 2]])
                em.emit("vector", lambda E_, dc=dc: E_.tensor_copy(
                    out=st3[dc % 2][:], in_=txf[:]), inc=(sV3m, 1))
                cV3 = em.counts[sV3m]
                em.wait("gpsimd", sV3m, cV3)
                if first_store[0]:
                    em.wait("gpsimd", em.dch, cpy_d)
                    first_store[0] = False
                em.emit("gpsimd", lambda E_, dc=dc, qt=qt: E_.dma_start(
                    out=x3_hbm[qt, dc],
                    in_=st3[dc % 2][:], accum_op=OP.add), inc=(sSt3[dc % 2], 16))
            cF_attnh[0] = em.counts[sF]
    em.join([(sSt3[0], em.counts[sSt3[0]]), (sSt3[1], em.counts[sSt3[1]])])


def emit_phase_d(b, x3_hbm, x4_hbm, n_cores=8):
    em, nc, T, NTT, NTC = b.em, b.nc, b.T, b.NTT, b.NTC
    pfx = "dd"
    ccap = float(2 * 2 * (T * n_cores) // E)
    wr_h = b.din("L1_ffn_Wg", [D, E])
    w1_h = b.din("L1_ffn_W1", [E, H // 256, 128, KC * 256])
    b1_h = b.din("L1_ffn_b1", [E, H])
    w2_h = b.din("L1_ffn_W2", [E, H, D])
    b2_h = b.din("L1_ffn_b2", [E, D])

    rows = {}
    xn = b.sb("d_xn", [128, KC * T], BF16)
    Mc = b.sb("d_Mc", [8, T], F32)
    b1cols = b.sb("d_b1c", [128, E * HC], F32)
    b2cols = b.sb("d_b2c", [128, E * KC], F32)
    emit_ln_pass(b, pfx, x3_hbm, rows, want_rms=False, xn_sb=xn,
                 xn_rows=("alpha", "beta"))
    em.ser("sync", lambda E_: E_.dma_start(out=x4_hbm[:], in_=x3_hbm[:]), dma=True)

    b.push()
    tmp32 = b.sb("d_tmp32", [32, 128], F32)
    tmp8 = b.sb("d_tmp8", [8, 128], F32)
    for e in range(E):
        em.ser("sync", lambda E_, e=e: E_.dma_start(
            out=tmp32[:], in_=b1_h[e].rearrange("(c p) -> c p", p=128)), dma=True)
        em.ser("tensor", lambda E_: E_.transpose(
            b.pb[6][:, 0:32], tmp32[:], b.cst["c_ident"][0:32, 0:32]))
        em.ser("vector", lambda E_, e=e: E_.tensor_copy(
            out=b1cols[:, e * HC:(e + 1) * HC], in_=b.pb[6][:, 0:32]))
        em.ser("sync", lambda E_, e=e: E_.dma_start(
            out=tmp8[:], in_=b2_h[e].rearrange("(c p) -> c p", p=128)), dma=True)
        em.ser("tensor", lambda E_: E_.transpose(
            b.pb[6][:, 0:8], tmp8[:], b.cst["c_ident"][0:8, 0:8]))
        em.ser("vector", lambda E_, e=e: E_.tensor_copy(
            out=b2cols[:, e * KC:(e + 1) * KC], in_=b.pb[6][:, 0:8]))

    wr_sb = b.sb("d_wr", [128, 8 * 8], F32)
    wrb = b.sb("d_wrb", [128, 8 * 8], BF16)
    logT = b.sb("d_logT", [8, T], F32)
    for fc in range(KC):
        em.ser("sync", lambda E_, fc=fc: E_.dma_start(
            out=wr_sb[:, fc * 8:(fc + 1) * 8], in_=wr_h[fc * 128:(fc + 1) * 128, :]),
            dma=True)
    em.ser("scalar", lambda E_: E_.copy(out=wrb[:], in_=wr_sb[:]))
    for tt in range(NTT):
        for fc in range(KC):
            em.ser("tensor", lambda E_, fc=fc, tt=tt: E_.matmul(
                b.pb[6][0:8, 0:512], wrb[:, fc * 8:(fc + 1) * 8],
                xn[:, fc * T + tt * 512: fc * T + tt * 512 + 512],
                start=(fc == 0), stop=(fc == KC - 1)))
        em.ser("vector", lambda E_, tt=tt: E_.tensor_copy(
            out=logT[0:8, tt * 512:(tt + 1) * 512], in_=b.pb[6][0:8, 0:512]))

    m1T = b.sb("d_m1T", [8, T], BF16)
    m2T = b.sb("d_m2T", [8, T], BF16)
    gT = b.sb("d_gT", [8, T], F32)
    scr96 = b.sb("d_scr96", [128, 96], F32)
    lg = b.sb("d_lg", [128, 8], F32)
    cc = b.sb("d_cc", [128, 8], F32)
    m1 = b.sb("d_m1", [128, 6], F32)
    for tc in range(NTC):
        em.ser("tensor", lambda E_, tc=tc: E_.transpose(
            b.pb[6][:, 0:8], logT[0:8, tc * 128:(tc + 1) * 128],
            b.cst["c_ident"][0:8, 0:8]))
        em.ser("vector", lambda E_: E_.tensor_copy(out=lg[:], in_=b.pb[6][:, 0:8]))
        em.ser("vector", lambda E_: E_.tensor_reduce(
            out=m1[:, 0:1], in_=lg[:], axis=AX, op=OP.max))
        em.ser("vector", lambda E_: E_.tensor_scalar(
            out=cc[:], in0=lg[:], scalar1=m1[:, 0:1], scalar2=None, op0=OP.is_equal))
        em.ser("vector", lambda E_: E_.tensor_mul(cc[:], cc[:], b.cst["c_p2row"][:]))
        em.ser("vector", lambda E_: E_.tensor_reduce(
            out=m1[:, 1:2], in_=cc[:], axis=AX, op=OP.max))
        em.ser("vector", lambda E_: E_.tensor_scalar(
            out=scr96[:, 0:8], in0=cc[:], scalar1=m1[:, 1:2], scalar2=None,
            op0=OP.is_equal))
        em.ser("vector", lambda E_: E_.scalar_tensor_tensor(
            out=cc[:], in0=scr96[:, 0:8], scalar=-1e30, in1=lg[:],
            op0=OP.mult, op1=OP.add))
        em.ser("vector", lambda E_: E_.tensor_reduce(
            out=m1[:, 2:3], in_=cc[:], axis=AX, op=OP.max))
        em.ser("vector", lambda E_: E_.tensor_scalar(
            out=cc[:], in0=cc[:], scalar1=m1[:, 2:3], scalar2=None, op0=OP.is_equal))
        em.ser("vector", lambda E_: E_.tensor_mul(cc[:], cc[:], b.cst["c_p2row"][:]))
        em.ser("vector", lambda E_: E_.tensor_reduce(
            out=m1[:, 3:4], in_=cc[:], axis=AX, op=OP.max))
        em.ser("vector", lambda E_: E_.tensor_scalar(
            out=scr96[:, 32:40], in0=cc[:], scalar1=m1[:, 3:4], scalar2=None,
            op0=OP.is_equal))
        em.ser("vector", lambda E_: E_.tensor_scalar_mul(m1[:, 4:5], m1[:, 0:1], -1.0))
        em.ser("scalar", lambda E_: E_.activation(
            out=scr96[:, 64:72], in_=lg[:], func=AF.Exp, bias=m1[:, 4:5], scale=1.0))
        em.ser("vector", lambda E_: E_.tensor_reduce(
            out=m1[:, 5:6], in_=scr96[:, 64:72], axis=AX, op=OP.add))
        em.ser("vector", lambda E_: E_.reciprocal(m1[:, 5:6], m1[:, 5:6]))
        em.ser("vector", lambda E_: E_.tensor_scalar(
            out=scr96[:, 64:72], in0=scr96[:, 64:72], scalar1=m1[:, 5:6],
            scalar2=None, op0=OP.mult))
        em.ser("tensor", lambda E_, tc=tc: E_.transpose(
            b.pb[6][0:96, 0:128], scr96[:], b.cst["c_ident"][:]))
        em.ser("vector", lambda E_, tc=tc: E_.tensor_copy(
            out=m1T[0:8, tc * 128:(tc + 1) * 128], in_=b.pb[6][0:8, 0:128]))
        em.ser("vector", lambda E_, tc=tc: E_.tensor_copy(
            out=m2T[0:8, tc * 128:(tc + 1) * 128], in_=b.pb[6][32:40, 0:128]))
        em.ser("vector", lambda E_, tc=tc: E_.tensor_copy(
            out=gT[0:8, tc * 128:(tc + 1) * 128], in_=b.pb[6][64:72, 0:128]))

    pos1 = b.sb("d_pos1", [8, T], F32)
    pos2 = b.sb("d_pos2", [8, T], F32)
    for (msk, pos) in ((m1T, pos1), (m2T, pos2)):
        for tt in range(NTT):
            if tt == 0:
                em.ser("vector", lambda E_, msk=msk, pos=pos: E_.tensor_tensor_scan(
                    out=pos[0:8, 0:512], data0=b.cst["c_ones8"][:],
                    data1=msk[0:8, 0:512], initial=0.0, op0=OP.mult, op1=OP.add))
            else:
                em.ser("vector", lambda E_, msk=msk, pos=pos, tt=tt: E_.tensor_tensor_scan(
                    out=pos[0:8, tt * 512:(tt + 1) * 512], data0=b.cst["c_ones8"][:],
                    data1=msk[0:8, tt * 512:(tt + 1) * 512],
                    initial=pos[0:8, tt * 512 - 1:tt * 512], op0=OP.mult, op1=OP.add))

    ccin_sb = b.sb("d_ccin", [1, 16], F32)
    ccall = b.sb("d_ccall", [8, 16], F32)
    ccin_d = nc.dram_tensor("d_ccin_d", [1, 16], F32, kind="Internal")
    ccout_d = nc.dram_tensor("d_ccout_d", [n_cores, 16], F32, kind="Internal",
                             addr_space="Shared" if n_cores > 1 else "Local")
    em.ser("tensor", lambda E_: E_.transpose(
        b.pb[6][0:1, 0:8], pos1[0:8, T - 1:T], b.cst["c_ident"][0:8, 0:8]))
    em.ser("vector", lambda E_: E_.tensor_copy(
        out=ccin_sb[0:1, 0:8], in_=b.pb[6][0:1, 0:8]))
    em.ser("tensor", lambda E_: E_.transpose(
        b.pb[6][0:1, 0:8], pos2[0:8, T - 1:T], b.cst["c_ident"][0:8, 0:8]))
    em.ser("vector", lambda E_: E_.tensor_copy(
        out=ccin_sb[0:1, 8:16], in_=b.pb[6][0:1, 0:8]))
    em.ser("sync", lambda E_: E_.dma_start(out=ccin_d[:], in_=ccin_sb[:]), dma=True)
    sCC = em.sem("d_cc")
    em.wait("gpsimd", em.cch, em.counts[em.cch])
    em.wait("gpsimd", em.dch, em.counts[em.dch])
    em.emit("gpsimd", lambda E_: E_.collective_compute(
        "AllGather", OP.bypass, replica_groups=[list(range(n_cores))],
        ins=[ccin_d[:]], outs=[ccout_d[:]]), inc=(sCC, 1))
    em.join([(sCC, 1)])
    em.ser("vector", lambda E_: E_.memset(ccall[:], 0.0))
    em.ser("sync", lambda E_, n_cores=n_cores: E_.dma_start(
        out=ccall[0:n_cores, :], in_=ccout_d[:]), dma=True)
    rx = b.sb("d_rx", [1, 8 * 4], F32)
    thrc = b.sb("d_thrc", [8, 2], F32)
    em.ser("tensor", lambda E_: E_.matmul(
        b.pb[6][0:1, 0:8], b.cst["c_pref"][:], ccall[0:8, 0:8], start=True, stop=True))
    em.ser("vector", lambda E_: E_.tensor_copy(out=rx[0:1, 0:8], in_=b.pb[6][0:1, 0:8]))
    em.ser("tensor", lambda E_: E_.matmul(
        b.pb[6][0:1, 0:8], b.cst["c_ones8col"][:], ccall[0:8, 0:8],
        start=True, stop=True))
    em.ser("vector", lambda E_: E_.tensor_copy(out=rx[0:1, 8:16], in_=b.pb[6][0:1, 0:8]))
    em.ser("tensor", lambda E_: E_.matmul(
        b.pb[6][0:1, 0:8], b.cst["c_pref"][:], ccall[0:8, 8:16], start=True, stop=True))
    em.ser("vector", lambda E_: E_.tensor_copy(out=rx[0:1, 16:24], in_=b.pb[6][0:1, 0:8]))
    em.ser("vector", lambda E_, ccap=ccap: E_.tensor_scalar(
        out=rx[0:1, 24:32], in0=rx[0:1, 0:8], scalar1=-1.0, scalar2=ccap,
        op0=OP.mult, op1=OP.add))
    row2col(b, rx[0:1, 24:32], thrc[:, 0:1], 8)
    em.ser("vector", lambda E_: E_.tensor_add(rx[0:1, 16:24], rx[0:1, 16:24],
                                              rx[0:1, 8:16]))
    em.ser("vector", lambda E_, ccap=ccap: E_.tensor_scalar(
        out=rx[0:1, 24:32], in0=rx[0:1, 16:24], scalar1=-1.0, scalar2=ccap,
        op0=OP.mult, op1=OP.add))
    row2col(b, rx[0:1, 24:32], thrc[:, 1:2], 8)

    M1K = m1T
    M2K = m2T
    s8a = b.sb("d_s8a", [8, 512], F32)
    s8b = b.sb("d_s8b", [8, 512], F32)
    denrow = b.sb("d_denrow", [1, T], F32)
    for tt in range(NTT):
        for (msk, pos, thr_i, MK, second) in ((m1T, pos1, 0, M1K, False),
                                              (m2T, pos2, 1, M2K, True)):
            sl_ = (slice(0, 8), slice(tt * 512, (tt + 1) * 512))
            em.ser("vector", lambda E_, msk=msk, pos=pos, sl_=sl_: E_.tensor_sub(
                s8a[0:8, 0:512], pos[sl_[0], sl_[1]], msk[sl_[0], sl_[1]]))
            em.ser("vector", lambda E_, thr_i=thr_i: E_.tensor_scalar(
                out=s8a[0:8, 0:512], in0=s8a[0:8, 0:512],
                scalar1=thrc[0:8, thr_i:thr_i + 1], scalar2=None, op0=OP.is_lt))
            em.ser("vector", lambda E_, msk=msk, MK=MK, sl_=sl_: E_.tensor_tensor(
                out=MK[sl_[0], sl_[1]], in0=msk[sl_[0], sl_[1]],
                in1=s8a[0:8, 0:512], op=OP.mult))
            em.ser("vector", lambda E_, msk=msk, sl_=sl_: E_.tensor_tensor(
                out=s8b[0:8, 0:512], in0=msk[sl_[0], sl_[1]],
                in1=gT[sl_[0], sl_[1]], op=OP.mult))
            em.ser("tensor", lambda E_: E_.matmul(
                b.pb[6][0:1, 0:512], b.cst["c_ones8col"][:], s8b[0:8, 0:512],
                start=True, stop=True))
            if not second:
                em.ser("vector", lambda E_, tt=tt: E_.tensor_copy(
                    out=denrow[0:1, tt * 512:(tt + 1) * 512], in_=b.pb[6][0:1, 0:512]))
            else:
                em.ser("vector", lambda E_, tt=tt: E_.tensor_tensor(
                    out=denrow[0:1, tt * 512:(tt + 1) * 512],
                    in0=denrow[0:1, tt * 512:(tt + 1) * 512],
                    in1=b.pb[6][0:1, 0:512], op=OP.add))
    em.ser("vector", lambda E_: E_.tensor_scalar_add(denrow[0:1, 0:T],
                                                     denrow[0:1, 0:T], 1e-9))
    em.ser("vector", lambda E_: E_.reciprocal(denrow[0:1, 0:T], denrow[0:1, 0:T]))
    for tt in range(NTT):
        em.ser("tensor", lambda E_, tt=tt: E_.matmul(
            b.pb[6][0:8, 0:512], b.cst["c_ones1"][0:1, 0:8],
            denrow[0:1, tt * 512:(tt + 1) * 512], start=True, stop=True))
        em.ser("vector", lambda E_, tt=tt: E_.tensor_tensor(
            out=s8a[0:8, 0:512], in0=M1K[0:8, tt * 512:(tt + 1) * 512],
            in1=b.pb[6][0:8, 0:512], op=OP.mult))
        em.ser("vector", lambda E_, tt=tt: E_.tensor_tensor(
            out=s8b[0:8, 0:512], in0=M2K[0:8, tt * 512:(tt + 1) * 512],
            in1=b.pb[6][0:8, 0:512], op=OP.mult))
        em.ser("vector", lambda E_, tt=tt: E_.tensor_tensor(
            out=s8a[0:8, 0:512], in0=s8a[0:8, 0:512],
            in1=gT[0:8, tt * 512:(tt + 1) * 512], op=OP.mult))
        em.ser("vector", lambda E_, tt=tt: E_.tensor_tensor(
            out=s8b[0:8, 0:512], in0=s8b[0:8, 0:512],
            in1=gT[0:8, tt * 512:(tt + 1) * 512], op=OP.mult))
        em.ser("vector", lambda E_, tt=tt: E_.tensor_add(
            Mc[0:8, tt * 512:(tt + 1) * 512], s8a[0:8, 0:512], s8b[0:8, 0:512]))
    em.join([(em.cch, em.counts[em.cch]), (em.dch, em.counts[em.dch])])
    b.pop()

    b.push()
    emit_moe_loop(b, "d", xn, Mc, x4_hbm, w1_h, None, w2_h, b1cols, b2cols,
                  silu_glu=False)
    b.pop()


def _phase_scope(b):
    return b.stack.enter_context(contextlib.ExitStack())


def build_program(T, phases, n_cores=NCORES, verbose=False):
    nc = bass.Bass("TRN2", target_bir_lowering=False)
    ext = {}
    with contextlib.ExitStack() as stack:
        em = Em(nc, stack)
        b = B(nc, stack, em, T)
        NTT = T // 512
        xshape = [NTT, KC, 128, 512]
        x_in = nc.dram_tensor("x_in", xshape, F32, kind="ExternalInput")
        b.ext["x_in"] = tuple(xshape)
        x_out = nc.dram_tensor("x_out", xshape, F32, kind="ExternalOutput")
        n = len(phases)
        bufs = [x_in]
        for i in range(n - 1):
            bufs.append(nc.dram_tensor(f"xmid{i}", xshape, F32, kind="Internal"))
        bufs.append(x_out)
        emitters = {"A": emit_phase_a, "B": emit_phase_b, "C": emit_phase_c,
                    "D": emit_phase_d}
        with nc.Block() as blk:
            setup_common(b)
            outer_stack = b.stack
            for i, p in enumerate(phases):
                sem_snapshot = dict(em.counts)
                phase_stack = contextlib.ExitStack()
                b.stack = phase_stack
                if p == "D":
                    emit_phase_d(b, bufs[i], bufs[i + 1], n_cores=n_cores)
                else:
                    emitters[p](b, bufs[i], bufs[i + 1])
                deps = [(s, c) for s, c in em.counts.items()
                        if c > sem_snapshot.get(s, 0)]
                em.join(deps)
                em.release_phase_sems()
                em.join([(em.cch, em.counts[em.cch])])
                b.stack = outer_stack
                phase_stack.close()
                if verbose:
                    print(f"phase {p}: sbuf_remaining={nc.sbuf_bytes_remaining} "
                          f"n_inst={nc.n_instructions()}")
            em.finalize(blk)
        ext = dict(b.ext)
    return nc, ext


def tile_updown(W):
    W = np.ascontiguousarray(W)
    Eo, Do, Ho = W.shape
    return np.ascontiguousarray(
        W.reshape(Eo, KC, 128, Ho // 256, 256).transpose(0, 3, 2, 1, 4)
        .reshape(Eo, Ho // 256, 128, KC * 256))


def tile_wf(W):
    W = np.ascontiguousarray(W)
    return np.ascontiguousarray(
        W.reshape(2, 8, 64, KC, 128).transpose(0, 3, 2, 1, 4)
        .reshape(2, KC, 64, 8 * 128))


def tile_x(xTD):
    T_, D_ = xTD.shape
    xT = np.ascontiguousarray(xTD.T)
    return np.ascontiguousarray(
        xT.reshape(KC, 128, T_ // 512, 512).transpose(2, 0, 1, 3))


def untile_x(xt):
    NTT_ = xt.shape[0]
    return np.ascontiguousarray(
        xt.transpose(1, 2, 0, 3).reshape(D, NTT_ * 512).T)


TILE_TRANSFORMS = {
    "L0_ffn_Wg": tile_updown,
    "L0_ffn_Wu": tile_updown,
    "L1_ffn_W1": tile_updown,
    "L1_attn_Wf": tile_wf,
}


def _shard_inputs(inputs, T, phases, n_cores=NCORES):
    w = dict(inputs)
    maps = []
    x0 = np.asarray(w["L0_attn_inputs"])
    need = set()
    for p in phases:
        need |= {
            "A": {"L0_attn_norm_weight", "L0_attn_router_w", "L0_attn_router_b",
                  "L0_attn_w_hg", "L0_attn_w_out"},
            "B": {"L0_ffn_Wr", "L0_ffn_Wg", "L0_ffn_Wu", "L0_ffn_Wd"},
            "C": {"L1_attn_Wq", "L1_attn_bq", "L1_attn_Wk", "L1_attn_bk",
                  "L1_attn_Wv", "L1_attn_bv", "L1_attn_Wf", "L1_attn_bf"},
            "D": {"L1_ffn_Wg", "L1_ffn_W1", "L1_ffn_b1", "L1_ffn_W2", "L1_ffn_b2"},
        }[p]
    reshape = {"L0_attn_router_b": (1, E)}
    for c in range(n_cores):
        m = {}
        for k, v in host_constants(c).items():
            m[k] = v
        for name in need:
            arr = np.ascontiguousarray(np.asarray(w[name]), dtype=np.float32)
            if name in reshape:
                arr = arr.reshape(reshape[name])
            if name in TILE_TRANSFORMS:
                arr = TILE_TRANSFORMS[name](arr)
            m[name] = arr
        m["x_in"] = tile_x(np.asarray(x0[c]))
        maps.append(m)
    return maps


_NC_CACHE = {}


def kernel(**inputs):
    T = int(np.asarray(inputs["L0_attn_inputs"]).shape[1])
    phases = ("A", "B", "C", "D")
    key = (T, phases)
    if key not in _NC_CACHE:
        _NC_CACHE[key] = build_program(T, phases)
    nc, ext = _NC_CACHE[key]
    maps = _shard_inputs(inputs, T, phases)
    res = run_bass_kernel_spmd(nc, maps, core_ids=list(range(NCORES)))
    out = np.stack([untile_x(r["x_out"].reshape(T // 512, KC, 128, 512))
                    for r in res.results])
    return out.astype(np.float32)
